# version 28
# speedup vs baseline: 1.1291x; 1.0200x over previous
import numpy as np
import ml_dtypes
import concourse.bass as bass
import concourse.mybir as mybir
from concourse.bass_utils import run_bass_kernel_spmd

F32 = mybir.dt.float32
BF16 = mybir.dt.bfloat16
AF = mybir.ActivationFunctionType
ALU = mybir.AluOpType
AX = mybir.AxisListType
BFND = ml_dtypes.bfloat16

B, S, D = 2, 4096, 2048
T = 1024
H = 256
TH = T + H
NCH = D // 128
HEADS, HD = 16, 128
QT = T // 128
KT = TH // 128
E, HE = 8, 1024
HC = HE // 128
OC = NCH
NPASS = 4
NTOT = S * D
EPS = 1e-5
MASKBIG = 1e4


def _bf(x):
    return np.ascontiguousarray(np.asarray(x, np.float32)).astype(BFND)


def _f32(x):
    return np.ascontiguousarray(np.asarray(x, np.float32))


def build_program(taps=False):
    nc = bass.Bass("TRN2", target_bir_lowering=False)
    tap_list = []

    d_xT = nc.dram_tensor("xT", [D, TH], F32, kind="ExternalInput").ap()
    d_wqT = nc.dram_tensor("wqT", [D, 2048], F32, kind="ExternalInput").ap()
    d_wkT = nc.dram_tensor("wkT", [D, 2048], F32, kind="ExternalInput").ap()
    d_wvT = nc.dram_tensor("wvT", [D, 2048], F32, kind="ExternalInput").ap()
    d_woT = nc.dram_tensor("woT", [D, 2048], F32, kind="ExternalInput").ap()
    d_wc1T = nc.dram_tensor("wc1T", [D, 512], F32, kind="ExternalInput").ap()
    _unused_wg2T = 0
    _unused_wc2r = 0
    d_wupT = nc.dram_tensor("wupT", [E, D, HE], F32, kind="ExternalInput").ap()
    d_wdn = nc.dram_tensor("wdn", [NPASS, OC, 2048, 128], F32, kind="ExternalInput").ap()
    _unused_bupr = 0
    d_bdown = nc.dram_tensor("bdown", [8, 2048], F32, kind="ExternalInput").ap()
    d_tabs = nc.dram_tensor("tabs", [128, 2 * TH], BF16, kind="ExternalInput").ap()
    d_cbf = nc.dram_tensor("cbf", [128, 260], BF16, kind="ExternalInput").ap()
    d_onesb = nc.dram_tensor("onesb", [128, 128], BF16, kind="ExternalInput").ap()
    d_selm = nc.dram_tensor("selm", [8, 1024], BF16, kind="ExternalInput").ap()
    _unused_maskR = 0
    _unused_maskC = 0
    _unused_idf = 0
    _unused_idf8 = 0
    _unused_onesf = 0
    _unused_sneg = 0
    _unused_bsel = 0
    d_out = nc.dram_tensor("outT", [D, T], F32, kind="ExternalOutput").ap()
    d_ccin = nc.dram_tensor("ccin", [128, 64], F32)
    d_ccout = nc.dram_tensor("ccout", [128, 64], F32)

    R1 = 0
    R2 = 40960
    R3 = 122880
    RWS = 155648
    RTAB = 163840
    RSM = 168960
    RXS = 185344
    RRW = 195584
    RMI = 203776

    SB_BASE = 16896

    def sb(name, shape, dtype, off):
        return nc.alloc_sbuf_tensor_at(name, shape, dtype, offset=SB_BASE + off).ap()

    xb = sb("xb", [128, NCH * TH], BF16, R1)
    yb = sb("yb", [128, NCH * T], BF16, R1)
    kr = sb("kr", [128, NCH * TH], BF16, R2)
    vT = sb("vT", [128, KT * 2048], BF16, R2 + 40960)
    rF = sb("rF", [128, NCH * T], F32, R2)
    bc = sb("bc", [128, 2 * 1024], BF16, R2 + 69632)
    rb1 = sb("rb1", [128, 2 * 1024], BF16, R2 + 73728)
    rq1 = sb("rq1", [128, 2 * 1024], BF16, R2 + 77824)
    mpak = sb("mpak", [128, 768], F32, R3)
    maskR = mpak[:, 0:384]
    maskC = mpak[:, 384:768]
    biasw = sb("biasw", [128, 384], F32, R3 + 3072)
    ao = sb("ao", [128, NCH * T], BF16, R3)
    hp = sb("hp", [128, 16 * 1024], BF16, R3)
    wsA = sb("wsA", [128, 2048], BF16, RWS)
    wsB = sb("wsB", [128, 2048], BF16, RWS + 4096)
    cosu = sb("cosu", [128, TH], BF16, RTAB)
    sinu = sb("sinu", [128, TH], BF16, RTAB + 2560)
    pswap = sb("pswap", [128, 128], BF16, RSM)
    ibf = sb("ibf", [128, 128], BF16, RSM + 256)
    onescol_bf = sb("onescol_bf", [128, 1], BF16, RSM + 512)
    qh = sb("qh", [128, 2 * T], BF16, RSM + 11520)
    biasc = sb("biasc", [128, 384], BF16, RSM + 3072)
    bias0 = sb("bias0", [128, 384], BF16, RSM + 3840)
    bias1 = sb("bias1", [128, 384], BF16, RSM + 4608)
    pbt = sb("pbt", [128, 1152], BF16, RSM + 5376)
    pbts = sb("pbts", [128, 768], BF16, RSM + 7680)
    xmb = sb("xmb", [128, 32], BF16, RSM + 9216)
    sfb = sb("sfb", [8, 1024], BF16, RSM + 9472)
    bdn = sb("bdn", [8, 2048], BF16, R2 + 65536)
    selm = sb("selm", [8, 1024], BF16, RMI + 5120)
    onesb = sb("onesb", [128, 128], BF16, RMI + 4864)
    xstage = sb("xstage", [128, 2 * TH], F32, RXS)
    qstage = sb("qstage", [128, TH], BF16, RXS)
    swp = sb("swp", [128, TH], BF16, RXS + 2560)
    tmp = sb("tmp", [128, TH], BF16, RXS + 5120)
    psE = sb("psE", [128, 384], F32, RXS + 8192)
    psE1 = sb("psE1", [128, 384], F32, RRW + 4096)
    mubc = sb("mubc", [128, 1024], F32, RXS)
    rsbc = sb("rsbc", [128, 1024], F32, RXS + 4096)
    LfT = sb("LfT", [8, 1024], F32, RXS)
    T1 = sb("T1", [8, 1024], F32, RXS + 4096)
    rows = sb("rows", [1, 2048], F32, RRW)
    sfF = sb("sfF", [8, 1024], F32, RRW)
    stats = sb("stats", [128, 32], F32, RMI)
    arpay = sb("arpay", [128, 64], F32, RMI + 128)
    arres = sb("arres", [128, 64], F32, RMI + 384)
    cpak = sb("cpak", [128, 928], F32, RMI + 640)
    idf = cpak[:, 0:128]
    idf8 = cpak[0:8, 128:136]
    onesf = cpak[:, 144:272]
    wg1T = cpak[:, 272:400]
    wg2T = cpak[0:8, 400:408]
    wc2r = cpak[:, 416:424]
    sneg = cpak[:, 432:433]
    bsel = cpak[:, 448:450]
    hid = sb("hid", [128, 8], F32, RMI + 2496)
    red4 = sb("red4", [128, 4], F32, RMI + 2560)
    hsum = sb("hsum", [128, 2], F32, RMI + 2624)
    wcol = sb("wcol", [128, 1], F32, RMI + 2688)
    Lt = sb("Lt", [128, 64], F32, RMI + 2752)
    stok = sb("stok", [128, 64], F32, RMI + 3072)
    m1 = sb("m1", [128, 1], F32, RMI + 3392)
    m2 = sb("m2", [128, 1], F32, RMI + 3456)
    p1c = sb("p1c", [128, 1], F32, RMI + 3520)
    p2c = sb("p2c", [128, 1], F32, RMI + 3584)
    dd = sb("dd", [128, 1], F32, RMI + 3648)
    l2t = sb("l2t", [128, 8], F32, RMI + 3712)
    is1t = sb("is1t", [128, 8], F32, RMI + 3776)
    is2t = sb("is2t", [128, 8], F32, RMI + 3840)
    negmax = sb("negmax", [128, 2], F32, RMI + 3904)
    ssum = sb("ssum", [128, 2], F32, RMI + 3968)
    rcp = sb("rcp", [128, 2], F32, RMI + 4032)
    lnv = sb("lnv", [128, 2], F32, RMI + 4064)
    bupr = cpak[:, 864:928]
    pre6 = sb("pre6", [128, 8], F32, RMI + 4352)
    brow = sb("brow", [128, 8], F32, RMI + 4608)

    pb = {}
    pb["PB0"] = nc.alloc_psum_tensor("pb0", [128, 512], F32).ap()
    pb["PB1"] = nc.alloc_psum_tensor("pb1", [128, 512], F32).ap()
    pb["pb2"] = nc.alloc_psum_tensor("pb2", [128, 1024], BF16).ap()
    pb["pb3"] = nc.alloc_psum_tensor("pb3", [128, 1024], BF16).ap()
    pb["PB4"] = nc.alloc_psum_tensor("pb4", [128, 512], F32).ap()
    pb["PB5"] = nc.alloc_psum_tensor("pb5", [128, 512], F32).ap()
    pb["PB6"] = nc.alloc_psum_tensor("pb6", [128, 512], F32).ap()
    pb["PB7"] = nc.alloc_psum_tensor("pb7", [128, 512], F32).ap()
    PB0, PB1, PB4, PB5, PB6, PB7 = (pb["PB0"], pb["PB1"], pb["PB4"], pb["PB5"],
                                    pb["PB6"], pb["PB7"])
    pb2, pb3 = pb["pb2"], pb["pb3"]

    from contextlib import ExitStack
    _stack = ExitStack()
    sems = {}
    cnt = {}
    for n in ["sx", "sxa", "sxb", "sw", "swa", "swb", "scc", "st", "sv",
              "ss", "se", "sst", "scd", "sa"]:
        sems[n] = _stack.enter_context(nc.semaphore(n))
        cnt[n] = 0

    def inc(ins, s, k=1):
        cnt[s] += k
        ins.then_inc(sems[s], k)
        return cnt[s]

    def val(s):
        return cnt[s]

    blocks = nc.Block()
    block = blocks.__enter__()

    def on(eng_name, fn):
        getattr(block, eng_name)(fn)

    def tap(name, ap_, sem, v):
        if not taps:
            return
        dt_ = ap_.dtype
        shp = list(ap_.shape)
        dten = nc.dram_tensor("tap_" + name, shp, dt_, kind="ExternalOutput").ap()

        def ftap(sync, dten=dten, ap_=ap_, sem=sem, v=v):
            sync.wait_ge(sems[sem], v)
            sync.dma_start(out=dten, in_=ap_).then_inc(sems["sst"], 16)
            cnt["sst"] += 16
        on("sync", ftap)
        tap_list.append(name)

    d_cpak = nc.dram_tensor("cpak", [128, 928], F32, kind="ExternalInput").ap()
    d_mpak = nc.dram_tensor("mpak", [128, 768], F32, kind="ExternalInput").ap()

    def c_loads(sync):
        for dst, src_ in [
            (cosu[:], d_tabs[:, 0:TH]), (sinu[:], d_tabs[:, TH:2 * TH]),
            (pswap[:], d_cbf[:, 0:128]), (ibf[:], d_cbf[:, 128:256]),
            (onescol_bf[:], d_cbf[:, 256:257]),
            (selm[:], d_selm), (onesb[:], d_onesb),
            (mpak[:], d_mpak), (cpak[:], d_cpak),
        ]:
            sync.dma_start(out=dst, in_=src_,
                           allow_slow_non_contiguous=True).then_inc(sems["sx"], 16)
            cnt["sx"] += 16
    on("sync", c_loads)
    V_CONSTS = val("sx")

    def s_wait_consts(scalar):
        scalar.wait_ge(sems["sx"], V_CONSTS)
    on("scalar", s_wait_consts)

    def t_wait_consts(tensor):
        tensor.wait_ge(sems["sx"], V_CONSTS)
    on("tensor", t_wait_consts)

    strip_state = {"idx": 0, "freed": []}

    def load_strip(src_ap, krows):
        i = strip_state["idx"]
        buf = wsA if i % 2 == 0 else wsB
        wsem = "swa" if i % 2 == 0 else "swb"
        nchunks = krows // 128

        def fn(gpsimd):
            if i >= 2:
                s_, v_ = strip_state["freed"][i - 2]
                gpsimd.wait_ge(sems[s_], v_)
            gpsimd.dma_start(
                out=buf[:].rearrange("p (c n) -> p c n", c=nchunks),
                in_=src_ap.rearrange("(c p) n -> p c n", p=128),
            ).then_inc(sems[wsem], 16)
            cnt[wsem] += 16
        on("gpsimd", fn)
        strip_state["idx"] += 1
        return buf, (wsem, val(wsem))

    def strip_consumed(sem_name):
        strip_state["freed"].append((sem_name, val(sem_name)))

    bank_r = {k: ("sv", 0) for k in pb}
    bank_w = {k: 0 for k in pb}

    def t_wait_bank(tensor, bk):
        s_, v_ = bank_r[bk]
        if v_ > 0:
            tensor.wait_ge(sems[s_], v_)

    def s0_memset(vector):
        vector.wait_ge(sems["sx"], V_CONSTS)
        inc(vector.engine_nop(), "sv")
    on("vector", s0_memset)

    x_done = []
    for c in range(NCH):
        sl = c % 2

        xsem = "sxa" if c % 2 == 0 else "sxb"

        def fx(sync, c=c, sl=sl, xsem=xsem):
            if c >= 2:
                sv_, ss_ = x_done[c - 2]
                sync.wait_ge(sems["sv"], sv_)
                sync.wait_ge(sems["ss"], ss_)
            sync.dma_start(out=xstage[:, sl * TH:(sl + 1) * TH],
                           in_=d_xT[c * 128:(c + 1) * 128, :]).then_inc(sems[xsem], 16)
            cnt[xsem] += 16
        on("sync", fx)
        VX = val(xsem)

        def fsq(scalar, c=c, sl=sl, VX=VX, xsem=xsem):
            scalar.wait_ge(sems[xsem], VX)
            if c >= 2:
                scalar.drain()
            ins = scalar.activation(rF[:, sl * 1024:(sl + 1) * 1024],
                                    xstage[:, sl * TH + H:(sl + 1) * TH], AF.Square,
                                    accum_out=stats[:, 16 + c:17 + c])
            inc(ins, "ss")
        on("scalar", fsq)
        VSQ = val("ss")

        def fv(vector, c=c, sl=sl, VX=VX, xsem=xsem):
            vector.wait_ge(sems[xsem], VX)
            vector.tensor_copy(xb[:, c * TH:(c + 1) * TH],
                               xstage[:, sl * TH:(sl + 1) * TH])
            ins = vector.reduce_sum(stats[:, c:c + 1],
                                    xstage[:, sl * TH + H:(sl + 1) * TH], axis=AX.X)
            inc(ins, "sv")
        on("vector", fv)
        x_done.append((val("sv"), val("ss")))
    V_XBALL = x_done[NCH - 1][0]

    def s0_pay(vector):
        vector.wait_ge(sems["ss"], x_done[NCH - 1][1])
        vector.drain()
        vector.tensor_scalar(arpay[:, 0:16], stats[:, 0:16], bsel[:, 0:1], None, op0=ALU.mult)
        vector.tensor_scalar(arpay[:, 16:32], stats[:, 16:32], bsel[:, 0:1], None, op0=ALU.mult)
        vector.tensor_scalar(arpay[:, 32:48], stats[:, 0:16], bsel[:, 1:2], None, op0=ALU.mult)
        ins = vector.tensor_scalar(arpay[:, 48:64], stats[:, 16:32], bsel[:, 1:2], None,
                                   op0=ALU.mult)
        inc(ins, "sv")
    on("vector", s0_pay)
    V_PAY = val("sv")

    rope_done = [0]

    def emit_k_head(nch):
        buf, vw = load_strip(d_wkT[:, nch * 128:(nch + 1) * 128], D)
        parts = [(0, 512), (512, 1024), (1024, 1280)]
        evs = []
        for pi, (a, b_) in enumerate(parts):
            bk = "PB6" if (nch * 3 + pi) % 2 == 0 else "PB7"

            def fmm(tensor, buf=buf, vw=vw, a=a, b_=b_, bk=bk, nch=nch, pi=pi):
                tensor.wait_ge(sems[vw[0]], vw[1])
                t_wait_bank(tensor, bk)
                for kc in range(NCH):
                    if nch == 0 and pi == 0:
                        tensor.wait_ge(sems["sv"], x_done[kc][0])
                    elif nch == 0 and pi == 1 and kc == 0:
                        tensor.wait_ge(sems["sv"], V_XBALL)
                    ins = tensor.matmul(
                        pb[bk][:, 0:b_ - a], buf[:, kc * 128:(kc + 1) * 128],
                        xb[:, kc * TH + a:kc * TH + b_],
                        start=(kc == 0), stop=(kc == NCH - 1))
                inc(ins, "st")
            on("tensor", fmm)
            bank_w[bk] = val("st")
            VW = val("st")

            def fev(vector, a=a, b_=b_, bk=bk, VW=VW):
                vector.wait_ge(sems["st"], VW)
                ins = vector.tensor_copy(qstage[:, a:b_], pb[bk][:, 0:b_ - a])
                inc(ins, "sv")
            on("vector", fev)
            bank_r[bk] = ("sv", val("sv"))
            evs.append(val("sv"))
        strip_consumed("st")
        for pi, (a, b_) in enumerate(parts):
            bk = "PB4" if pi % 2 == 0 else "PB5"

            def fsw(tensor, a=a, b_=b_, bk=bk, ev=evs[pi]):
                tensor.wait_ge(sems["sv"], ev)
                t_wait_bank(tensor, bk)
                ins = tensor.matmul(pb[bk][:, 0:b_ - a], pswap[:],
                                    qstage[:, a:b_], start=True, stop=True)
                inc(ins, "st")
            on("tensor", fsw)
            bank_w[bk] = val("st")
            VW2 = val("st")

            def fr1(vector, a=a, b_=b_, bk=bk, VW2=VW2):
                vector.wait_ge(sems["st"], VW2)
                ins = vector.tensor_mul(swp[:, a:b_], pb[bk][:, 0:b_ - a], sinu[:, a:b_])
                inc(ins, "sv")
            on("vector", fr1)
            bank_r[bk] = ("sv", val("sv"))

        def fr2(vector, nch=nch):
            vector.drain()
            vector.tensor_mul(tmp[:, 0:TH], qstage[:, 0:TH], cosu[:, 0:TH])
            vector.drain()
            ins = vector.tensor_add(kr[:, nch * TH:(nch + 1) * TH],
                                    tmp[:, 0:TH], swp[:, 0:TH])
            inc(ins, "sv")
        on("vector", fr2)
        rope_done[0] = val("sv")

    for nch in range(NCH):
        emit_k_head(nch)

    def s0cc_in(sync):
        sync.wait_ge(sems["sv"], V_PAY)
        sync.dma_start(out=d_ccin[:, :], in_=arpay[:]).then_inc(sems["scd"], 16)
        cnt["scd"] += 16
    on("sync", s0cc_in)
    V_CCIN = val("scd")

    def emit_s0cc():
        def s0_cc(gpsimd):
            gpsimd.wait_ge(sems["scd"], V_CCIN)
            gpsimd.collective_compute(
                "AllReduce", ALU.add, replica_groups=[list(range(8))],
                ins=[d_ccin.ap().opt()], outs=[d_ccout.ap().opt()],
            ).then_inc(sems["scc"], 1)
            cnt["scc"] += 1
        on("gpsimd", s0_cc)

        def s0cc_out(sync):
            sync.wait_ge(sems["scc"], cnt["scc"])
            sync.dma_start(out=arres[:], in_=d_ccout[:, :]).then_inc(sems["scd"], 16)
            cnt["scd"] += 16
        on("sync", s0cc_out)

    for nch in range(NCH):
        if nch == 3:
            emit_s0cc()
            V_ARRES = val("scd")
            tap("arres", arres[:], "scd", V_ARRES)
        buf, vw = load_strip(d_wvT[:, nch * 128:(nch + 1) * 128], D)
        parts = [(0, 512), (512, 1024), (1024, 1280)]
        evs = []
        for pi, (a, b_) in enumerate(parts):
            bk = "PB6" if (nch * 3 + pi) % 2 == 0 else "PB7"

            def fmm(tensor, buf=buf, vw=vw, a=a, b_=b_, bk=bk):
                tensor.wait_ge(sems[vw[0]], vw[1])
                t_wait_bank(tensor, bk)
                for kc in range(NCH):
                    ins = tensor.matmul(
                        pb[bk][:, 0:b_ - a], buf[:, kc * 128:(kc + 1) * 128],
                        xb[:, kc * TH + a:kc * TH + b_],
                        start=(kc == 0), stop=(kc == NCH - 1))
                inc(ins, "st")
            on("tensor", fmm)
            bank_w[bk] = val("st")
            VW = val("st")

            def fev(vector, a=a, b_=b_, bk=bk, VW=VW):
                vector.wait_ge(sems["st"], VW)
                ins = vector.tensor_copy(qstage[:, a:b_], pb[bk][:, 0:b_ - a])
                inc(ins, "sv")
            on("vector", fev)
            bank_r[bk] = ("sv", val("sv"))
            evs.append(val("sv"))
        strip_consumed("st")

        def ftrv(tensor, nch=nch, ev=evs[2]):
            tensor.wait_ge(sems["sv"], ev)
            t_wait_bank(tensor, "pb2")
            t_wait_bank(tensor, "pb3")
            for tt in range(KT):
                bk_ = pb2 if tt < 8 else pb3
                ins = tensor.transpose(bk_[:, (tt % 8) * 128:(tt % 8 + 1) * 128],
                                       qstage[:, tt * 128:(tt + 1) * 128], ibf[:])
            inc(ins, "st")
        on("tensor", ftrv)
        bank_w["pb2"] = val("st")
        bank_w["pb3"] = val("st")
        VTR = val("st")

        def fevv(vector, nch=nch, VTR=VTR):
            vector.wait_ge(sems["st"], VTR)
            for tt in range(KT):
                bk_ = pb2 if tt < 8 else pb3
                ins = vector.tensor_copy(
                    vT[:, tt * 2048 + nch * 128:tt * 2048 + (nch + 1) * 128],
                    bk_[:, (tt % 8) * 128:(tt % 8 + 1) * 128])
            inc(ins, "sv")
        on("vector", fevv)
        bank_r["pb2"] = ("sv", val("sv"))
        bank_r["pb3"] = ("sv", val("sv"))

    def s1_xm(vector):
        vector.wait_ge(sems["scd"], V_ARRES)
        for b_ in range(2):
            ins = vector.tensor_scalar(
                xmb[:].rearrange("p (c two) -> p c two", two=2)[:, :, b_],
                arres[:, b_ * 32:b_ * 32 + 16], 1.0 / S, None, op0=ALU.mult)
        inc(ins, "sv")
    on("vector", s1_xm)
    V_XM = val("sv")

    for hc in range(4):
        buf, vw = load_strip(d_wc1T[:, hc * 128:(hc + 1) * 128], D)

        def fmm(tensor, buf=buf, vw=vw, hc=hc):
            tensor.wait_ge(sems[vw[0]], vw[1])
            tensor.wait_ge(sems["sv"], V_XM)
            t_wait_bank(tensor, "PB6")
            for kc in range(NCH):
                ins = tensor.matmul(
                    PB6[:, 0:2], buf[:, kc * 128:(kc + 1) * 128],
                    xmb[:, kc * 2:kc * 2 + 2],
                    start=(kc == 0), stop=(kc == NCH - 1))
            inc(ins, "st")
        on("tensor", fmm)
        bank_w["PB6"] = val("st")
        VW = val("st")

        def fev(vector, hc=hc, VW=VW):
            vector.wait_ge(sems["st"], VW)
            ins = vector.tensor_copy(hid[:].rearrange("p (b h) -> p b h", b=2)[:, :, hc],
                                     PB6[:, 0:2])
            inc(ins, "sv")
        on("vector", fev)
        bank_r["PB6"] = ("sv", val("sv"))
        strip_consumed("st")
    V_HID = val("sv")
    tap("hid", hid[:], "ss", V_HID)
    tap("xmb", xmb[:], "sv", V_XM)

    def s1_sig2(scalar):
        scalar.wait_ge(sems["sv"], V_HID)
        ins = scalar.activation(l2t[:], hid[:], AF.Sigmoid)
        inc(ins, "ss")
    on("scalar", s1_sig2)
    V_SIG = val("ss")

    def s1_hv(vector):
        vector.wait_ge(sems["ss"], V_SIG)
        vector.tensor_mul(is1t[:], hid[:], l2t[:])
        vector.drain()
        vector.tensor_mul(is1t[:], is1t[:], wc2r[:])
        vector.drain()
        vector.reduce_sum(pre6[:, 0:1], is1t[:, 0:4], axis=AX.X)
        vector.reduce_sum(pre6[:, 1:2], is1t[:, 4:8], axis=AX.X)
        for g_, dc in enumerate([2, 4, 3, 5]):
            ins = vector.reduce_sum(pre6[:, dc:dc + 1],
                                    arres[:, g_ * 16:(g_ + 1) * 16], axis=AX.X)
        vector.drain()
        inc(ins, "sv")
    on("vector", s1_hv)
    V_HS = val("sv")
    tap("pre6", pre6[:], "sv", V_HS)
    tap("is1t", is1t[:], "sv", V_HS)

    def s1_pred(tensor):
        tensor.wait_ge(sems["sv"], V_HS)
        t_wait_bank(tensor, "PB6")
        ins = tensor.matmul(PB6[:, 0:6], onesf[:], pre6[:, 0:6], start=True, stop=True)
        inc(ins, "st")
    on("tensor", s1_pred)
    bank_w["PB6"] = val("st")
    V_PRED = val("st")

    def s1_chain_v(vector):
        vector.wait_ge(sems["st"], V_PRED)
        vector.tensor_copy(brow[:, 0:6], PB6[:, 0:6])
        vector.drain()
        inc(vector.engine_nop(), "sv")
        vector.tensor_mul(brow[:, 6:8], brow[:, 2:4], brow[:, 2:4])
        vector.drain()
        vector.tensor_scalar(brow[:, 6:8], brow[:, 6:8], -1.0 / NTOT, None, op0=ALU.mult)
        vector.drain()
        vector.tensor_add(brow[:, 6:8], brow[:, 6:8], brow[:, 4:6])
        vector.drain()
        vector.tensor_scalar(brow[:, 6:8], brow[:, 6:8], 1.0 / (NTOT - 1), None,
                             op0=ALU.mult)
        vector.drain()
        ins = vector.tensor_scalar(brow[:, 6:8], brow[:, 6:8], 10.0, -5.0,
                                   op0=ALU.mult, op1=ALU.add)
        vector.drain()
        inc(ins, "sv")
    on("vector", s1_chain_v)
    bank_r["PB6"] = ("sv", val("sv"))
    V_VAR = val("sv")

    def s1_fin_s(scalar):
        scalar.wait_ge(sems["sv"], V_VAR)
        scalar.activation(brow[:, 6:8], brow[:, 6:8], AF.Sigmoid)
        ins = scalar.activation(brow[:, 0:2], brow[:, 0:2], AF.Sigmoid)
        inc(ins, "ss")
    on("scalar", s1_fin_s)
    V_SIG3 = val("ss")

    def s1_w(vector):
        vector.wait_ge(sems["ss"], V_SIG3)
        vector.tensor_add(brow[:, 0:2], brow[:, 0:2], brow[:, 6:8])
        vector.drain()
        vector.reduce_sum(wcol[:], brow[:, 0:2], axis=AX.X)
        vector.drain()
        ins = vector.tensor_scalar(wcol[:], wcol[:], 48.0, 64.0,
                                   op0=ALU.mult, op1=ALU.add)
        vector.drain()
        inc(ins, "sv")
    on("vector", s1_w)
    V_WROW = val("sv")

    def s1_bias(vector):
        vector.tensor_scalar(biasw[:], maskR[:], wcol[:], -1.0,
                             op0=ALU.subtract, op1=ALU.is_le)
        vector.drain()
        vector.tensor_max(biasw[:], biasw[:], maskC[:])
        vector.drain()
        vector.tensor_scalar(biasw[:], biasw[:], 1.0, MASKBIG,
                             op0=ALU.subtract, op1=ALU.mult)
        vector.drain()
        vector.tensor_copy(biasc[:], biasw[:])
        vector.tensor_copy(bias0[:], biasw[:])
        vector.tensor_copy(bias1[:], biasw[:])
        vector.drain()
        vector.tensor_scalar(bias0[:, 0:256], biasw[:, 0:256], sneg[:], None, op0=ALU.min)
        ins = vector.tensor_scalar(bias1[:, 0:128], biasw[:, 0:128], sneg[:], None,
                                   op0=ALU.min)
        vector.drain()
        inc(ins, "sv")
    on("vector", s1_bias)
    V_BIAS = val("sv")
    tap("wcol", wcol[:], "sv", V_BIAS)
    tap("biasc", biasc[:], "sv", V_BIAS)
    tap("kr", kr[:], "sv", V_BIAS)
    tap("vT", vT[:], "sv", V_BIAS)
    tap("xb", xb[:], "sv", V_BIAS)

    qh_rope = {}
    SE0 = val("se")
    ss_base_h = {}
    exp_specs_h = {}
    tail_tr = {}
    pend = {}

    pre_state = {}

    def pre_mms(hn, pi, kcs, first, last):
        buf, vw = pre_state[hn]["strip"]
        bk = "PB6" if pi == 0 else "PB7"
        a, b_ = pi * 512, (pi + 1) * 512

        def fmm(tensor, buf=buf, vw=vw, a=a, b_=b_, bk=bk, kcs=tuple(kcs),
                first=first, last=last):
            if first:
                tensor.wait_ge(sems[vw[0]], vw[1])
                t_wait_bank(tensor, bk)
            for i, kc in enumerate(kcs):
                ins = tensor.matmul(
                    pb[bk][:, 0:512], buf[:, kc * 128:(kc + 1) * 128],
                    xb[:, kc * TH + H + a:kc * TH + H + b_],
                    start=(first and i == 0), stop=(last and i == len(kcs) - 1))
            inc(ins, "st")
        on("tensor", fmm)
        if last:
            bank_w[bk] = val("st")
            pre_state[hn]["mmv%d" % pi] = val("st")
            if pi == 1:
                strip_consumed("st")

    def pre_evac(hn, pi):
        bk = "PB6" if pi == 0 else "PB7"
        a, b_ = pi * 512, (pi + 1) * 512
        VW = pre_state[hn]["mmv%d" % pi]

        def fev(vector, a=a, b_=b_, bk=bk, VW=VW):
            vector.wait_ge(sems["st"], VW)
            ins = vector.tensor_copy(qstage[:, a:b_], pb[bk][:, 0:512])
            inc(ins, "sv")
        on("vector", fev)
        bank_r[bk] = ("sv", val("sv"))
        pre_state[hn]["ev%d" % pi] = val("sv")

    def pre_pswap(hn, pi):
        bk = "PB6" if pi == 0 else "PB7"
        a, b_ = pi * 512, (pi + 1) * 512
        ev = pre_state[hn]["ev%d" % pi]

        def fsw(tensor, a=a, b_=b_, bk=bk, ev=ev):
            tensor.wait_ge(sems["sv"], ev)
            t_wait_bank(tensor, bk)
            ins = tensor.matmul(pb[bk][:, 0:512], pswap[:],
                                qstage[:, a:b_], start=True, stop=True)
            inc(ins, "st")
        on("tensor", fsw)
        bank_w[bk] = val("st")
        pre_state[hn]["sw%d" % pi] = val("st")

    def pre_swpmul(hn, pi):
        bk = "PB6" if pi == 0 else "PB7"
        a, b_ = pi * 512, (pi + 1) * 512
        VW2 = pre_state[hn]["sw%d" % pi]

        def fr1(vector, a=a, b_=b_, bk=bk, VW2=VW2):
            vector.wait_ge(sems["st"], VW2)
            ins = vector.tensor_mul(swp[:, a:b_], pb[bk][:, 0:512],
                                    sinu[:, H + a:H + b_])
            inc(ins, "sv")
        on("vector", fr1)
        bank_r[bk] = ("sv", val("sv"))

    def emit_pre_piece(hn, step):
        if step == 0:
            pre_state[hn] = {
                "strip": load_strip(d_wqT[:, hn * 128:(hn + 1) * 128], D)}
            pre_mms(hn, 0, range(0, 7), True, False)
        elif step == 1:
            pre_mms(hn, 0, range(7, 14), False, False)
        elif step == 2:
            pre_mms(hn, 0, range(14, 16), False, True)
            pre_mms(hn, 1, range(0, 5), True, False)
            pre_evac(hn, 0)
        elif step == 3:
            pre_mms(hn, 1, range(5, 12), False, False)
            pre_pswap(hn, 0)
        elif step == 4:
            pre_mms(hn, 1, range(12, 16), False, True)
            pre_evac(hn, 1)
        elif step == 5:
            pre_pswap(hn, 1)
            pre_swpmul(hn, 0)
        elif step == 6:
            def fr2a(vector):
                vector.drain()
                ins = vector.tensor_mul(tmp[:, 0:T], qstage[:, 0:T],
                                        cosu[:, H:H + T])
                inc(ins, "sv")
            on("vector", fr2a)
            pre_swpmul(hn, 1)
        elif step == 7:
            def fr2b(vector, hn=hn):
                vector.drain()
                ins = vector.tensor_add(qh[:, (hn % 2) * T:(hn % 2 + 1) * T],
                                        tmp[:, 0:T], swp[:, 0:T])
                inc(ins, "sv")
            on("vector", fr2b)
            qh_rope[hn] = val("sv")

    def emit_pre(h):
        for step in range(8):
            emit_pre_piece(h, step)

    def emit_scores_f(fl, h, qt):
        bk = "PB0" if fl % 2 == 0 else "PB1"
        bias_t = bias0 if qt == 0 else (bias1 if qt == 1 else biasc)

        def fsc(tensor, h=h, qt=qt, bk=bk, bias_t=bias_t):
            if qt == 0:
                tensor.wait_ge(sems["sv"], qh_rope[h])
            t_wait_bank(tensor, bk)
            tensor.matmul(pb[bk][:, 0:384],
                          qh[:, (h % 2) * T + qt * 128:(h % 2) * T + (qt + 1) * 128],
                          kr[:, h * TH + qt * 128:h * TH + qt * 128 + 384],
                          start=True, stop=False)
            ins = tensor.matmul(pb[bk][:, 0:384], ibf[:], bias_t[:],
                                start=False, stop=True)
            inc(ins, "st")
        on("tensor", fsc)
        bank_w[bk] = val("st")
        return (h, qt, bk, val("st"))

    def emit_sm1_f(fl):
        h, qt, bk, VSC = pend[fl]
        nslot = fl % 2

        def fsm1(vector, bk=bk, VSC=VSC, nslot=nslot):
            vector.wait_ge(sems["st"], VSC)
            ins = vector.reduce_max(negmax[:, nslot:nslot + 1], pb[bk][:, 0:384],
                                    axis=AX.X, negate=True)
            inc(ins, "sv")
        on("vector", fsm1)
        V_NM = val("sv")
        V_EXP = ss_base_h[h] + qt + 1
        exp_specs_h[h].append((bk, V_NM, fl))
        bank_r[bk] = ("se", V_EXP)
        pend[fl] = (h, qt, V_EXP)

    def emit_sm2_f(fl):
        h, qt, V_EXP = pend[fl]
        pslot = (fl % 3) * 384
        pes = psE if fl % 2 == 0 else psE1
        trw = tail_tr.get(fl - 3, 0)

        def fsm2(vector, V_EXP=V_EXP, pslot=pslot, pes=pes, fl=fl, trw=trw):
            vector.wait_ge(sems["se"], V_EXP)
            if trw:
                vector.wait_ge(sems["st"], trw)
            ins = vector.tensor_scalar(pbt[:, pslot:pslot + 384], pes[:],
                                       rcp[:, fl % 2:fl % 2 + 1], None,
                                       op0=ALU.mult)
            inc(ins, "sv")
        on("vector", fsm2)
        pend[fl] = (h, qt, V_EXP, val("sv"), pslot)

    def emit_ftr_f(fl):
        h, qt, V_EXP, V_PB, pslot = pend[fl]
        tb = "pb2" if fl % 2 == 0 else "pb3"
        tbank = pb2 if fl % 2 == 0 else pb3
        tslot = (fl % 2) * 384

        def ftr(tensor, V_PB=V_PB, pslot=pslot, tbank=tbank, tb=tb):
            tensor.wait_ge(sems["sv"], V_PB)
            t_wait_bank(tensor, tb)
            for j in range(3):
                ins = tensor.transpose(tbank[:, j * 128:(j + 1) * 128],
                                       pbt[:, pslot + j * 128:pslot + (j + 1) * 128],
                                       ibf[:])
            inc(ins, "st")
        on("tensor", ftr)
        bank_w[tb] = val("st")
        tail_tr[fl] = val("st")
        pend[fl] = (h, qt, tslot, val("st"))

    def emit_av_f(fl):
        h, qt, tslot, V_TR = pend[fl]
        tb = "pb2" if fl % 2 == 0 else "pb3"
        tbank = pb2 if fl % 2 == 0 else pb3

        def fevp(vector, V_TR=V_TR, tslot=tslot, tbank=tbank):
            vector.wait_ge(sems["st"], V_TR)
            ins = vector.tensor_copy(pbts[:, tslot:tslot + 384],
                                     tbank[:, 0:384])
            inc(ins, "sv")
        on("vector", fevp)
        bank_r[tb] = ("sv", val("sv"))
        V_EVP = val("sv")

        bkav = "PB4" if fl % 2 == 0 else "PB5"

        def fav(tensor, h=h, qt=qt, bkav=bkav, V_EVP=V_EVP, tslot=tslot):
            tensor.wait_ge(sems["sv"], V_EVP)
            t_wait_bank(tensor, bkav)
            for j in range(3):
                ins = tensor.matmul(
                    pb[bkav][:, 0:128],
                    vT[:, (qt + j) * 2048 + h * 128:(qt + j) * 2048 + (h + 1) * 128],
                    pbts[:, tslot + j * 128:tslot + (j + 1) * 128],
                    start=(j == 0), stop=(j == 2))
            inc(ins, "st")
        on("tensor", fav)
        bank_w[bkav] = val("st")
        pend[fl] = (h, qt, bkav, val("st"))

    def emit_feva_f(fl):
        h, qt, bkav, V_AV = pend.pop(fl)

        def feva(vector, h=h, qt=qt, bkav=bkav, V_AV=V_AV):
            vector.wait_ge(sems["st"], V_AV)
            ins = vector.tensor_copy(ao[:, h * T + qt * 128:h * T + (qt + 1) * 128],
                                     pb[bkav][:, 0:128])
            inc(ins, "sv")
        on("vector", feva)
        bank_r[bkav] = ("sv", val("sv"))

    emit_pre(0)
    NFL = HEADS * QT
    for fl in range(NFL + 4):
        if fl < NFL:
            h, qt = fl // QT, fl % QT
            if qt == 0:
                ss_base_h[h] = SE0 + QT * h
                exp_specs_h[h] = []
            pend[fl] = emit_scores_f(fl, h, qt)
        if 0 <= fl - 2 < NFL:
            emit_ftr_f(fl - 2)
        if 0 <= fl - 3 < NFL:
            emit_av_f(fl - 3)
        if 0 <= fl - 4 < NFL:
            emit_feva_f(fl - 4)
        if fl < NFL and fl // QT + 1 < HEADS:
            emit_pre_piece(fl // QT + 1, fl % QT)
        if fl < NFL:
            emit_sm1_f(fl)
        if 0 <= fl - 1 < NFL:
            emit_sm2_f(fl - 1)
        if fl < NFL and fl % QT == QT - 1:
            h = fl // QT

            def fexp_all(scalar, specs=exp_specs_h[h], h=h):
                for bk_, V_NM_, fl_ in specs:
                    scalar.wait_ge(sems["sv"], V_NM_)
                    pes = psE if fl_ % 2 == 0 else psE1
                    sl_ = fl_ % 2
                    scalar.activation(pes[:], pb[bk_][:, 0:384], AF.Exp,
                                      bias=negmax[:, sl_:sl_ + 1],
                                      accum_out=ssum[:, sl_:sl_ + 1])
                    scalar.drain()
                    scalar.activation(lnv[:, sl_:sl_ + 1], ssum[:, sl_:sl_ + 1],
                                      AF.Ln)
                    scalar.drain()
                    ins = scalar.activation(rcp[:, sl_:sl_ + 1], lnv[:, sl_:sl_ + 1],
                                            AF.Exp, scale=-1.0)
                    inc(ins, "se")
            on("scalar", fexp_all)
            assert val("se") == SE0 + QT * (h + 1)

    V_AOALL = val("sv")
    tap("ao", ao[:], "sv", V_AOALL)

    for oc in range(OC):
        buf, vw = load_strip(d_woT[:, oc * 128:(oc + 1) * 128], D)
        for pi in range(2):
            a, b_ = pi * 512, (pi + 1) * 512
            bk = "PB6" if (oc * 2 + pi) % 2 == 0 else "PB7"

            def fmm(tensor, buf=buf, vw=vw, a=a, b_=b_, bk=bk):
                tensor.wait_ge(sems[vw[0]], vw[1])
                tensor.wait_ge(sems["sv"], V_AOALL)
                t_wait_bank(tensor, bk)
                for kc in range(NCH):
                    ins = tensor.matmul(
                        pb[bk][:, 0:512], buf[:, kc * 128:(kc + 1) * 128],
                        ao[:, kc * T + a:kc * T + b_],
                        start=(kc == 0), stop=(kc == NCH - 1))
                inc(ins, "st")
            on("tensor", fmm)
            bank_w[bk] = val("st")
            VW = val("st")

            def fev(vector, oc=oc, a=a, b_=b_, bk=bk, VW=VW):
                vector.wait_ge(sems["st"], VW)
                ins = vector.tensor_add(rF[:, oc * T + a:oc * T + b_],
                                        pb[bk][:, 0:512],
                                        xb[:, oc * TH + H + a:oc * TH + H + b_])
                inc(ins, "sv")
            on("vector", fev)
            bank_r[bk] = ("sv", val("sv"))
        strip_consumed("st")
    tap("r1", rF[:], "sv", val("sv"))

    def ln_stats():
        evs = []
        for c2 in range(NCH):
            sl = c2 % 2

            def fcast(vector, c2=c2, sl=sl):
                if c2 >= 2:
                    vector.wait_ge(sems["st"], evs[c2 - 2])
                ins = vector.tensor_copy(rb1[:, sl * 1024:(sl + 1) * 1024],
                                         rF[:, c2 * T:(c2 + 1) * T])
                inc(ins, "sv")
            on("vector", fcast)
            VC = val("sv")

            def fsq(scalar, c2=c2, sl=sl, VC=VC):
                scalar.wait_ge(sems["sv"], VC)
                if c2 >= 2:
                    scalar.wait_ge(sems["st"], evs[c2 - 2])
                ins = scalar.activation(rq1[:, sl * 1024:(sl + 1) * 1024],
                                        rb1[:, sl * 1024:(sl + 1) * 1024], AF.Square)
                inc(ins, "ss")
            on("scalar", fsq)
            VQ = val("ss")

            def fst(tensor, c2=c2, sl=sl, VC=VC, VQ=VQ):
                tensor.wait_ge(sems["sv"], VC)
                tensor.wait_ge(sems["ss"], VQ)
                if c2 == 0:
                    for bkx in ["PB0", "PB1", "PB4", "PB5"]:
                        t_wait_bank(tensor, bkx)
                for half in range(2):
                    ha = half * 512
                    bs = pb["PB0" if half == 0 else "PB1"]
                    bq = pb["PB4" if half == 0 else "PB5"]
                    tensor.matmul(bs[:, 0:512], onesb[:],
                                  rb1[:, sl * 1024 + ha:sl * 1024 + ha + 512],
                                  start=(c2 == 0), stop=(c2 == NCH - 1))
                    ins = tensor.matmul(bq[:, 0:512], onesb[:],
                                        rq1[:, sl * 1024 + ha:sl * 1024 + ha + 512],
                                        start=(c2 == 0), stop=(c2 == NCH - 1))
                inc(ins, "st")
            on("tensor", fst)
            evs.append(val("st"))
        for bkx in ["PB0", "PB1", "PB4", "PB5"]:
            bank_w[bkx] = val("st")
        V_ST = val("st")

        def ffin_v(vector, V_ST=V_ST):
            vector.wait_ge(sems["st"], V_ST)
            for half in range(2):
                ha = half * 512
                bmu = pb["PB0" if half == 0 else "PB1"]
                bsq = pb["PB4" if half == 0 else "PB5"]
                vector.tensor_scalar(mubc[:, ha:ha + 512], bmu[:, 0:512], 1.0 / D,
                                     None, op0=ALU.mult)
                vector.tensor_scalar(rsbc[:, ha:ha + 512], bsq[:, 0:512], 1.0 / D,
                                     None, op0=ALU.mult)
                vector.drain()
                vector.tensor_mul(bmu[:, 0:512], mubc[:, ha:ha + 512],
                                  mubc[:, ha:ha + 512])
                vector.drain()
                vector.tensor_sub(rsbc[:, ha:ha + 512], rsbc[:, ha:ha + 512],
                                  bmu[:, 0:512])
                vector.drain()
                ins = vector.tensor_scalar(rsbc[:, ha:ha + 512], rsbc[:, ha:ha + 512],
                                           EPS, None, op0=ALU.add)
            inc(ins, "sv")
        on("vector", ffin_v)
        V_MU = val("sv")
        for bkx in ["PB0", "PB1"]:
            bank_r[bkx] = ("sv", V_MU)

        def ffin_s(scalar, V_MU=V_MU):
            scalar.wait_ge(sems["sv"], V_MU)
            for half in range(2):
                ha = half * 512
                bsq = pb["PB4" if half == 0 else "PB5"]
                scalar.activation(bsq[:, 0:512], rsbc[:, ha:ha + 512], AF.Ln)
                scalar.drain()
                ins = scalar.activation(rsbc[:, ha:ha + 512], bsq[:, 0:512], AF.Exp,
                                        scale=-0.5)
            inc(ins, "ss")
        on("scalar", ffin_s)
        V_RSTD = val("ss")
        for bkx in ["PB4", "PB5"]:
            bank_r[bkx] = ("ss", V_RSTD)
        return V_RSTD

    V_LN1 = ln_stats()

    for c2 in range(NCH):
        def fnorm(vector, c2=c2):
            if c2 == 0:
                vector.wait_ge(sems["ss"], V_LN1)
            vector.tensor_sub(rF[:, c2 * T:(c2 + 1) * T], rF[:, c2 * T:(c2 + 1) * T],
                              mubc[:])
            vector.drain()
            ins = vector.tensor_mul(rF[:, c2 * T:(c2 + 1) * T],
                                    rF[:, c2 * T:(c2 + 1) * T], rsbc[:])
            inc(ins, "sv")
        on("vector", fnorm)
        VN = val("sv")

        def fg(tensor, c2=c2, VN=VN):
            tensor.wait_ge(sems["sv"], VN)
            if c2 == 0:
                t_wait_bank(tensor, "PB0")
                t_wait_bank(tensor, "PB1")
            for half in range(2):
                ha = half * 512
                bk_ = pb["PB0" if half == 0 else "PB1"]
                ins = tensor.matmul(bk_[0:8, 0:512], wg1T[:, c2 * 8:(c2 + 1) * 8],
                                    rF[:, c2 * T + ha:c2 * T + ha + 512],
                                    start=(c2 == 0), stop=(c2 == NCH - 1))
            inc(ins, "st")
        on("tensor", fg)

        def fyb(vector, c2=c2):
            vector.drain()
            ins = vector.tensor_copy(yb[:, c2 * T:(c2 + 1) * T],
                                     rF[:, c2 * T:(c2 + 1) * T])
            inc(ins, "sv")
        on("vector", fyb)
    bank_w["PB0"] = val("st")
    bank_w["PB1"] = val("st")
    V_G1 = val("st")
    V_YB = val("sv")
    tap("yb", yb[:], "sv", V_YB)

    def ftanh(scalar):
        scalar.wait_ge(sems["st"], V_G1)
        scalar.activation(T1[:, 0:512], PB0[0:8, 0:512], AF.Tanh)
        ins = scalar.activation(T1[:, 512:1024], PB1[0:8, 0:512], AF.Tanh)
        inc(ins, "ss")
    on("scalar", ftanh)
    bank_r["PB0"] = ("ss", val("ss"))
    bank_r["PB1"] = ("ss", val("ss"))
    V_T1 = val("ss")

    def fg2(tensor):
        tensor.wait_ge(sems["ss"], V_T1)
        t_wait_bank(tensor, "PB4")
        t_wait_bank(tensor, "PB5")
        tensor.matmul(PB4[0:8, 0:512], wg2T[:], T1[:, 0:512], start=True, stop=True)
        ins = tensor.matmul(PB5[0:8, 0:512], wg2T[:], T1[:, 512:1024],
                            start=True, stop=True)
        inc(ins, "st")
    on("tensor", fg2)
    bank_w["PB4"] = val("st")
    bank_w["PB5"] = val("st")
    V_G2 = val("st")

    def fevL(vector):
        vector.wait_ge(sems["st"], V_G2)
        vector.tensor_copy(LfT[:, 0:512], PB4[0:8, 0:512])
        ins = vector.tensor_copy(LfT[:, 512:1024], PB5[0:8, 0:512])
        inc(ins, "sv")
    on("vector", fevL)
    bank_r["PB4"] = ("sv", val("sv"))
    bank_r["PB5"] = ("sv", val("sv"))
    V_LF = val("sv")

    V_SFB_box = [0]

    def emit_s5():
        def ftrL(tensor):
            tensor.wait_ge(sems["sv"], V_LF)
            t_wait_bank(tensor, "PB6")
            for tc in range(8):
                ins = tensor.transpose(PB6[:, tc * 8:(tc + 1) * 8],
                                       LfT[:, tc * 128:(tc + 1) * 128], idf8[:])
            inc(ins, "st")
        on("tensor", ftrL)
        bank_w["PB6"] = val("st")
        V_LT = val("st")

        def fevLt(vector):
            vector.wait_ge(sems["st"], V_LT)
            ins = vector.tensor_copy(Lt[:], PB6[:, 0:64])
            vector.drain()
            inc(ins, "sv")
        on("vector", fevLt)
        bank_r["PB6"] = ("sv", val("sv"))

        for tc in range(8):
            def ftop_v1(vector, tc=tc):
                Ltc = Lt[:, tc * 8:(tc + 1) * 8]
                vector.reduce_max(m1[:], Ltc, axis=AX.X)
                vector.drain()
                vector.tensor_scalar(is1t[:], Ltc, m1[:], 0.0, op0=ALU.subtract,
                                     op1=ALU.is_ge)
                vector.drain()
                vector.scalar_tensor_tensor(l2t[:], is1t[:], -1e30, Ltc,
                                            op0=ALU.mult, op1=ALU.add)
                vector.drain()
                vector.reduce_max(m2[:], l2t[:], axis=AX.X)
                vector.drain()
                vector.tensor_scalar(is2t[:], l2t[:], m2[:], 0.0, op0=ALU.subtract,
                                     op1=ALU.is_ge)
                ins = vector.tensor_scalar(dd[:], m1[:], m2[:, 0:1], None,
                                           op0=ALU.subtract)
                vector.drain()
                inc(ins, "sv")
            on("vector", ftop_v1)
            V_TOPA = val("sv")

            def ftop_s(scalar, V_TOPA=V_TOPA):
                scalar.wait_ge(sems["sv"], V_TOPA)
                ins = scalar.activation(p1c[:], dd[:], AF.Sigmoid)
                inc(ins, "ss")
            on("scalar", ftop_s)
            V_TOPS = val("ss")

            def ftop_v2(vector, tc=tc, V_TOPS=V_TOPS):
                vector.wait_ge(sems["ss"], V_TOPS)
                vector.tensor_scalar(p2c[:], p1c[:], -1.0, 1.0, op0=ALU.mult, op1=ALU.add)
                vector.tensor_scalar(stok[:, tc * 8:(tc + 1) * 8], is1t[:], p1c[:], None,
                                     op0=ALU.mult)
                vector.drain()
                ins = vector.scalar_tensor_tensor(stok[:, tc * 8:(tc + 1) * 8], is2t[:],
                                                  p2c[:], stok[:, tc * 8:(tc + 1) * 8],
                                                  op0=ALU.mult, op1=ALU.add)
                vector.drain()
                inc(ins, "sv")
            on("vector", ftop_v2)
        V_STOK = val("sv")

        def ftrS(tensor):
            tensor.wait_ge(sems["sv"], V_STOK)
            t_wait_bank(tensor, "PB6")
            t_wait_bank(tensor, "PB7")
            for tc in range(8):
                bk_ = PB6 if tc < 4 else PB7
                ins = tensor.transpose(bk_[0:8, (tc % 4) * 128:(tc % 4 + 1) * 128],
                                       stok[:, tc * 8:(tc + 1) * 8], idf[:])
            inc(ins, "st")
        on("tensor", ftrS)
        bank_w["PB6"] = val("st")
        bank_w["PB7"] = val("st")
        V_SF = val("st")

        def fevS(vector):
            vector.wait_ge(sems["st"], V_SF)
            vector.tensor_copy(sfF[:, 0:512], PB6[0:8, 0:512])
            vector.tensor_copy(sfF[:, 512:1024], PB7[0:8, 0:512])
            vector.drain()
            ins = vector.tensor_copy(sfb[:], sfF[:])
            inc(ins, "sv")
        on("vector", fevS)
        bank_r["PB6"] = ("sv", val("sv"))
        bank_r["PB7"] = ("sv", val("sv"))
        V_SFB_box[0] = val("sv")

    tap("sfF", sfF[:], "sv", V_SFB_box[0])

    def g_bdn(gpsimd):
        gpsimd.dma_start(out=bdn[:], in_=d_bdown).then_inc(sems["sw"], 16)
        cnt["sw"] += 16
    on("gpsimd", g_bdn)

    up_state = {}

    def emit_up_mm(g, el):
        e = 2 * g + el
        ssE = val("ss")
        silu_specs = []
        for hc in range(HC):
            buf, vw = load_strip(d_wupT[e][:, hc * 128:(hc + 1) * 128], D)
            unit = el * HC + hc
            for pi in range(2):
                a, b_ = pi * 512, (pi + 1) * 512
                bk = ["PB4", "PB5", "PB6", "PB7"][(unit * 2 + pi) % 4]

                def fmm(tensor, buf=buf, vw=vw, a=a, b_=b_, bk=bk):
                    tensor.wait_ge(sems[vw[0]], vw[1])
                    t_wait_bank(tensor, bk)
                    for kc in range(NCH):
                        ins = tensor.matmul(
                            pb[bk][:, 0:512], buf[:, kc * 128:(kc + 1) * 128],
                            yb[:, kc * T + a:kc * T + b_],
                            start=(kc == 0), stop=(kc == NCH - 1))
                    inc(ins, "st")
                on("tensor", fmm)
                bank_w[bk] = val("st")
                silu_specs.append((val("st"), bk, e, hc, unit, a, b_))
                bank_r[bk] = ("ss", ssE + len(silu_specs))
            strip_consumed("st")

        def fsilu_all(scalar, silu_specs=silu_specs):
            for VW_, bk_, e_, hc_, unit_, a_, b__ in silu_specs:
                scalar.wait_ge(sems["st"], VW_)
                ins = scalar.activation(
                    hp[:, unit_ * 1024 + a_:unit_ * 1024 + b__],
                    pb[bk_][:, 0:512], AF.Silu,
                    bias=bupr[:, e_ * 8 + hc_:e_ * 8 + hc_ + 1])
                inc(ins, "ss")
        on("scalar", fsilu_all)
        assert val("ss") == ssE + 16
        V_SI = val("ss")

        up_state[e] = (ssE, el)

    def emit_up_fin(g, el):
        e = 2 * g + el
        ssE, _el = up_state[e]
        V_SFB = V_SFB_box[0]
        def fbce(tensor, e=e):
            tensor.wait_ge(sems["sv"], V_SFB)
            t_wait_bank(tensor, "PB0")
            t_wait_bank(tensor, "PB1")
            tensor.matmul(PB0[:, 0:512], selm[:, e * 128:(e + 1) * 128],
                          sfb[:, 0:512], start=True, stop=True)
            ins = tensor.matmul(PB1[:, 0:512], selm[:, e * 128:(e + 1) * 128],
                                sfb[:, 512:1024], start=True, stop=True)
            inc(ins, "st")
        on("tensor", fbce)
        bank_w["PB0"] = val("st")
        bank_w["PB1"] = val("st")
        V_BCE = val("st")
        bslot = el

        def fevbc(vector, bslot=bslot, V_BCE=V_BCE):
            vector.wait_ge(sems["st"], V_BCE)
            vector.tensor_copy(bc[:, bslot * 1024:bslot * 1024 + 512], PB0[:, 0:512])
            ins = vector.tensor_copy(bc[:, bslot * 1024 + 512:(bslot + 1) * 1024],
                                     PB1[:, 0:512])
            vector.drain()
            inc(ins, "sv")
        on("vector", fevbc)
        bank_r["PB0"] = ("sv", val("sv"))
        bank_r["PB1"] = ("sv", val("sv"))

        def fmul_all(vector, el=el, bslot=bslot, ssE=ssE):
            for hc in range(HC):
                unit = el * HC + hc
                vector.wait_ge(sems["ss"], ssE + 2 * (hc + 1))
                ins = vector.tensor_mul(
                    hp[:, unit * 1024:(unit + 1) * 1024],
                    hp[:, unit * 1024:(unit + 1) * 1024],
                    bc[:, bslot * 1024:(bslot + 1) * 1024])
                inc(ins, "sv")
        on("vector", fmul_all)

    for g in range(NPASS):
        if g == 0:
            emit_up_mm(0, 0)
            emit_up_mm(0, 1)
            emit_s5()
            emit_up_fin(0, 0)
            emit_up_fin(0, 1)
        else:
            for el in range(2):
                emit_up_mm(g, el)
                emit_up_fin(g, el)
        V_HP = val("sv")

        for oc in range(OC):
            bkpair = ["PB4", "PB5"] if oc % 2 == 0 else ["PB6", "PB7"]
            buf, vw = load_strip(d_wdn[g, oc], 2048)

            def fmm(tensor, buf=buf, vw=vw, oc=oc, g=g, bkpair=bkpair, V_HP=V_HP):
                tensor.wait_ge(sems[vw[0]], vw[1])
                tensor.wait_ge(sems["sv"], V_HP)
                t_wait_bank(tensor, bkpair[0])
                t_wait_bank(tensor, bkpair[1])
                for c_ in range(16):
                    el = c_ // 8
                    hc = c_ % 8
                    for half in range(2):
                        ha = half * 512
                        stop = (g < NPASS - 1 and c_ == 15)
                        ins = tensor.matmul(
                            pb[bkpair[half]][:, 0:512],
                            buf[:, c_ * 128:(c_ + 1) * 128],
                            hp[:, (el * 8 + hc) * 1024 + ha:
                               (el * 8 + hc) * 1024 + ha + 512],
                            start=(c_ == 0), stop=stop)
                if g == NPASS - 1:
                    tensor.wait_ge(sems["sw"], 16)
                    for half in range(2):
                        ha = half * 512
                        ins = tensor.matmul(
                            pb[bkpair[half]][:, 0:512],
                            bdn[:, oc * 128:(oc + 1) * 128],
                            sfb[:, ha:ha + 512], start=False, stop=True)
                inc(ins, "st")
            on("tensor", fmm)
            strip_consumed("st")
            mm_done = val("st")
            for half in range(2):
                bank_w[bkpair[half]] = mm_done

            def fevd(vector, oc=oc, g=g, bkpair=bkpair, mm_done=mm_done):
                vector.wait_ge(sems["st"], mm_done)
                other = yb if g == 0 else rF
                for half in range(2):
                    ha = half * 512
                    ins = vector.tensor_add(rF[:, oc * T + ha:oc * T + ha + 512],
                                            pb[bkpair[half]][:, 0:512],
                                            other[:, oc * T + ha:oc * T + ha + 512])
                vector.drain()
                inc(ins, "sv")
            on("vector", fevd)
            for half in range(2):
                bank_r[bkpair[half]] = ("sv", val("sv"))

    tap("r2", rF[:], "sv", val("sv"))
    V_LN2 = ln_stats()
    for c2 in range(NCH):
        def fnorm2(vector, c2=c2):
            if c2 == 0:
                vector.wait_ge(sems["ss"], V_LN2)
            vector.tensor_sub(rF[:, c2 * T:(c2 + 1) * T], rF[:, c2 * T:(c2 + 1) * T],
                              mubc[:])
            vector.drain()
            ins = vector.tensor_mul(rF[:, c2 * T:(c2 + 1) * T],
                                    rF[:, c2 * T:(c2 + 1) * T], rsbc[:])
            inc(ins, "sv")
        on("vector", fnorm2)
        VN2 = val("sv")

        def fstorec(sync, c2=c2, VN2=VN2):
            sync.wait_ge(sems["sv"], VN2)
            sync.dma_start(out=d_out[c2 * 128:(c2 + 1) * 128, :],
                           in_=rF[:, c2 * T:(c2 + 1) * T]).then_inc(sems["sst"], 16)
            cnt["sst"] += 16
        on("sync", fstorec)

    def fstore(sync):
        sync.wait_ge(sems["sst"], cnt["sst"])
    on("sync", fstore)

    blocks.__exit__(None, None, None)
    nc._keepalive = _stack
    return nc



def _host_prepare(inputs):
    x = _f32(inputs["L0_attn_x"])
    wqkv = _f32(inputs["L0_attn_w_qkv"])
    wout = _f32(inputs["L0_attn_w_out"])
    wc1 = _f32(inputs["L0_attn_w_c1"])
    wc2 = _f32(inputs["L0_attn_w_c2"])
    wg1 = _f32(inputs["L0_ffn_Wg1"])
    wg2 = _f32(inputs["L0_ffn_Wg2"])
    wup = _f32(inputs["L0_ffn_Wup"])
    bup = _f32(inputs["L0_ffn_bup"])
    wdown = _f32(inputs["L0_ffn_Wdown"])
    bdown = _f32(inputs["L0_ffn_bdown"])

    shared = {}
    shared["wqT"] = np.ascontiguousarray(wqkv[0:2048].T)
    shared["wkT"] = np.ascontiguousarray(wqkv[2048:4096].T)
    shared["wvT"] = np.ascontiguousarray(wqkv[4096:6144].T)
    shared["woT"] = np.ascontiguousarray(wout.T)
    shared["wc1T"] = np.ascontiguousarray(wc1.T)
    wc2r = np.zeros((128, 8), np.float32)
    for hc in range(4):
        for b_ in range(2):
            wc2r[:, b_ * 4 + hc] = wc2[0, hc * 128:(hc + 1) * 128]
    shared["wupT"] = np.ascontiguousarray(np.transpose(wup, (0, 2, 1)))
    wdnT = np.transpose(wdown, (0, 2, 1))
    wdn = np.zeros((NPASS, OC, 2048, 128), np.float32)
    for g in range(NPASS):
        for oc in range(OC):
            wdn[g, oc] = np.concatenate(
                [wdnT[e][:, oc * 128:(oc + 1) * 128] for e in (2 * g, 2 * g + 1)],
                axis=0)
    shared["wdn"] = wdn
    bupr_ = np.zeros((128, 64), np.float32)
    for e in range(E):
        for hc in range(HC):
            bupr_[:, e * 8 + hc] = bup[e, hc * 128:(hc + 1) * 128]
    shared["bdown"] = bdown

    inv = 1.0 / (10000.0 ** (np.arange(0, HD, 2, dtype=np.float64) / HD))
    ftil = np.concatenate([inv, inv])
    sgn = np.where(np.arange(HD) % 2 == 0, -1.0, 1.0)[:, None]
    u = 128.0 ** -0.25

    qi = np.arange(128)[:, None]
    kj = np.arange(384)[None, :]
    rel = (256 + qi - kj).astype(np.float64)
    maskR = np.where(rel < 0, 1e9, rel).astype(np.float32)
    maskC = ((rel >= 0) & (rel <= 63)).astype(np.float32)

    cbf = np.zeros((128, 260), np.float32)
    cbf[:, 0:128] = np.eye(128)[np.arange(128) ^ 1]
    cbf[:, 128:256] = np.eye(128)
    cbf[:, 256] = 1.0

    selm = np.zeros((8, 1024), np.float32)
    for e in range(8):
        selm[e, e * 128:(e + 1) * 128] = 1.0

    cpak = np.zeros((128, 928), np.float32)

    def put(col_byte, arr):
        a = np.asarray(arr, np.float32)
        c0 = (col_byte - 640) // 4
        cpak[: a.shape[0], c0:c0 + a.shape[1]] = a
    put(640, np.eye(128, dtype=np.float32))
    put(1152, np.eye(8, dtype=np.float32))
    put(1216, np.ones((128, 128), np.float32))
    wg1r = np.zeros((128, 128), np.float32)
    wg1Tfull = np.ascontiguousarray(wg1.T)
    for cch in range(16):
        wg1r[:, cch * 8:(cch + 1) * 8] = wg1Tfull[cch * 128:(cch + 1) * 128, :]
    put(1728, wg1r)
    put(2240, np.asarray(wg2.T, np.float32))
    put(2304, wc2r)
    put(2432, np.broadcast_to(np.array([0.0, 0.0], np.float32), (128, 2)))
    put(4096, bupr_)
    mpak = np.concatenate([maskR, maskC], axis=1)

    shared.update({
        "cbf": _bf(cbf),
        "selm": _bf(selm),
        "mpak": np.ascontiguousarray(mpak),
        "onesb": _bf(np.ones((128, 128), np.float32)),
    })
    shared.pop("maskR", None)
    shared.pop("maskC", None)

    in_maps = []
    for c in range(8):
        b_ = c // 4
        p0 = (c % 4) * T
        xs = np.zeros((TH, D), np.float32)
        lo = max(p0 - H, 0)
        xs[H - (p0 - lo):] = x[b_, lo:p0 + T]
        xT = np.ascontiguousarray(xs.T)

        pos = np.arange(p0 - H, p0 + T, dtype=np.float64)
        ang = pos[None, :] * ftil[:, None]
        tabs = np.concatenate([np.cos(ang) * u, np.sin(ang) * sgn * u], axis=1)

        cpak_c = cpak.copy()
        cpak_c[:, (2368 - 640) // 4] = -MASKBIG if p0 == 0 else 0.0
        cpak_c[:, (2432 - 640) // 4:(2432 - 640) // 4 + 2] = (
            np.array([1.0, 0.0] if b_ == 0 else [0.0, 1.0], np.float32))
        m = {
            "xT": xT,
            "tabs": _bf(tabs),
            "cpak": cpak_c,
        }
        m.update(shared)
        in_maps.append(m)
    return in_maps


_NC_CACHE = {}


def kernel(**inputs):
    in_maps = _host_prepare(inputs)
    if "nc" not in _NC_CACHE:
        _NC_CACHE["nc"] = build_program()
    nc = _NC_CACHE["nc"]
    res = run_bass_kernel_spmd(nc, in_maps, core_ids=list(range(8)))
    out = np.zeros((B, S, D), np.float32)
    for c in range(8):
        b_ = c // 4
        p0 = (c % 4) * T
        out[b_, p0:p0 + T] = np.asarray(res.results[c]["outT"], np.float32).T
    return out


# version 30
# speedup vs baseline: 1.1354x; 1.0055x over previous
import numpy as np
import ml_dtypes
import concourse.bass as bass
import concourse.mybir as mybir
from concourse.bass_utils import run_bass_kernel_spmd

F32 = mybir.dt.float32
BF16 = mybir.dt.bfloat16
AF = mybir.ActivationFunctionType
ALU = mybir.AluOpType
AX = mybir.AxisListType
BFND = ml_dtypes.bfloat16

B, S, D = 2, 4096, 2048
T = 1024
H = 256
TH = T + H
NCH = D // 128
HEADS, HD = 16, 128
QT = T // 128
KT = TH // 128
E, HE = 8, 1024
HC = HE // 128
OC = NCH
NPASS = 4
NTOT = S * D
EPS = 1e-5
MASKBIG = 1e4


def _bf(x):
    return np.ascontiguousarray(np.asarray(x, np.float32)).astype(BFND)


def _f32(x):
    return np.ascontiguousarray(np.asarray(x, np.float32))


def build_program(taps=False):
    nc = bass.Bass("TRN2", target_bir_lowering=False)
    tap_list = []

    d_xT = nc.dram_tensor("xT", [D, TH], F32, kind="ExternalInput").ap()
    d_wqT = nc.dram_tensor("wqT", [D, 2048], F32, kind="ExternalInput").ap()
    d_wkT = nc.dram_tensor("wkT", [D, 2048], F32, kind="ExternalInput").ap()
    d_wvT = nc.dram_tensor("wvT", [D, 2048], F32, kind="ExternalInput").ap()
    d_woT = nc.dram_tensor("woT", [D, 2048], F32, kind="ExternalInput").ap()
    d_wc1T = nc.dram_tensor("wc1T", [D, 512], F32, kind="ExternalInput").ap()
    _unused_wg2T = 0
    _unused_wc2r = 0
    d_wupT = nc.dram_tensor("wupT", [E, D, HE], BF16, kind="ExternalInput").ap()
    d_wdn = nc.dram_tensor("wdn", [NPASS, OC, 2048, 128], BF16,
                           kind="ExternalInput").ap()
    _unused_bupr = 0
    d_bdown = nc.dram_tensor("bdown", [8, 2048], F32, kind="ExternalInput").ap()
    d_tabs = nc.dram_tensor("tabs", [128, 2 * TH], BF16, kind="ExternalInput").ap()
    d_cbf = nc.dram_tensor("cbf", [128, 260], BF16, kind="ExternalInput").ap()
    d_onesb = nc.dram_tensor("onesb", [128, 128], BF16, kind="ExternalInput").ap()
    d_selm = nc.dram_tensor("selm", [8, 1024], BF16, kind="ExternalInput").ap()
    _unused_maskR = 0
    _unused_maskC = 0
    _unused_idf = 0
    _unused_idf8 = 0
    _unused_onesf = 0
    _unused_sneg = 0
    _unused_bsel = 0
    d_out = nc.dram_tensor("outT", [D, T], F32, kind="ExternalOutput").ap()
    d_ccin = nc.dram_tensor("ccin", [128, 64], F32)
    d_ccout = nc.dram_tensor("ccout", [128, 64], F32)

    R1 = 0
    R2 = 40960
    R3 = 122880
    RWS = 155648
    RTAB = 163840
    RSM = 168960
    RXS = 185344
    RRW = 195584
    RMI = 203776

    SB_BASE = 16896

    def sb(name, shape, dtype, off):
        return nc.alloc_sbuf_tensor_at(name, shape, dtype, offset=SB_BASE + off).ap()

    xb = sb("xb", [128, NCH * TH], BF16, R1)
    yb = sb("yb", [128, NCH * T], BF16, R1)
    kr = sb("kr", [128, NCH * TH], BF16, R2)
    vT = sb("vT", [128, KT * 2048], BF16, R2 + 40960)
    rF = sb("rF", [128, NCH * T], F32, R2)
    bc = sb("bc", [128, 2 * 1024], BF16, R2 + 69632)
    rb1 = sb("rb1", [128, 2 * 1024], BF16, R2 + 73728)
    rq1 = sb("rq1", [128, 2 * 1024], BF16, R2 + 77824)
    mpak = sb("mpak", [128, 768], F32, R3)
    maskR = mpak[:, 0:384]
    maskC = mpak[:, 384:768]
    biasw = sb("biasw", [128, 384], F32, R3 + 3072)
    ao = sb("ao", [128, NCH * T], BF16, R3)
    hp = sb("hp", [128, 16 * 1024], BF16, R3)
    wsA = sb("wsA", [128, 2048], BF16, RWS)
    wsB = sb("wsB", [128, 2048], BF16, RWS + 4096)
    cosu = sb("cosu", [128, TH], BF16, RTAB)
    sinu = sb("sinu", [128, TH], BF16, RTAB + 2560)
    pswap = sb("pswap", [128, 128], BF16, RSM)
    ibf = sb("ibf", [128, 128], BF16, RSM + 256)
    onescol_bf = sb("onescol_bf", [128, 1], BF16, RSM + 512)
    qh = sb("qh", [128, 2 * T], BF16, RSM + 11520)
    biasc = sb("biasc", [128, 384], BF16, RSM + 3072)
    bias0 = sb("bias0", [128, 384], BF16, RSM + 3840)
    bias1 = sb("bias1", [128, 384], BF16, RSM + 4608)
    pbt = sb("pbt", [128, 1152], BF16, RSM + 5376)
    pbts = sb("pbts", [128, 768], BF16, RSM + 7680)
    xmb = sb("xmb", [128, 32], BF16, RSM + 9216)
    sfb = sb("sfb", [8, 1024], BF16, RSM + 9472)
    bdn = sb("bdn", [8, 2048], BF16, R2 + 65536)
    selm = sb("selm", [8, 1024], BF16, RMI + 5120)
    onesb = sb("onesb", [128, 128], BF16, RMI + 4864)
    xstage = sb("xstage", [128, 2 * TH], F32, RXS)
    qstage = sb("qstage", [128, TH], BF16, RXS)
    swp = sb("swp", [128, TH], BF16, RXS + 2560)
    tmp = sb("tmp", [128, TH], BF16, RXS + 5120)
    psE = sb("psE", [128, 384], F32, RXS + 8192)
    psE1 = sb("psE1", [128, 384], F32, RRW + 4096)
    mubc = sb("mubc", [128, 1024], F32, RXS)
    rsbc = sb("rsbc", [128, 1024], F32, RXS + 4096)
    LfT = sb("LfT", [8, 1024], F32, RXS)
    T1 = sb("T1", [8, 1024], F32, RXS + 4096)
    rows = sb("rows", [1, 2048], F32, RRW)
    sfF = sb("sfF", [8, 1024], F32, RRW)
    stats = sb("stats", [128, 32], F32, RMI)
    arpay = sb("arpay", [128, 64], F32, RMI + 128)
    arres = sb("arres", [128, 64], F32, RMI + 384)
    cpak = sb("cpak", [128, 928], F32, RMI + 640)
    idf = cpak[:, 0:128]
    idf8 = cpak[0:8, 128:136]
    onesf = cpak[:, 144:272]
    wg1T = cpak[:, 272:400]
    wg2T = cpak[0:8, 400:408]
    wc2r = cpak[:, 416:424]
    sneg = cpak[:, 432:433]
    bsel = cpak[:, 448:450]
    hid = sb("hid", [128, 8], F32, RMI + 2496)
    red4 = sb("red4", [128, 4], F32, RMI + 2560)
    hsum = sb("hsum", [128, 2], F32, RMI + 2624)
    wcol = sb("wcol", [128, 1], F32, RMI + 2688)
    Lt = sb("Lt", [128, 64], F32, RMI + 2752)
    stok = sb("stok", [128, 64], F32, RMI + 3072)
    m1 = sb("m1", [128, 1], F32, RMI + 3392)
    m2 = sb("m2", [128, 1], F32, RMI + 3456)
    p1c = sb("p1c", [128, 1], F32, RMI + 3520)
    p2c = sb("p2c", [128, 1], F32, RMI + 3584)
    dd = sb("dd", [128, 1], F32, RMI + 3648)
    l2t = sb("l2t", [128, 8], F32, RMI + 3712)
    is1t = sb("is1t", [128, 8], F32, RMI + 3776)
    is2t = sb("is2t", [128, 8], F32, RMI + 3840)
    negmax = sb("negmax", [128, 2], F32, RMI + 3904)
    ssum = sb("ssum", [128, 2], F32, RMI + 3968)
    rcp = sb("rcp", [128, 2], F32, RMI + 4032)
    lnv = sb("lnv", [128, 2], F32, RMI + 4064)
    bupr = cpak[:, 864:928]
    pre6 = sb("pre6", [128, 8], F32, RMI + 4352)
    brow = sb("brow", [128, 8], F32, RMI + 4608)

    pb = {}
    pb["PB0"] = nc.alloc_psum_tensor("pb0", [128, 512], F32).ap()
    pb["PB1"] = nc.alloc_psum_tensor("pb1", [128, 512], F32).ap()
    pb["pb2"] = nc.alloc_psum_tensor("pb2", [128, 1024], BF16).ap()
    pb["pb3"] = nc.alloc_psum_tensor("pb3", [128, 1024], BF16).ap()
    pb["PB4"] = nc.alloc_psum_tensor("pb4", [128, 512], F32).ap()
    pb["PB5"] = nc.alloc_psum_tensor("pb5", [128, 512], F32).ap()
    pb["PB6"] = nc.alloc_psum_tensor("pb6", [128, 512], F32).ap()
    pb["PB7"] = nc.alloc_psum_tensor("pb7", [128, 512], F32).ap()
    PB0, PB1, PB4, PB5, PB6, PB7 = (pb["PB0"], pb["PB1"], pb["PB4"], pb["PB5"],
                                    pb["PB6"], pb["PB7"])
    pb2, pb3 = pb["pb2"], pb["pb3"]

    from contextlib import ExitStack
    _stack = ExitStack()
    sems = {}
    cnt = {}
    for n in ["sx", "sxa", "sxb", "sw", "swa", "swb", "scc", "st", "sv",
              "ss", "se", "sst", "scd", "sa"]:
        sems[n] = _stack.enter_context(nc.semaphore(n))
        cnt[n] = 0

    def inc(ins, s, k=1):
        cnt[s] += k
        ins.then_inc(sems[s], k)
        return cnt[s]

    def val(s):
        return cnt[s]

    blocks = nc.Block()
    block = blocks.__enter__()

    def on(eng_name, fn):
        getattr(block, eng_name)(fn)

    def tap(name, ap_, sem, v):
        if not taps:
            return
        dt_ = ap_.dtype
        shp = list(ap_.shape)
        dten = nc.dram_tensor("tap_" + name, shp, dt_, kind="ExternalOutput").ap()

        def ftap(sync, dten=dten, ap_=ap_, sem=sem, v=v):
            sync.wait_ge(sems[sem], v)
            sync.dma_start(out=dten, in_=ap_).then_inc(sems["sst"], 16)
            cnt["sst"] += 16
        on("sync", ftap)
        tap_list.append(name)

    d_cpak = nc.dram_tensor("cpak", [128, 928], F32, kind="ExternalInput").ap()
    d_mpak = nc.dram_tensor("mpak", [128, 768], F32, kind="ExternalInput").ap()

    def c_loads(sync):
        for dst, src_ in [
            (cosu[:], d_tabs[:, 0:TH]), (sinu[:], d_tabs[:, TH:2 * TH]),
            (pswap[:], d_cbf[:, 0:128]), (ibf[:], d_cbf[:, 128:256]),
            (onescol_bf[:], d_cbf[:, 256:257]),
            (selm[:], d_selm), (onesb[:], d_onesb),
            (mpak[:], d_mpak), (cpak[:], d_cpak),
        ]:
            sync.dma_start(out=dst, in_=src_,
                           allow_slow_non_contiguous=True).then_inc(sems["sx"], 16)
            cnt["sx"] += 16
    on("sync", c_loads)
    V_CONSTS = val("sx")

    def s_wait_consts(scalar):
        scalar.wait_ge(sems["sx"], V_CONSTS)
    on("scalar", s_wait_consts)

    def t_wait_consts(tensor):
        tensor.wait_ge(sems["sx"], V_CONSTS)
    on("tensor", t_wait_consts)

    strip_state = {"idx": 0, "freed": []}

    def load_strip(src_ap, krows, alt=False):
        i = strip_state["idx"]
        buf = wsA if i % 2 == 0 else wsB
        wsem = "swa" if i % 2 == 0 else "swb"
        nchunks = krows // 128
        qeng = "sync" if (alt and i % 2 == 1) else "gpsimd"

        def fn(eng):
            if i >= 2:
                s_, v_ = strip_state["freed"][i - 2]
                eng.wait_ge(sems[s_], v_)
            eng.dma_start(
                out=buf[:].rearrange("p (c n) -> p c n", c=nchunks),
                in_=src_ap.rearrange("(c p) n -> p c n", p=128),
            ).then_inc(sems[wsem], 16)
            cnt[wsem] += 16
        on(qeng, fn)
        strip_state["idx"] += 1
        return buf, (wsem, val(wsem))

    def strip_consumed(sem_name):
        strip_state["freed"].append((sem_name, val(sem_name)))

    bank_r = {k: ("sv", 0) for k in pb}
    bank_w = {k: 0 for k in pb}

    def t_wait_bank(tensor, bk):
        s_, v_ = bank_r[bk]
        if v_ > 0:
            tensor.wait_ge(sems[s_], v_)

    def s0_memset(vector):
        vector.wait_ge(sems["sx"], V_CONSTS)
        inc(vector.engine_nop(), "sv")
    on("vector", s0_memset)

    x_done = []
    for c in range(NCH):
        sl = c % 2

        xsem = "sxa" if c % 2 == 0 else "sxb"

        def fx(sync, c=c, sl=sl, xsem=xsem):
            if c >= 2:
                sv_, ss_ = x_done[c - 2]
                sync.wait_ge(sems["sv"], sv_)
                sync.wait_ge(sems["ss"], ss_)
            sync.dma_start(out=xstage[:, sl * TH:(sl + 1) * TH],
                           in_=d_xT[c * 128:(c + 1) * 128, :]).then_inc(sems[xsem], 16)
            cnt[xsem] += 16
        on("sync", fx)
        VX = val(xsem)

        def fsq(scalar, c=c, sl=sl, VX=VX, xsem=xsem):
            scalar.wait_ge(sems[xsem], VX)
            if c >= 2:
                scalar.drain()
            ins = scalar.activation(rF[:, sl * 1024:(sl + 1) * 1024],
                                    xstage[:, sl * TH + H:(sl + 1) * TH], AF.Square,
                                    accum_out=stats[:, 16 + c:17 + c])
            inc(ins, "ss")
        on("scalar", fsq)
        VSQ = val("ss")

        def fv(vector, c=c, sl=sl, VX=VX, xsem=xsem):
            vector.wait_ge(sems[xsem], VX)
            vector.tensor_copy(xb[:, c * TH:(c + 1) * TH],
                               xstage[:, sl * TH:(sl + 1) * TH])
            ins = vector.reduce_sum(stats[:, c:c + 1],
                                    xstage[:, sl * TH + H:(sl + 1) * TH], axis=AX.X)
            inc(ins, "sv")
        on("vector", fv)
        x_done.append((val("sv"), val("ss")))
    V_XBALL = x_done[NCH - 1][0]

    def s0_pay(vector):
        vector.wait_ge(sems["ss"], x_done[NCH - 1][1])
        vector.drain()
        vector.tensor_scalar(arpay[:, 0:16], stats[:, 0:16], bsel[:, 0:1], None, op0=ALU.mult)
        vector.tensor_scalar(arpay[:, 16:32], stats[:, 16:32], bsel[:, 0:1], None, op0=ALU.mult)
        vector.tensor_scalar(arpay[:, 32:48], stats[:, 0:16], bsel[:, 1:2], None, op0=ALU.mult)
        ins = vector.tensor_scalar(arpay[:, 48:64], stats[:, 16:32], bsel[:, 1:2], None,
                                   op0=ALU.mult)
        inc(ins, "sv")
    on("vector", s0_pay)
    V_PAY = val("sv")

    rope_done = [0]

    def emit_k_head(nch):
        buf, vw = load_strip(d_wkT[:, nch * 128:(nch + 1) * 128], D)
        parts = [(0, 512), (512, 1024), (1024, 1280)]
        evs = []
        for pi, (a, b_) in enumerate(parts):
            bk = "PB6" if (nch * 3 + pi) % 2 == 0 else "PB7"

            def fmm(tensor, buf=buf, vw=vw, a=a, b_=b_, bk=bk, nch=nch, pi=pi):
                tensor.wait_ge(sems[vw[0]], vw[1])
                t_wait_bank(tensor, bk)
                for kc in range(NCH):
                    if nch == 0 and pi == 0:
                        tensor.wait_ge(sems["sv"], x_done[kc][0])
                    elif nch == 0 and pi == 1 and kc == 0:
                        tensor.wait_ge(sems["sv"], V_XBALL)
                    ins = tensor.matmul(
                        pb[bk][:, 0:b_ - a], buf[:, kc * 128:(kc + 1) * 128],
                        xb[:, kc * TH + a:kc * TH + b_],
                        start=(kc == 0), stop=(kc == NCH - 1))
                inc(ins, "st")
            on("tensor", fmm)
            bank_w[bk] = val("st")
            VW = val("st")

            def fev(vector, a=a, b_=b_, bk=bk, VW=VW):
                vector.wait_ge(sems["st"], VW)
                ins = vector.tensor_copy(qstage[:, a:b_], pb[bk][:, 0:b_ - a])
                inc(ins, "sv")
            on("vector", fev)
            bank_r[bk] = ("sv", val("sv"))
            evs.append(val("sv"))
        strip_consumed("st")
        for pi, (a, b_) in enumerate(parts):
            bk = "PB4" if pi % 2 == 0 else "PB5"

            def fsw(tensor, a=a, b_=b_, bk=bk, ev=evs[pi]):
                tensor.wait_ge(sems["sv"], ev)
                t_wait_bank(tensor, bk)
                ins = tensor.matmul(pb[bk][:, 0:b_ - a], pswap[:],
                                    qstage[:, a:b_], start=True, stop=True)
                inc(ins, "st")
            on("tensor", fsw)
            bank_w[bk] = val("st")
            VW2 = val("st")

            def fr1(vector, a=a, b_=b_, bk=bk, VW2=VW2):
                vector.wait_ge(sems["st"], VW2)
                ins = vector.tensor_mul(swp[:, a:b_], pb[bk][:, 0:b_ - a], sinu[:, a:b_])
                inc(ins, "sv")
            on("vector", fr1)
            bank_r[bk] = ("sv", val("sv"))

        def fr2(vector, nch=nch):
            vector.drain()
            vector.tensor_mul(tmp[:, 0:TH], qstage[:, 0:TH], cosu[:, 0:TH])
            vector.drain()
            ins = vector.tensor_add(kr[:, nch * TH:(nch + 1) * TH],
                                    tmp[:, 0:TH], swp[:, 0:TH])
            inc(ins, "sv")
        on("vector", fr2)
        rope_done[0] = val("sv")

    for nch in range(NCH):
        emit_k_head(nch)

    def s0cc_in(sync):
        sync.wait_ge(sems["sv"], V_PAY)
        sync.dma_start(out=d_ccin[:, :], in_=arpay[:]).then_inc(sems["scd"], 16)
        cnt["scd"] += 16
    on("sync", s0cc_in)
    V_CCIN = val("scd")

    def emit_s0cc():
        def s0_cc(gpsimd):
            gpsimd.wait_ge(sems["scd"], V_CCIN)
            gpsimd.collective_compute(
                "AllReduce", ALU.add, replica_groups=[list(range(8))],
                ins=[d_ccin.ap().opt()], outs=[d_ccout.ap().opt()],
            ).then_inc(sems["scc"], 1)
            cnt["scc"] += 1
        on("gpsimd", s0_cc)

        def s0cc_out(sync):
            sync.wait_ge(sems["scc"], cnt["scc"])
            sync.dma_start(out=arres[:], in_=d_ccout[:, :]).then_inc(sems["scd"], 16)
            cnt["scd"] += 16
        on("sync", s0cc_out)

    for nch in range(NCH):
        if nch == 3:
            emit_s0cc()
            V_ARRES = val("scd")
            tap("arres", arres[:], "scd", V_ARRES)
        buf, vw = load_strip(d_wvT[:, nch * 128:(nch + 1) * 128], D)
        parts = [(0, 512), (512, 1024), (1024, 1280)]
        evs = []
        for pi, (a, b_) in enumerate(parts):
            bk = "PB6" if (nch * 3 + pi) % 2 == 0 else "PB7"

            def fmm(tensor, buf=buf, vw=vw, a=a, b_=b_, bk=bk):
                tensor.wait_ge(sems[vw[0]], vw[1])
                t_wait_bank(tensor, bk)
                for kc in range(NCH):
                    ins = tensor.matmul(
                        pb[bk][:, 0:b_ - a], buf[:, kc * 128:(kc + 1) * 128],
                        xb[:, kc * TH + a:kc * TH + b_],
                        start=(kc == 0), stop=(kc == NCH - 1))
                inc(ins, "st")
            on("tensor", fmm)
            bank_w[bk] = val("st")
            VW = val("st")

            def fev(vector, a=a, b_=b_, bk=bk, VW=VW):
                vector.wait_ge(sems["st"], VW)
                ins = vector.tensor_copy(qstage[:, a:b_], pb[bk][:, 0:b_ - a])
                inc(ins, "sv")
            on("vector", fev)
            bank_r[bk] = ("sv", val("sv"))
            evs.append(val("sv"))
        strip_consumed("st")

        def ftrv(tensor, nch=nch, ev=evs[2]):
            tensor.wait_ge(sems["sv"], ev)
            t_wait_bank(tensor, "pb2")
            t_wait_bank(tensor, "pb3")
            for tt in range(KT):
                bk_ = pb2 if tt < 8 else pb3
                ins = tensor.transpose(bk_[:, (tt % 8) * 128:(tt % 8 + 1) * 128],
                                       qstage[:, tt * 128:(tt + 1) * 128], ibf[:])
            inc(ins, "st")
        on("tensor", ftrv)
        bank_w["pb2"] = val("st")
        bank_w["pb3"] = val("st")
        VTR = val("st")

        def fevv(vector, nch=nch, VTR=VTR):
            vector.wait_ge(sems["st"], VTR)
            for tt in range(KT):
                bk_ = pb2 if tt < 8 else pb3
                ins = vector.tensor_copy(
                    vT[:, tt * 2048 + nch * 128:tt * 2048 + (nch + 1) * 128],
                    bk_[:, (tt % 8) * 128:(tt % 8 + 1) * 128])
            inc(ins, "sv")
        on("vector", fevv)
        bank_r["pb2"] = ("sv", val("sv"))
        bank_r["pb3"] = ("sv", val("sv"))

    def s1_xm(vector):
        vector.wait_ge(sems["scd"], V_ARRES)
        for b_ in range(2):
            ins = vector.tensor_scalar(
                xmb[:].rearrange("p (c two) -> p c two", two=2)[:, :, b_],
                arres[:, b_ * 32:b_ * 32 + 16], 1.0 / S, None, op0=ALU.mult)
        inc(ins, "sv")
    on("vector", s1_xm)
    V_XM = val("sv")

    for hc in range(4):
        buf, vw = load_strip(d_wc1T[:, hc * 128:(hc + 1) * 128], D)

        def fmm(tensor, buf=buf, vw=vw, hc=hc):
            tensor.wait_ge(sems[vw[0]], vw[1])
            tensor.wait_ge(sems["sv"], V_XM)
            t_wait_bank(tensor, "PB6")
            for kc in range(NCH):
                ins = tensor.matmul(
                    PB6[:, 0:2], buf[:, kc * 128:(kc + 1) * 128],
                    xmb[:, kc * 2:kc * 2 + 2],
                    start=(kc == 0), stop=(kc == NCH - 1))
            inc(ins, "st")
        on("tensor", fmm)
        bank_w["PB6"] = val("st")
        VW = val("st")

        def fev(vector, hc=hc, VW=VW):
            vector.wait_ge(sems["st"], VW)
            ins = vector.tensor_copy(hid[:].rearrange("p (b h) -> p b h", b=2)[:, :, hc],
                                     PB6[:, 0:2])
            inc(ins, "sv")
        on("vector", fev)
        bank_r["PB6"] = ("sv", val("sv"))
        strip_consumed("st")
    V_HID = val("sv")
    tap("hid", hid[:], "ss", V_HID)
    tap("xmb", xmb[:], "sv", V_XM)

    def s1_sig2(scalar):
        scalar.wait_ge(sems["sv"], V_HID)
        ins = scalar.activation(l2t[:], hid[:], AF.Sigmoid)
        inc(ins, "ss")
    on("scalar", s1_sig2)
    V_SIG = val("ss")

    def s1_hv(vector):
        vector.wait_ge(sems["ss"], V_SIG)
        vector.tensor_mul(is1t[:], hid[:], l2t[:])
        vector.drain()
        vector.tensor_mul(is1t[:], is1t[:], wc2r[:])
        vector.drain()
        vector.reduce_sum(pre6[:, 0:1], is1t[:, 0:4], axis=AX.X)
        vector.reduce_sum(pre6[:, 1:2], is1t[:, 4:8], axis=AX.X)
        for g_, dc in enumerate([2, 4, 3, 5]):
            ins = vector.reduce_sum(pre6[:, dc:dc + 1],
                                    arres[:, g_ * 16:(g_ + 1) * 16], axis=AX.X)
        vector.drain()
        inc(ins, "sv")
    on("vector", s1_hv)
    V_HS = val("sv")
    tap("pre6", pre6[:], "sv", V_HS)
    tap("is1t", is1t[:], "sv", V_HS)

    def s1_pred(tensor):
        tensor.wait_ge(sems["sv"], V_HS)
        t_wait_bank(tensor, "PB6")
        ins = tensor.matmul(PB6[:, 0:6], onesf[:], pre6[:, 0:6], start=True, stop=True)
        inc(ins, "st")
    on("tensor", s1_pred)
    bank_w["PB6"] = val("st")
    V_PRED = val("st")

    def s1_chain_v(vector):
        vector.wait_ge(sems["st"], V_PRED)
        vector.tensor_copy(brow[:, 0:6], PB6[:, 0:6])
        vector.drain()
        inc(vector.engine_nop(), "sv")
        vector.tensor_mul(brow[:, 6:8], brow[:, 2:4], brow[:, 2:4])
        vector.drain()
        vector.tensor_scalar(brow[:, 6:8], brow[:, 6:8], -1.0 / NTOT, None, op0=ALU.mult)
        vector.drain()
        vector.tensor_add(brow[:, 6:8], brow[:, 6:8], brow[:, 4:6])
        vector.drain()
        vector.tensor_scalar(brow[:, 6:8], brow[:, 6:8], 1.0 / (NTOT - 1), None,
                             op0=ALU.mult)
        vector.drain()
        ins = vector.tensor_scalar(brow[:, 6:8], brow[:, 6:8], 10.0, -5.0,
                                   op0=ALU.mult, op1=ALU.add)
        vector.drain()
        inc(ins, "sv")
    on("vector", s1_chain_v)
    bank_r["PB6"] = ("sv", val("sv"))
    V_VAR = val("sv")

    def s1_fin_s(scalar):
        scalar.wait_ge(sems["sv"], V_VAR)
        scalar.activation(brow[:, 6:8], brow[:, 6:8], AF.Sigmoid)
        ins = scalar.activation(brow[:, 0:2], brow[:, 0:2], AF.Sigmoid)
        inc(ins, "ss")
    on("scalar", s1_fin_s)
    V_SIG3 = val("ss")

    def s1_w(vector):
        vector.wait_ge(sems["ss"], V_SIG3)
        vector.tensor_add(brow[:, 0:2], brow[:, 0:2], brow[:, 6:8])
        vector.drain()
        vector.reduce_sum(wcol[:], brow[:, 0:2], axis=AX.X)
        vector.drain()
        ins = vector.tensor_scalar(wcol[:], wcol[:], 48.0, 64.0,
                                   op0=ALU.mult, op1=ALU.add)
        vector.drain()
        inc(ins, "sv")
    on("vector", s1_w)
    V_WROW = val("sv")

    def s1_bias(vector):
        vector.tensor_scalar(biasw[:], maskR[:], wcol[:], -1.0,
                             op0=ALU.subtract, op1=ALU.is_le)
        vector.drain()
        vector.tensor_max(biasw[:], biasw[:], maskC[:])
        vector.drain()
        vector.tensor_scalar(biasw[:], biasw[:], 1.0, MASKBIG,
                             op0=ALU.subtract, op1=ALU.mult)
        vector.drain()
        vector.tensor_copy(biasc[:], biasw[:])
        vector.tensor_copy(bias0[:], biasw[:])
        vector.tensor_copy(bias1[:], biasw[:])
        vector.drain()
        vector.tensor_scalar(bias0[:, 0:256], biasw[:, 0:256], sneg[:], None, op0=ALU.min)
        ins = vector.tensor_scalar(bias1[:, 0:128], biasw[:, 0:128], sneg[:], None,
                                   op0=ALU.min)
        vector.drain()
        inc(ins, "sv")
    on("vector", s1_bias)
    V_BIAS = val("sv")
    tap("wcol", wcol[:], "sv", V_BIAS)
    tap("biasc", biasc[:], "sv", V_BIAS)
    tap("kr", kr[:], "sv", V_BIAS)
    tap("vT", vT[:], "sv", V_BIAS)
    tap("xb", xb[:], "sv", V_BIAS)

    qh_rope = {}
    SE0 = val("se")
    ss_base_h = {}
    exp_specs_h = {}
    tail_tr = {}
    pend = {}

    pre_state = {}

    def pre_mms(hn, pi, kcs, first, last):
        buf, vw = pre_state[hn]["strip"]
        bk = "PB6" if pi == 0 else "PB7"
        a, b_ = pi * 512, (pi + 1) * 512

        def fmm(tensor, buf=buf, vw=vw, a=a, b_=b_, bk=bk, kcs=tuple(kcs),
                first=first, last=last):
            if first:
                tensor.wait_ge(sems[vw[0]], vw[1])
                t_wait_bank(tensor, bk)
            for i, kc in enumerate(kcs):
                ins = tensor.matmul(
                    pb[bk][:, 0:512], buf[:, kc * 128:(kc + 1) * 128],
                    xb[:, kc * TH + H + a:kc * TH + H + b_],
                    start=(first and i == 0), stop=(last and i == len(kcs) - 1))
            inc(ins, "st")
        on("tensor", fmm)
        if last:
            bank_w[bk] = val("st")
            pre_state[hn]["mmv%d" % pi] = val("st")
            if pi == 1:
                strip_consumed("st")

    def pre_evac(hn, pi):
        bk = "PB6" if pi == 0 else "PB7"
        a, b_ = pi * 512, (pi + 1) * 512
        VW = pre_state[hn]["mmv%d" % pi]

        def fev(vector, a=a, b_=b_, bk=bk, VW=VW):
            vector.wait_ge(sems["st"], VW)
            ins = vector.tensor_copy(qstage[:, a:b_], pb[bk][:, 0:512])
            inc(ins, "sv")
        on("vector", fev)
        bank_r[bk] = ("sv", val("sv"))
        pre_state[hn]["ev%d" % pi] = val("sv")

    def pre_pswap(hn, pi):
        bk = "PB6" if pi == 0 else "PB7"
        a, b_ = pi * 512, (pi + 1) * 512
        ev = pre_state[hn]["ev%d" % pi]

        def fsw(tensor, a=a, b_=b_, bk=bk, ev=ev):
            tensor.wait_ge(sems["sv"], ev)
            t_wait_bank(tensor, bk)
            ins = tensor.matmul(pb[bk][:, 0:512], pswap[:],
                                qstage[:, a:b_], start=True, stop=True)
            inc(ins, "st")
        on("tensor", fsw)
        bank_w[bk] = val("st")
        pre_state[hn]["sw%d" % pi] = val("st")

    def pre_swpmul(hn, pi):
        bk = "PB6" if pi == 0 else "PB7"
        a, b_ = pi * 512, (pi + 1) * 512
        VW2 = pre_state[hn]["sw%d" % pi]

        def fr1(vector, a=a, b_=b_, bk=bk, VW2=VW2):
            vector.wait_ge(sems["st"], VW2)
            ins = vector.tensor_mul(swp[:, a:b_], pb[bk][:, 0:512],
                                    sinu[:, H + a:H + b_])
            inc(ins, "sv")
        on("vector", fr1)
        bank_r[bk] = ("sv", val("sv"))

    def emit_pre_piece(hn, step):
        if step == 0:
            pre_state[hn] = {
                "strip": load_strip(d_wqT[:, hn * 128:(hn + 1) * 128], D)}
            pre_mms(hn, 0, range(0, 7), True, False)
        elif step == 1:
            pre_mms(hn, 0, range(7, 14), False, False)
        elif step == 2:
            pre_mms(hn, 0, range(14, 16), False, True)
            pre_mms(hn, 1, range(0, 5), True, False)
            pre_evac(hn, 0)
        elif step == 3:
            pre_mms(hn, 1, range(5, 12), False, False)
            pre_pswap(hn, 0)
        elif step == 4:
            pre_mms(hn, 1, range(12, 16), False, True)
            pre_evac(hn, 1)
        elif step == 5:
            pre_pswap(hn, 1)
            pre_swpmul(hn, 0)
        elif step == 6:
            def fr2a(vector):
                vector.drain()
                ins = vector.tensor_mul(tmp[:, 0:T], qstage[:, 0:T],
                                        cosu[:, H:H + T])
                inc(ins, "sv")
            on("vector", fr2a)
            pre_swpmul(hn, 1)
        elif step == 7:
            def fr2b(vector, hn=hn):
                vector.drain()
                ins = vector.tensor_add(qh[:, (hn % 2) * T:(hn % 2 + 1) * T],
                                        tmp[:, 0:T], swp[:, 0:T])
                inc(ins, "sv")
            on("vector", fr2b)
            qh_rope[hn] = val("sv")

    def emit_pre(h):
        for step in range(8):
            emit_pre_piece(h, step)

    def emit_scores_f(fl, h, qt):
        bk = "PB0" if fl % 2 == 0 else "PB1"
        bias_t = bias0 if qt == 0 else (bias1 if qt == 1 else biasc)

        def fsc(tensor, h=h, qt=qt, bk=bk, bias_t=bias_t):
            if qt == 0:
                tensor.wait_ge(sems["sv"], qh_rope[h])
            t_wait_bank(tensor, bk)
            tensor.matmul(pb[bk][:, 0:384],
                          qh[:, (h % 2) * T + qt * 128:(h % 2) * T + (qt + 1) * 128],
                          kr[:, h * TH + qt * 128:h * TH + qt * 128 + 384],
                          start=True, stop=False)
            ins = tensor.matmul(pb[bk][:, 0:384], ibf[:], bias_t[:],
                                start=False, stop=True)
            inc(ins, "st")
        on("tensor", fsc)
        bank_w[bk] = val("st")
        return (h, qt, bk, val("st"))

    def emit_sm1_f(fl):
        h, qt, bk, VSC = pend[fl]
        nslot = fl % 2

        def fsm1(vector, bk=bk, VSC=VSC, nslot=nslot):
            vector.wait_ge(sems["st"], VSC)
            ins = vector.reduce_max(negmax[:, nslot:nslot + 1], pb[bk][:, 0:384],
                                    axis=AX.X, negate=True)
            inc(ins, "sv")
        on("vector", fsm1)
        V_NM = val("sv")
        V_EXP = ss_base_h[h] + qt + 1
        exp_specs_h[h].append((bk, V_NM, fl))
        bank_r[bk] = ("se", V_EXP)
        pend[fl] = (h, qt, V_EXP)

    def emit_sm2_f(fl):
        h, qt, V_EXP = pend[fl]
        pslot = (fl % 3) * 384
        pes = psE if fl % 2 == 0 else psE1
        trw = tail_tr.get(fl - 3, 0)

        def fsm2(vector, V_EXP=V_EXP, pslot=pslot, pes=pes, fl=fl, trw=trw):
            vector.wait_ge(sems["se"], V_EXP)
            if trw:
                vector.wait_ge(sems["st"], trw)
            ins = vector.tensor_scalar(pbt[:, pslot:pslot + 384], pes[:],
                                       rcp[:, fl % 2:fl % 2 + 1], None,
                                       op0=ALU.mult)
            inc(ins, "sv")
        on("vector", fsm2)
        pend[fl] = (h, qt, V_EXP, val("sv"), pslot)

    def emit_ftr_f(fl):
        h, qt, V_EXP, V_PB, pslot = pend[fl]
        tb = "pb2" if fl % 2 == 0 else "pb3"
        tbank = pb2 if fl % 2 == 0 else pb3
        tslot = (fl % 2) * 384

        def ftr(tensor, V_PB=V_PB, pslot=pslot, tbank=tbank, tb=tb):
            tensor.wait_ge(sems["sv"], V_PB)
            t_wait_bank(tensor, tb)
            for j in range(3):
                ins = tensor.transpose(tbank[:, j * 128:(j + 1) * 128],
                                       pbt[:, pslot + j * 128:pslot + (j + 1) * 128],
                                       ibf[:])
            inc(ins, "st")
        on("tensor", ftr)
        bank_w[tb] = val("st")
        tail_tr[fl] = val("st")
        pend[fl] = (h, qt, tslot, val("st"))

    def emit_av_f(fl):
        h, qt, tslot, V_TR = pend[fl]
        tb = "pb2" if fl % 2 == 0 else "pb3"
        tbank = pb2 if fl % 2 == 0 else pb3

        def fevp(vector, V_TR=V_TR, tslot=tslot, tbank=tbank):
            vector.wait_ge(sems["st"], V_TR)
            ins = vector.tensor_copy(pbts[:, tslot:tslot + 384],
                                     tbank[:, 0:384])
            inc(ins, "sv")
        on("vector", fevp)
        bank_r[tb] = ("sv", val("sv"))
        V_EVP = val("sv")

        bkav = "PB4" if fl % 2 == 0 else "PB5"

        def fav(tensor, h=h, qt=qt, bkav=bkav, V_EVP=V_EVP, tslot=tslot):
            tensor.wait_ge(sems["sv"], V_EVP)
            t_wait_bank(tensor, bkav)
            for j in range(3):
                ins = tensor.matmul(
                    pb[bkav][:, 0:128],
                    vT[:, (qt + j) * 2048 + h * 128:(qt + j) * 2048 + (h + 1) * 128],
                    pbts[:, tslot + j * 128:tslot + (j + 1) * 128],
                    start=(j == 0), stop=(j == 2))
            inc(ins, "st")
        on("tensor", fav)
        bank_w[bkav] = val("st")
        pend[fl] = (h, qt, bkav, val("st"))

    def emit_feva_f(fl):
        h, qt, bkav, V_AV = pend.pop(fl)

        def feva(vector, h=h, qt=qt, bkav=bkav, V_AV=V_AV):
            vector.wait_ge(sems["st"], V_AV)
            ins = vector.tensor_copy(ao[:, h * T + qt * 128:h * T + (qt + 1) * 128],
                                     pb[bkav][:, 0:128])
            inc(ins, "sv")
        on("vector", feva)
        bank_r[bkav] = ("sv", val("sv"))

    emit_pre(0)
    NFL = HEADS * QT
    for fl in range(NFL + 4):
        if fl < NFL:
            h, qt = fl // QT, fl % QT
            if qt == 0:
                ss_base_h[h] = SE0 + QT * h
                exp_specs_h[h] = []
            pend[fl] = emit_scores_f(fl, h, qt)
        if 0 <= fl - 2 < NFL:
            emit_ftr_f(fl - 2)
        if 0 <= fl - 3 < NFL:
            emit_av_f(fl - 3)
        if 0 <= fl - 4 < NFL:
            emit_feva_f(fl - 4)
        if fl < NFL and fl // QT + 1 < HEADS:
            emit_pre_piece(fl // QT + 1, fl % QT)
        if fl < NFL:
            emit_sm1_f(fl)
        if 0 <= fl - 1 < NFL:
            emit_sm2_f(fl - 1)
        if fl < NFL and fl % QT == QT - 1:
            h = fl // QT

            def fexp_all(scalar, specs=exp_specs_h[h], h=h):
                for bk_, V_NM_, fl_ in specs:
                    scalar.wait_ge(sems["sv"], V_NM_)
                    pes = psE if fl_ % 2 == 0 else psE1
                    sl_ = fl_ % 2
                    scalar.activation(pes[:], pb[bk_][:, 0:384], AF.Exp,
                                      bias=negmax[:, sl_:sl_ + 1],
                                      accum_out=ssum[:, sl_:sl_ + 1])
                    scalar.drain()
                    scalar.activation(lnv[:, sl_:sl_ + 1], ssum[:, sl_:sl_ + 1],
                                      AF.Ln)
                    scalar.drain()
                    ins = scalar.activation(rcp[:, sl_:sl_ + 1], lnv[:, sl_:sl_ + 1],
                                            AF.Exp, scale=-1.0)
                    inc(ins, "se")
            on("scalar", fexp_all)
            assert val("se") == SE0 + QT * (h + 1)

    V_AOALL = val("sv")
    tap("ao", ao[:], "sv", V_AOALL)

    for oc in range(OC):
        buf, vw = load_strip(d_woT[:, oc * 128:(oc + 1) * 128], D)
        for pi in range(2):
            a, b_ = pi * 512, (pi + 1) * 512
            bk = "PB6" if (oc * 2 + pi) % 2 == 0 else "PB7"

            def fmm(tensor, buf=buf, vw=vw, a=a, b_=b_, bk=bk):
                tensor.wait_ge(sems[vw[0]], vw[1])
                tensor.wait_ge(sems["sv"], V_AOALL)
                t_wait_bank(tensor, bk)
                for kc in range(NCH):
                    ins = tensor.matmul(
                        pb[bk][:, 0:512], buf[:, kc * 128:(kc + 1) * 128],
                        ao[:, kc * T + a:kc * T + b_],
                        start=(kc == 0), stop=(kc == NCH - 1))
                inc(ins, "st")
            on("tensor", fmm)
            bank_w[bk] = val("st")
            VW = val("st")

            def fev(vector, oc=oc, a=a, b_=b_, bk=bk, VW=VW):
                vector.wait_ge(sems["st"], VW)
                ins = vector.tensor_add(rF[:, oc * T + a:oc * T + b_],
                                        pb[bk][:, 0:512],
                                        xb[:, oc * TH + H + a:oc * TH + H + b_])
                inc(ins, "sv")
            on("vector", fev)
            bank_r[bk] = ("sv", val("sv"))
        strip_consumed("st")
    tap("r1", rF[:], "sv", val("sv"))

    def ln_stats():
        evs = []
        for c2 in range(NCH):
            sl = c2 % 2

            def fcast(vector, c2=c2, sl=sl):
                if c2 >= 2:
                    vector.wait_ge(sems["st"], evs[c2 - 2])
                ins = vector.tensor_copy(rb1[:, sl * 1024:(sl + 1) * 1024],
                                         rF[:, c2 * T:(c2 + 1) * T])
                inc(ins, "sv")
            on("vector", fcast)
            VC = val("sv")

            def fsq(scalar, c2=c2, sl=sl, VC=VC):
                scalar.wait_ge(sems["sv"], VC)
                if c2 >= 2:
                    scalar.wait_ge(sems["st"], evs[c2 - 2])
                ins = scalar.activation(rq1[:, sl * 1024:(sl + 1) * 1024],
                                        rb1[:, sl * 1024:(sl + 1) * 1024], AF.Square)
                inc(ins, "ss")
            on("scalar", fsq)
            VQ = val("ss")

            def fst(tensor, c2=c2, sl=sl, VC=VC, VQ=VQ):
                tensor.wait_ge(sems["sv"], VC)
                tensor.wait_ge(sems["ss"], VQ)
                if c2 == 0:
                    for bkx in ["PB0", "PB1", "PB4", "PB5"]:
                        t_wait_bank(tensor, bkx)
                for half in range(2):
                    ha = half * 512
                    bs = pb["PB0" if half == 0 else "PB1"]
                    bq = pb["PB4" if half == 0 else "PB5"]
                    tensor.matmul(bs[:, 0:512], onesb[:],
                                  rb1[:, sl * 1024 + ha:sl * 1024 + ha + 512],
                                  start=(c2 == 0), stop=(c2 == NCH - 1))
                    ins = tensor.matmul(bq[:, 0:512], onesb[:],
                                        rq1[:, sl * 1024 + ha:sl * 1024 + ha + 512],
                                        start=(c2 == 0), stop=(c2 == NCH - 1))
                inc(ins, "st")
            on("tensor", fst)
            evs.append(val("st"))
        for bkx in ["PB0", "PB1", "PB4", "PB5"]:
            bank_w[bkx] = val("st")
        V_ST = val("st")

        def ffin_v(vector, V_ST=V_ST):
            vector.wait_ge(sems["st"], V_ST)
            for half in range(2):
                ha = half * 512
                bmu = pb["PB0" if half == 0 else "PB1"]
                bsq = pb["PB4" if half == 0 else "PB5"]
                vector.tensor_scalar(mubc[:, ha:ha + 512], bmu[:, 0:512], 1.0 / D,
                                     None, op0=ALU.mult)
                vector.tensor_scalar(rsbc[:, ha:ha + 512], bsq[:, 0:512], 1.0 / D,
                                     None, op0=ALU.mult)
                vector.drain()
                vector.tensor_mul(bmu[:, 0:512], mubc[:, ha:ha + 512],
                                  mubc[:, ha:ha + 512])
                vector.drain()
                vector.tensor_sub(rsbc[:, ha:ha + 512], rsbc[:, ha:ha + 512],
                                  bmu[:, 0:512])
                vector.drain()
                ins = vector.tensor_scalar(rsbc[:, ha:ha + 512], rsbc[:, ha:ha + 512],
                                           EPS, None, op0=ALU.add)
            inc(ins, "sv")
        on("vector", ffin_v)
        V_MU = val("sv")
        for bkx in ["PB0", "PB1"]:
            bank_r[bkx] = ("sv", V_MU)

        def ffin_s(scalar, V_MU=V_MU):
            scalar.wait_ge(sems["sv"], V_MU)
            for half in range(2):
                ha = half * 512
                bsq = pb["PB4" if half == 0 else "PB5"]
                scalar.activation(bsq[:, 0:512], rsbc[:, ha:ha + 512], AF.Ln)
                scalar.drain()
                ins = scalar.activation(rsbc[:, ha:ha + 512], bsq[:, 0:512], AF.Exp,
                                        scale=-0.5)
            inc(ins, "ss")
        on("scalar", ffin_s)
        V_RSTD = val("ss")
        for bkx in ["PB4", "PB5"]:
            bank_r[bkx] = ("ss", V_RSTD)
        return V_RSTD

    V_LN1 = ln_stats()

    for c2 in range(NCH):
        def fnorm(vector, c2=c2):
            if c2 == 0:
                vector.wait_ge(sems["ss"], V_LN1)
            vector.tensor_sub(rF[:, c2 * T:(c2 + 1) * T], rF[:, c2 * T:(c2 + 1) * T],
                              mubc[:])
            vector.drain()
            ins = vector.tensor_mul(rF[:, c2 * T:(c2 + 1) * T],
                                    rF[:, c2 * T:(c2 + 1) * T], rsbc[:])
            inc(ins, "sv")
        on("vector", fnorm)
        VN = val("sv")

        def fg(tensor, c2=c2, VN=VN):
            tensor.wait_ge(sems["sv"], VN)
            if c2 == 0:
                t_wait_bank(tensor, "PB0")
                t_wait_bank(tensor, "PB1")
            for half in range(2):
                ha = half * 512
                bk_ = pb["PB0" if half == 0 else "PB1"]
                ins = tensor.matmul(bk_[0:8, 0:512], wg1T[:, c2 * 8:(c2 + 1) * 8],
                                    rF[:, c2 * T + ha:c2 * T + ha + 512],
                                    start=(c2 == 0), stop=(c2 == NCH - 1))
            inc(ins, "st")
        on("tensor", fg)

        def fyb(vector, c2=c2):
            vector.drain()
            ins = vector.tensor_copy(yb[:, c2 * T:(c2 + 1) * T],
                                     rF[:, c2 * T:(c2 + 1) * T])
            inc(ins, "sv")
        on("vector", fyb)
    bank_w["PB0"] = val("st")
    bank_w["PB1"] = val("st")
    V_G1 = val("st")
    V_YB = val("sv")
    tap("yb", yb[:], "sv", V_YB)

    def ftanh(scalar):
        scalar.wait_ge(sems["st"], V_G1)
        scalar.activation(T1[:, 0:512], PB0[0:8, 0:512], AF.Tanh)
        ins = scalar.activation(T1[:, 512:1024], PB1[0:8, 0:512], AF.Tanh)
        inc(ins, "ss")
    on("scalar", ftanh)
    bank_r["PB0"] = ("ss", val("ss"))
    bank_r["PB1"] = ("ss", val("ss"))
    V_T1 = val("ss")

    def fg2(tensor):
        tensor.wait_ge(sems["ss"], V_T1)
        t_wait_bank(tensor, "PB4")
        t_wait_bank(tensor, "PB5")
        tensor.matmul(PB4[0:8, 0:512], wg2T[:], T1[:, 0:512], start=True, stop=True)
        ins = tensor.matmul(PB5[0:8, 0:512], wg2T[:], T1[:, 512:1024],
                            start=True, stop=True)
        inc(ins, "st")
    on("tensor", fg2)
    bank_w["PB4"] = val("st")
    bank_w["PB5"] = val("st")
    V_G2 = val("st")

    def fevL(vector):
        vector.wait_ge(sems["st"], V_G2)
        vector.tensor_copy(LfT[:, 0:512], PB4[0:8, 0:512])
        ins = vector.tensor_copy(LfT[:, 512:1024], PB5[0:8, 0:512])
        inc(ins, "sv")
    on("vector", fevL)
    bank_r["PB4"] = ("sv", val("sv"))
    bank_r["PB5"] = ("sv", val("sv"))
    V_LF = val("sv")

    V_SFB_box = [0]

    def emit_s5():
        def ftrL(tensor):
            tensor.wait_ge(sems["sv"], V_LF)
            t_wait_bank(tensor, "PB6")
            for tc in range(8):
                ins = tensor.transpose(PB6[:, tc * 8:(tc + 1) * 8],
                                       LfT[:, tc * 128:(tc + 1) * 128], idf8[:])
            inc(ins, "st")
        on("tensor", ftrL)
        bank_w["PB6"] = val("st")
        V_LT = val("st")

        def fevLt(vector):
            vector.wait_ge(sems["st"], V_LT)
            ins = vector.tensor_copy(Lt[:], PB6[:, 0:64])
            vector.drain()
            inc(ins, "sv")
        on("vector", fevLt)
        bank_r["PB6"] = ("sv", val("sv"))

        for tc in range(8):
            def ftop_v1(vector, tc=tc):
                Ltc = Lt[:, tc * 8:(tc + 1) * 8]
                vector.reduce_max(m1[:], Ltc, axis=AX.X)
                vector.drain()
                vector.tensor_scalar(is1t[:], Ltc, m1[:], 0.0, op0=ALU.subtract,
                                     op1=ALU.is_ge)
                vector.drain()
                vector.scalar_tensor_tensor(l2t[:], is1t[:], -1e30, Ltc,
                                            op0=ALU.mult, op1=ALU.add)
                vector.drain()
                vector.reduce_max(m2[:], l2t[:], axis=AX.X)
                vector.drain()
                vector.tensor_scalar(is2t[:], l2t[:], m2[:], 0.0, op0=ALU.subtract,
                                     op1=ALU.is_ge)
                ins = vector.tensor_scalar(dd[:], m1[:], m2[:, 0:1], None,
                                           op0=ALU.subtract)
                vector.drain()
                inc(ins, "sv")
            on("vector", ftop_v1)
            V_TOPA = val("sv")

            def ftop_s(scalar, V_TOPA=V_TOPA):
                scalar.wait_ge(sems["sv"], V_TOPA)
                ins = scalar.activation(p1c[:], dd[:], AF.Sigmoid)
                inc(ins, "ss")
            on("scalar", ftop_s)
            V_TOPS = val("ss")

            def ftop_v2(vector, tc=tc, V_TOPS=V_TOPS):
                vector.wait_ge(sems["ss"], V_TOPS)
                vector.tensor_scalar(p2c[:], p1c[:], -1.0, 1.0, op0=ALU.mult, op1=ALU.add)
                vector.tensor_scalar(stok[:, tc * 8:(tc + 1) * 8], is1t[:], p1c[:], None,
                                     op0=ALU.mult)
                vector.drain()
                ins = vector.scalar_tensor_tensor(stok[:, tc * 8:(tc + 1) * 8], is2t[:],
                                                  p2c[:], stok[:, tc * 8:(tc + 1) * 8],
                                                  op0=ALU.mult, op1=ALU.add)
                vector.drain()
                inc(ins, "sv")
            on("vector", ftop_v2)
        V_STOK = val("sv")

        def ftrS(tensor):
            tensor.wait_ge(sems["sv"], V_STOK)
            t_wait_bank(tensor, "PB6")
            t_wait_bank(tensor, "PB7")
            for tc in range(8):
                bk_ = PB6 if tc < 4 else PB7
                ins = tensor.transpose(bk_[0:8, (tc % 4) * 128:(tc % 4 + 1) * 128],
                                       stok[:, tc * 8:(tc + 1) * 8], idf[:])
            inc(ins, "st")
        on("tensor", ftrS)
        bank_w["PB6"] = val("st")
        bank_w["PB7"] = val("st")
        V_SF = val("st")

        def fevS(vector):
            vector.wait_ge(sems["st"], V_SF)
            vector.tensor_copy(sfF[:, 0:512], PB6[0:8, 0:512])
            vector.tensor_copy(sfF[:, 512:1024], PB7[0:8, 0:512])
            vector.drain()
            ins = vector.tensor_copy(sfb[:], sfF[:])
            inc(ins, "sv")
        on("vector", fevS)
        bank_r["PB6"] = ("sv", val("sv"))
        bank_r["PB7"] = ("sv", val("sv"))
        V_SFB_box[0] = val("sv")

    tap("sfF", sfF[:], "sv", V_SFB_box[0])

    def g_bdn(gpsimd):
        gpsimd.dma_start(out=bdn[:], in_=d_bdown).then_inc(sems["sw"], 16)
        cnt["sw"] += 16
    on("gpsimd", g_bdn)

    up_state = {}

    def emit_up_mm(g, el):
        e = 2 * g + el
        ssE = val("ss")
        silu_specs = []
        for hc in range(HC):
            buf, vw = load_strip(d_wupT[e][:, hc * 128:(hc + 1) * 128], D,
                                 alt=True)
            unit = el * HC + hc
            for pi in range(2):
                a, b_ = pi * 512, (pi + 1) * 512
                bk = ["PB4", "PB5", "PB6", "PB7"][(unit * 2 + pi) % 4]

                def fmm(tensor, buf=buf, vw=vw, a=a, b_=b_, bk=bk):
                    tensor.wait_ge(sems[vw[0]], vw[1])
                    t_wait_bank(tensor, bk)
                    for kc in range(NCH):
                        ins = tensor.matmul(
                            pb[bk][:, 0:512], buf[:, kc * 128:(kc + 1) * 128],
                            yb[:, kc * T + a:kc * T + b_],
                            start=(kc == 0), stop=(kc == NCH - 1))
                    inc(ins, "st")
                on("tensor", fmm)
                bank_w[bk] = val("st")
                silu_specs.append((val("st"), bk, e, hc, unit, a, b_))
                bank_r[bk] = ("ss", ssE + len(silu_specs))
            strip_consumed("st")

        def fsilu_all(scalar, silu_specs=silu_specs):
            for VW_, bk_, e_, hc_, unit_, a_, b__ in silu_specs:
                scalar.wait_ge(sems["st"], VW_)
                ins = scalar.activation(
                    hp[:, unit_ * 1024 + a_:unit_ * 1024 + b__],
                    pb[bk_][:, 0:512], AF.Silu,
                    bias=bupr[:, e_ * 8 + hc_:e_ * 8 + hc_ + 1])
                inc(ins, "ss")
        on("scalar", fsilu_all)
        assert val("ss") == ssE + 16
        V_SI = val("ss")

        up_state[e] = (ssE, el)

    def emit_up_fin(g, el):
        e = 2 * g + el
        ssE, _el = up_state[e]
        V_SFB = V_SFB_box[0]
        def fbce(tensor, e=e):
            tensor.wait_ge(sems["sv"], V_SFB)
            t_wait_bank(tensor, "PB0")
            t_wait_bank(tensor, "PB1")
            tensor.matmul(PB0[:, 0:512], selm[:, e * 128:(e + 1) * 128],
                          sfb[:, 0:512], start=True, stop=True)
            ins = tensor.matmul(PB1[:, 0:512], selm[:, e * 128:(e + 1) * 128],
                                sfb[:, 512:1024], start=True, stop=True)
            inc(ins, "st")
        on("tensor", fbce)
        bank_w["PB0"] = val("st")
        bank_w["PB1"] = val("st")
        V_BCE = val("st")
        bslot = el

        def fevbc(vector, bslot=bslot, V_BCE=V_BCE):
            vector.wait_ge(sems["st"], V_BCE)
            vector.tensor_copy(bc[:, bslot * 1024:bslot * 1024 + 512], PB0[:, 0:512])
            ins = vector.tensor_copy(bc[:, bslot * 1024 + 512:(bslot + 1) * 1024],
                                     PB1[:, 0:512])
            vector.drain()
            inc(ins, "sv")
        on("vector", fevbc)
        bank_r["PB0"] = ("sv", val("sv"))
        bank_r["PB1"] = ("sv", val("sv"))

        def fmul_all(vector, el=el, bslot=bslot, ssE=ssE):
            for hc in range(HC):
                unit = el * HC + hc
                vector.wait_ge(sems["ss"], ssE + 2 * (hc + 1))
                ins = vector.tensor_mul(
                    hp[:, unit * 1024:(unit + 1) * 1024],
                    hp[:, unit * 1024:(unit + 1) * 1024],
                    bc[:, bslot * 1024:(bslot + 1) * 1024])
                inc(ins, "sv")
        on("vector", fmul_all)

    for g in range(NPASS):
        if g == 0:
            emit_up_mm(0, 0)
            emit_up_mm(0, 1)
            emit_s5()
            emit_up_fin(0, 0)
            emit_up_fin(0, 1)
        else:
            for el in range(2):
                emit_up_mm(g, el)
                emit_up_fin(g, el)
        V_HP = val("sv")

        for oc in range(OC):
            bkpair = ["PB4", "PB5"] if oc % 2 == 0 else ["PB6", "PB7"]
            buf, vw = load_strip(d_wdn[g, oc], 2048, alt=True)

            def fmm(tensor, buf=buf, vw=vw, oc=oc, g=g, bkpair=bkpair, V_HP=V_HP):
                tensor.wait_ge(sems[vw[0]], vw[1])
                tensor.wait_ge(sems["sv"], V_HP)
                t_wait_bank(tensor, bkpair[0])
                t_wait_bank(tensor, bkpair[1])
                for c_ in range(16):
                    el = c_ // 8
                    hc = c_ % 8
                    for half in range(2):
                        ha = half * 512
                        stop = (g < NPASS - 1 and c_ == 15)
                        ins = tensor.matmul(
                            pb[bkpair[half]][:, 0:512],
                            buf[:, c_ * 128:(c_ + 1) * 128],
                            hp[:, (el * 8 + hc) * 1024 + ha:
                               (el * 8 + hc) * 1024 + ha + 512],
                            start=(c_ == 0), stop=stop)
                if g == NPASS - 1:
                    tensor.wait_ge(sems["sw"], 16)
                    for half in range(2):
                        ha = half * 512
                        ins = tensor.matmul(
                            pb[bkpair[half]][:, 0:512],
                            bdn[:, oc * 128:(oc + 1) * 128],
                            sfb[:, ha:ha + 512], start=False, stop=True)
                inc(ins, "st")
            on("tensor", fmm)
            strip_consumed("st")
            mm_done = val("st")
            for half in range(2):
                bank_w[bkpair[half]] = mm_done

            def fevd(vector, oc=oc, g=g, bkpair=bkpair, mm_done=mm_done):
                vector.wait_ge(sems["st"], mm_done)
                other = yb if g == 0 else rF
                for half in range(2):
                    ha = half * 512
                    ins = vector.tensor_add(rF[:, oc * T + ha:oc * T + ha + 512],
                                            pb[bkpair[half]][:, 0:512],
                                            other[:, oc * T + ha:oc * T + ha + 512])
                vector.drain()
                inc(ins, "sv")
            on("vector", fevd)
            for half in range(2):
                bank_r[bkpair[half]] = ("sv", val("sv"))

    tap("r2", rF[:], "sv", val("sv"))
    V_LN2 = ln_stats()
    for c2 in range(NCH):
        def fnorm2(vector, c2=c2):
            if c2 == 0:
                vector.wait_ge(sems["ss"], V_LN2)
            vector.tensor_sub(rF[:, c2 * T:(c2 + 1) * T], rF[:, c2 * T:(c2 + 1) * T],
                              mubc[:])
            vector.drain()
            ins = vector.tensor_mul(rF[:, c2 * T:(c2 + 1) * T],
                                    rF[:, c2 * T:(c2 + 1) * T], rsbc[:])
            inc(ins, "sv")
        on("vector", fnorm2)
        VN2 = val("sv")

        def fstorec(sync, c2=c2, VN2=VN2):
            sync.wait_ge(sems["sv"], VN2)
            sync.dma_start(out=d_out[c2 * 128:(c2 + 1) * 128, :],
                           in_=rF[:, c2 * T:(c2 + 1) * T]).then_inc(sems["sst"], 16)
            cnt["sst"] += 16
        on("sync", fstorec)

    def fstore(sync):
        sync.wait_ge(sems["sst"], cnt["sst"])
    on("sync", fstore)

    blocks.__exit__(None, None, None)
    nc._keepalive = _stack
    return nc



def _host_prepare(inputs):
    x = _f32(inputs["L0_attn_x"])
    wqkv = _f32(inputs["L0_attn_w_qkv"])
    wout = _f32(inputs["L0_attn_w_out"])
    wc1 = _f32(inputs["L0_attn_w_c1"])
    wc2 = _f32(inputs["L0_attn_w_c2"])
    wg1 = _f32(inputs["L0_ffn_Wg1"])
    wg2 = _f32(inputs["L0_ffn_Wg2"])
    wup = _f32(inputs["L0_ffn_Wup"])
    bup = _f32(inputs["L0_ffn_bup"])
    wdown = _f32(inputs["L0_ffn_Wdown"])
    bdown = _f32(inputs["L0_ffn_bdown"])

    shared = {}
    shared["wqT"] = np.ascontiguousarray(wqkv[0:2048].T)
    shared["wkT"] = np.ascontiguousarray(wqkv[2048:4096].T)
    shared["wvT"] = np.ascontiguousarray(wqkv[4096:6144].T)
    shared["woT"] = np.ascontiguousarray(wout.T)
    shared["wc1T"] = np.ascontiguousarray(wc1.T)
    wc2r = np.zeros((128, 8), np.float32)
    for hc in range(4):
        for b_ in range(2):
            wc2r[:, b_ * 4 + hc] = wc2[0, hc * 128:(hc + 1) * 128]
    shared["wupT"] = np.ascontiguousarray(
        np.transpose(wup, (0, 2, 1))).astype(ml_dtypes.bfloat16)
    wdnT = np.transpose(wdown, (0, 2, 1))
    wdn = np.zeros((NPASS, OC, 2048, 128), np.float32)
    for g in range(NPASS):
        for oc in range(OC):
            wdn[g, oc] = np.concatenate(
                [wdnT[e][:, oc * 128:(oc + 1) * 128] for e in (2 * g, 2 * g + 1)],
                axis=0)
    shared["wdn"] = wdn.astype(ml_dtypes.bfloat16)
    bupr_ = np.zeros((128, 64), np.float32)
    for e in range(E):
        for hc in range(HC):
            bupr_[:, e * 8 + hc] = bup[e, hc * 128:(hc + 1) * 128]
    shared["bdown"] = bdown

    inv = 1.0 / (10000.0 ** (np.arange(0, HD, 2, dtype=np.float64) / HD))
    ftil = np.concatenate([inv, inv])
    sgn = np.where(np.arange(HD) % 2 == 0, -1.0, 1.0)[:, None]
    u = 128.0 ** -0.25

    qi = np.arange(128)[:, None]
    kj = np.arange(384)[None, :]
    rel = (256 + qi - kj).astype(np.float64)
    maskR = np.where(rel < 0, 1e9, rel).astype(np.float32)
    maskC = ((rel >= 0) & (rel <= 63)).astype(np.float32)

    cbf = np.zeros((128, 260), np.float32)
    cbf[:, 0:128] = np.eye(128)[np.arange(128) ^ 1]
    cbf[:, 128:256] = np.eye(128)
    cbf[:, 256] = 1.0

    selm = np.zeros((8, 1024), np.float32)
    for e in range(8):
        selm[e, e * 128:(e + 1) * 128] = 1.0

    cpak = np.zeros((128, 928), np.float32)

    def put(col_byte, arr):
        a = np.asarray(arr, np.float32)
        c0 = (col_byte - 640) // 4
        cpak[: a.shape[0], c0:c0 + a.shape[1]] = a
    put(640, np.eye(128, dtype=np.float32))
    put(1152, np.eye(8, dtype=np.float32))
    put(1216, np.ones((128, 128), np.float32))
    wg1r = np.zeros((128, 128), np.float32)
    wg1Tfull = np.ascontiguousarray(wg1.T)
    for cch in range(16):
        wg1r[:, cch * 8:(cch + 1) * 8] = wg1Tfull[cch * 128:(cch + 1) * 128, :]
    put(1728, wg1r)
    put(2240, np.asarray(wg2.T, np.float32))
    put(2304, wc2r)
    put(2432, np.broadcast_to(np.array([0.0, 0.0], np.float32), (128, 2)))
    put(4096, bupr_)
    mpak = np.concatenate([maskR, maskC], axis=1)

    shared.update({
        "cbf": _bf(cbf),
        "selm": _bf(selm),
        "mpak": np.ascontiguousarray(mpak),
        "onesb": _bf(np.ones((128, 128), np.float32)),
    })
    shared.pop("maskR", None)
    shared.pop("maskC", None)

    in_maps = []
    for c in range(8):
        b_ = c // 4
        p0 = (c % 4) * T
        xs = np.zeros((TH, D), np.float32)
        lo = max(p0 - H, 0)
        xs[H - (p0 - lo):] = x[b_, lo:p0 + T]
        xT = np.ascontiguousarray(xs.T)

        pos = np.arange(p0 - H, p0 + T, dtype=np.float64)
        ang = pos[None, :] * ftil[:, None]
        tabs = np.concatenate([np.cos(ang) * u, np.sin(ang) * sgn * u], axis=1)

        cpak_c = cpak.copy()
        cpak_c[:, (2368 - 640) // 4] = -MASKBIG if p0 == 0 else 0.0
        cpak_c[:, (2432 - 640) // 4:(2432 - 640) // 4 + 2] = (
            np.array([1.0, 0.0] if b_ == 0 else [0.0, 1.0], np.float32))
        m = {
            "xT": xT,
            "tabs": _bf(tabs),
            "cpak": cpak_c,
        }
        m.update(shared)
        in_maps.append(m)
    return in_maps


_NC_CACHE = {}


def kernel(**inputs):
    in_maps = _host_prepare(inputs)
    if "nc" not in _NC_CACHE:
        _NC_CACHE["nc"] = build_program()
    nc = _NC_CACHE["nc"]
    res = run_bass_kernel_spmd(nc, in_maps, core_ids=list(range(8)))
    out = np.zeros((B, S, D), np.float32)
    for c in range(8):
        b_ = c // 4
        p0 = (c % 4) * T
        out[b_, p0:p0 + T] = np.asarray(res.results[c]["outT"], np.float32).T
    return out


# version 32 (final)
# speedup vs baseline: 1.1367x; 1.0012x over previous
import numpy as np
import ml_dtypes
import concourse.bass as bass
import concourse.mybir as mybir
from concourse.bass_utils import run_bass_kernel_spmd

F32 = mybir.dt.float32
BF16 = mybir.dt.bfloat16
AF = mybir.ActivationFunctionType
ALU = mybir.AluOpType
AX = mybir.AxisListType
BFND = ml_dtypes.bfloat16

B, S, D = 2, 4096, 2048
T = 1024
H = 256
TH = T + H
NCH = D // 128
HEADS, HD = 16, 128
QT = T // 128
KT = TH // 128
E, HE = 8, 1024
HC = HE // 128
OC = NCH
NPASS = 4
NTOT = S * D
EPS = 1e-5
MASKBIG = 1e4


def _bf(x):
    return np.ascontiguousarray(np.asarray(x, np.float32)).astype(BFND)


def _f32(x):
    return np.ascontiguousarray(np.asarray(x, np.float32))


def build_program(taps=False):
    nc = bass.Bass("TRN2", target_bir_lowering=False)
    tap_list = []

    d_xT = nc.dram_tensor("xT", [D, TH], F32, kind="ExternalInput").ap()
    d_wqT = nc.dram_tensor("wqT", [D, 2048], F32, kind="ExternalInput").ap()
    d_wkT = nc.dram_tensor("wkT", [D, 2048], F32, kind="ExternalInput").ap()
    d_wvT = nc.dram_tensor("wvT", [D, 2048], F32, kind="ExternalInput").ap()
    d_woT = nc.dram_tensor("woT", [D, 2048], F32, kind="ExternalInput").ap()
    d_wc1T = nc.dram_tensor("wc1T", [D, 512], F32, kind="ExternalInput").ap()
    _unused_wg2T = 0
    _unused_wc2r = 0
    d_wupT = nc.dram_tensor("wupT", [E, D, HE], BF16, kind="ExternalInput").ap()
    d_wdn = nc.dram_tensor("wdn", [NPASS, OC, 2048, 128], BF16,
                           kind="ExternalInput").ap()
    _unused_bupr = 0
    d_bdown = nc.dram_tensor("bdown", [8, 2048], F32, kind="ExternalInput").ap()
    d_tabs = nc.dram_tensor("tabs", [128, 2 * TH], BF16, kind="ExternalInput").ap()
    d_cbf = nc.dram_tensor("cbf", [128, 260], BF16, kind="ExternalInput").ap()
    d_onesb = nc.dram_tensor("onesb", [128, 128], BF16, kind="ExternalInput").ap()
    d_selm = nc.dram_tensor("selm", [8, 1024], BF16, kind="ExternalInput").ap()
    _unused_maskR = 0
    _unused_maskC = 0
    _unused_idf = 0
    _unused_idf8 = 0
    _unused_onesf = 0
    _unused_sneg = 0
    _unused_bsel = 0
    d_out = nc.dram_tensor("outT", [D, T], F32, kind="ExternalOutput").ap()
    d_ccin = nc.dram_tensor("ccin", [128, 64], F32)
    d_ccout = nc.dram_tensor("ccout", [128, 64], F32)

    R1 = 0
    R2 = 40960
    R3 = 122880
    RWS = 155648
    RTAB = 163840
    RSM = 168960
    RXS = 185344
    RRW = 195584
    RMI = 203776

    SB_BASE = 16896

    def sb(name, shape, dtype, off):
        return nc.alloc_sbuf_tensor_at(name, shape, dtype, offset=SB_BASE + off).ap()

    xb = sb("xb", [128, NCH * TH], BF16, R1)
    yb = sb("yb", [128, NCH * T], BF16, R1)
    kr = sb("kr", [128, NCH * TH], BF16, R2)
    vT = sb("vT", [128, KT * 2048], BF16, R2 + 40960)
    rF = sb("rF", [128, NCH * T], F32, R2)
    bc = sb("bc", [128, 2 * 1024], BF16, R2 + 69632)
    rb1 = sb("rb1", [128, 2 * 1024], BF16, R2 + 73728)
    rq1 = sb("rq1", [128, 2 * 1024], BF16, R2 + 77824)
    mpak = sb("mpak", [128, 768], F32, R3)
    maskR = mpak[:, 0:384]
    maskC = mpak[:, 384:768]
    biasw = sb("biasw", [128, 384], F32, R3 + 3072)
    ao = sb("ao", [128, NCH * T], BF16, R3)
    hp = sb("hp", [128, 16 * 1024], BF16, R3)
    wsA = sb("wsA", [128, 2048], BF16, RWS)
    wsB = sb("wsB", [128, 2048], BF16, RWS + 4096)
    cosu = sb("cosu", [128, TH], BF16, RTAB)
    sinu = sb("sinu", [128, TH], BF16, RTAB + 2560)
    pswap = sb("pswap", [128, 128], BF16, RSM)
    ibf = sb("ibf", [128, 128], BF16, RSM + 256)
    onescol_bf = sb("onescol_bf", [128, 1], BF16, RSM + 512)
    qh = sb("qh", [128, 2 * T], BF16, RSM + 11520)
    biasc = sb("biasc", [128, 384], BF16, RSM + 3072)
    bias0 = sb("bias0", [128, 384], BF16, RSM + 3840)
    bias1 = sb("bias1", [128, 384], BF16, RSM + 4608)
    pbt = sb("pbt", [128, 1152], BF16, RSM + 5376)
    pbts = sb("pbts", [128, 768], BF16, RSM + 7680)
    xmb = sb("xmb", [128, 32], BF16, RSM + 9216)
    sfb = sb("sfb", [8, 1024], BF16, RSM + 9472)
    bdn = sb("bdn", [8, 2048], BF16, R2 + 65536)
    selm = sb("selm", [8, 1024], BF16, RMI + 5120)
    onesb = sb("onesb", [128, 128], BF16, RMI + 4864)
    xstage = sb("xstage", [128, 2 * TH], F32, RXS)
    qstage = sb("qstage", [128, TH], BF16, RXS)
    swp = sb("swp", [128, TH], BF16, RXS + 2560)
    tmp = sb("tmp", [128, TH], BF16, RXS + 5120)
    psE = sb("psE", [128, 384], F32, RXS + 8192)
    psE1 = sb("psE1", [128, 384], F32, RRW + 4096)
    mubc = sb("mubc", [128, 1024], F32, RXS)
    rsbc = sb("rsbc", [128, 1024], F32, RXS + 4096)
    LfT = sb("LfT", [8, 1024], F32, RXS)
    T1 = sb("T1", [8, 1024], F32, RXS + 4096)
    rows = sb("rows", [1, 2048], F32, RRW)
    sfF = sb("sfF", [8, 1024], F32, RRW)
    stats = sb("stats", [128, 32], F32, RMI)
    arpay = sb("arpay", [128, 64], F32, RMI + 128)
    arres = sb("arres", [128, 64], F32, RMI + 384)
    cpak = sb("cpak", [128, 928], F32, RMI + 640)
    idf = cpak[:, 0:128]
    idf8 = cpak[0:8, 128:136]
    onesf = cpak[:, 144:272]
    wg1T = cpak[:, 272:400]
    wg2T = cpak[0:8, 400:408]
    wc2r = cpak[:, 416:424]
    sneg = cpak[:, 432:433]
    bsel = cpak[:, 448:450]
    hid = sb("hid", [128, 8], F32, RMI + 2496)
    red4 = sb("red4", [128, 4], F32, RMI + 2560)
    hsum = sb("hsum", [128, 2], F32, RMI + 2624)
    wcol = sb("wcol", [128, 1], F32, RMI + 2688)
    Lt = sb("Lt", [128, 64], F32, RMI + 2752)
    stok = sb("stok", [128, 64], F32, RMI + 3072)
    m1 = sb("m1", [128, 1], F32, RMI + 3392)
    m2 = sb("m2", [128, 1], F32, RMI + 3456)
    p1c = sb("p1c", [128, 1], F32, RMI + 3520)
    p2c = sb("p2c", [128, 1], F32, RMI + 3584)
    dd = sb("dd", [128, 1], F32, RMI + 3648)
    l2t = sb("l2t", [128, 8], F32, RMI + 3712)
    is1t = sb("is1t", [128, 8], F32, RMI + 3776)
    is2t = sb("is2t", [128, 8], F32, RMI + 3840)
    negmax = sb("negmax", [128, 2], F32, RMI + 3904)
    ssum = sb("ssum", [128, 2], F32, RMI + 3968)
    rcp = sb("rcp", [128, 2], F32, RMI + 4032)
    lnv = sb("lnv", [128, 2], F32, RMI + 4064)
    bupr = cpak[:, 864:928]
    pre6 = sb("pre6", [128, 8], F32, RMI + 4352)
    brow = sb("brow", [128, 8], F32, RMI + 4608)

    pb = {}
    pb["PB0"] = nc.alloc_psum_tensor("pb0", [128, 512], F32).ap()
    pb["PB1"] = nc.alloc_psum_tensor("pb1", [128, 512], F32).ap()
    pb["pb2"] = nc.alloc_psum_tensor("pb2", [128, 1024], BF16).ap()
    pb["pb3"] = nc.alloc_psum_tensor("pb3", [128, 1024], BF16).ap()
    pb["PB4"] = nc.alloc_psum_tensor("pb4", [128, 512], F32).ap()
    pb["PB5"] = nc.alloc_psum_tensor("pb5", [128, 512], F32).ap()
    pb["PB6"] = nc.alloc_psum_tensor("pb6", [128, 512], F32).ap()
    pb["PB7"] = nc.alloc_psum_tensor("pb7", [128, 512], F32).ap()
    PB0, PB1, PB4, PB5, PB6, PB7 = (pb["PB0"], pb["PB1"], pb["PB4"], pb["PB5"],
                                    pb["PB6"], pb["PB7"])
    pb2, pb3 = pb["pb2"], pb["pb3"]

    from contextlib import ExitStack
    _stack = ExitStack()
    sems = {}
    cnt = {}
    for n in ["sx", "sxa", "sxb", "sw", "swa", "swb", "scc", "st", "sv",
              "ss", "se", "sst", "scd", "sa"]:
        sems[n] = _stack.enter_context(nc.semaphore(n))
        cnt[n] = 0

    def inc(ins, s, k=1):
        cnt[s] += k
        ins.then_inc(sems[s], k)
        return cnt[s]

    def val(s):
        return cnt[s]

    blocks = nc.Block()
    block = blocks.__enter__()

    def on(eng_name, fn):
        getattr(block, eng_name)(fn)

    def tap(name, ap_, sem, v):
        if not taps:
            return
        dt_ = ap_.dtype
        shp = list(ap_.shape)
        dten = nc.dram_tensor("tap_" + name, shp, dt_, kind="ExternalOutput").ap()

        def ftap(sync, dten=dten, ap_=ap_, sem=sem, v=v):
            sync.wait_ge(sems[sem], v)
            sync.dma_start(out=dten, in_=ap_).then_inc(sems["sst"], 16)
            cnt["sst"] += 16
        on("sync", ftap)
        tap_list.append(name)

    d_cpak = nc.dram_tensor("cpak", [128, 928], F32, kind="ExternalInput").ap()
    d_mpak = nc.dram_tensor("mpak", [128, 768], F32, kind="ExternalInput").ap()

    def c_loads(sync):
        for dst, src_ in [
            (cosu[:], d_tabs[:, 0:TH]), (sinu[:], d_tabs[:, TH:2 * TH]),
            (pswap[:], d_cbf[:, 0:128]), (ibf[:], d_cbf[:, 128:256]),
            (onescol_bf[:], d_cbf[:, 256:257]),
            (selm[:], d_selm), (onesb[:], d_onesb),
            (mpak[:], d_mpak), (cpak[:], d_cpak),
        ]:
            sync.dma_start(out=dst, in_=src_,
                           allow_slow_non_contiguous=True).then_inc(sems["sx"], 16)
            cnt["sx"] += 16
    on("sync", c_loads)
    V_CONSTS = val("sx")

    def s_wait_consts(scalar):
        scalar.wait_ge(sems["sx"], V_CONSTS)
    on("scalar", s_wait_consts)

    def t_wait_consts(tensor):
        tensor.wait_ge(sems["sx"], V_CONSTS)
    on("tensor", t_wait_consts)

    strip_state = {"idx": 0, "freed": []}

    def load_strip(src_ap, krows, alt=False):
        i = strip_state["idx"]
        buf = wsA if i % 2 == 0 else wsB
        wsem = "swa" if i % 2 == 0 else "swb"
        nchunks = krows // 128
        qeng = "sync" if (alt and i % 2 == 1) else "gpsimd"

        def fn(eng):
            if i >= 2:
                s_, v_ = strip_state["freed"][i - 2]
                eng.wait_ge(sems[s_], v_)
            eng.dma_start(
                out=buf[:].rearrange("p (c n) -> p c n", c=nchunks),
                in_=src_ap.rearrange("(c p) n -> p c n", p=128),
            ).then_inc(sems[wsem], 16)
            cnt[wsem] += 16
        on(qeng, fn)
        strip_state["idx"] += 1
        return buf, (wsem, val(wsem))

    def strip_consumed(sem_name):
        strip_state["freed"].append((sem_name, val(sem_name)))

    bank_r = {k: ("sv", 0) for k in pb}
    bank_w = {k: 0 for k in pb}

    def t_wait_bank(tensor, bk):
        s_, v_ = bank_r[bk]
        if v_ > 0:
            tensor.wait_ge(sems[s_], v_)

    def s0_memset(vector):
        vector.wait_ge(sems["sx"], V_CONSTS)
        inc(vector.engine_nop(), "sv")
    on("vector", s0_memset)

    x_done = []
    for c in range(NCH):
        sl = c % 2

        xsem = "sxa" if c % 2 == 0 else "sxb"

        def fx(sync, c=c, sl=sl, xsem=xsem):
            if c >= 2:
                sv_, ss_ = x_done[c - 2]
                sync.wait_ge(sems["sv"], sv_)
                sync.wait_ge(sems["ss"], ss_)
            sync.dma_start(out=xstage[:, sl * TH:(sl + 1) * TH],
                           in_=d_xT[c * 128:(c + 1) * 128, :]).then_inc(sems[xsem], 16)
            cnt[xsem] += 16
        on("sync", fx)
        VX = val(xsem)

        def fsq(scalar, c=c, sl=sl, VX=VX, xsem=xsem):
            scalar.wait_ge(sems[xsem], VX)
            if c >= 2:
                scalar.drain()
            ins = scalar.activation(rF[:, sl * 1024:(sl + 1) * 1024],
                                    xstage[:, sl * TH + H:(sl + 1) * TH], AF.Square,
                                    accum_out=stats[:, 16 + c:17 + c])
            inc(ins, "ss")
        on("scalar", fsq)
        VSQ = val("ss")

        def fv(vector, c=c, sl=sl, VX=VX, xsem=xsem):
            vector.wait_ge(sems[xsem], VX)
            vector.tensor_copy(xb[:, c * TH:(c + 1) * TH],
                               xstage[:, sl * TH:(sl + 1) * TH])
            ins = vector.reduce_sum(stats[:, c:c + 1],
                                    xstage[:, sl * TH + H:(sl + 1) * TH], axis=AX.X)
            inc(ins, "sv")
        on("vector", fv)
        x_done.append((val("sv"), val("ss")))
    V_XBALL = x_done[NCH - 1][0]

    def s0_pay(vector):
        vector.wait_ge(sems["ss"], x_done[NCH - 1][1])
        vector.drain()
        vector.tensor_scalar(arpay[:, 0:16], stats[:, 0:16], bsel[:, 0:1], None, op0=ALU.mult)
        vector.tensor_scalar(arpay[:, 16:32], stats[:, 16:32], bsel[:, 0:1], None, op0=ALU.mult)
        vector.tensor_scalar(arpay[:, 32:48], stats[:, 0:16], bsel[:, 1:2], None, op0=ALU.mult)
        ins = vector.tensor_scalar(arpay[:, 48:64], stats[:, 16:32], bsel[:, 1:2], None,
                                   op0=ALU.mult)
        inc(ins, "sv")
    on("vector", s0_pay)
    V_PAY = val("sv")

    rope_done = [0]

    def emit_k_head(nch):
        buf, vw = load_strip(d_wkT[:, nch * 128:(nch + 1) * 128], D)
        parts = [(0, 512), (512, 1024), (1024, 1280)]
        evs = []
        for pi, (a, b_) in enumerate(parts):
            bk = "PB6" if (nch * 3 + pi) % 2 == 0 else "PB7"

            def fmm(tensor, buf=buf, vw=vw, a=a, b_=b_, bk=bk, nch=nch, pi=pi):
                tensor.wait_ge(sems[vw[0]], vw[1])
                t_wait_bank(tensor, bk)
                for kc in range(NCH):
                    if nch == 0 and pi == 0:
                        tensor.wait_ge(sems["sv"], x_done[kc][0])
                    elif nch == 0 and pi == 1 and kc == 0:
                        tensor.wait_ge(sems["sv"], V_XBALL)
                    ins = tensor.matmul(
                        pb[bk][:, 0:b_ - a], buf[:, kc * 128:(kc + 1) * 128],
                        xb[:, kc * TH + a:kc * TH + b_],
                        start=(kc == 0), stop=(kc == NCH - 1))
                inc(ins, "st")
            on("tensor", fmm)
            bank_w[bk] = val("st")
            VW = val("st")

            def fev(vector, a=a, b_=b_, bk=bk, VW=VW):
                vector.wait_ge(sems["st"], VW)
                ins = vector.tensor_copy(qstage[:, a:b_], pb[bk][:, 0:b_ - a])
                inc(ins, "sv")
            on("vector", fev)
            bank_r[bk] = ("sv", val("sv"))
            evs.append(val("sv"))
        strip_consumed("st")
        for pi, (a, b_) in enumerate(parts):
            bk = "PB4" if pi % 2 == 0 else "PB5"

            def fsw(tensor, a=a, b_=b_, bk=bk, ev=evs[pi]):
                tensor.wait_ge(sems["sv"], ev)
                t_wait_bank(tensor, bk)
                ins = tensor.matmul(pb[bk][:, 0:b_ - a], pswap[:],
                                    qstage[:, a:b_], start=True, stop=True)
                inc(ins, "st")
            on("tensor", fsw)
            bank_w[bk] = val("st")
            VW2 = val("st")

            def fr1(vector, a=a, b_=b_, bk=bk, VW2=VW2):
                vector.wait_ge(sems["st"], VW2)
                ins = vector.tensor_mul(swp[:, a:b_], pb[bk][:, 0:b_ - a], sinu[:, a:b_])
                inc(ins, "sv")
            on("vector", fr1)
            bank_r[bk] = ("sv", val("sv"))

        def fr2(vector, nch=nch):
            vector.drain()
            vector.tensor_mul(tmp[:, 0:TH], qstage[:, 0:TH], cosu[:, 0:TH])
            vector.drain()
            ins = vector.tensor_add(kr[:, nch * TH:(nch + 1) * TH],
                                    tmp[:, 0:TH], swp[:, 0:TH])
            inc(ins, "sv")
        on("vector", fr2)
        rope_done[0] = val("sv")

    for nch in range(NCH):
        emit_k_head(nch)

    def s0cc_in(sync):
        sync.wait_ge(sems["sv"], V_PAY)
        sync.dma_start(out=d_ccin[:, :], in_=arpay[:]).then_inc(sems["scd"], 16)
        cnt["scd"] += 16
    on("sync", s0cc_in)
    V_CCIN = val("scd")

    def emit_s0cc():
        def s0_cc(gpsimd):
            gpsimd.wait_ge(sems["scd"], V_CCIN)
            gpsimd.collective_compute(
                "AllReduce", ALU.add, replica_groups=[list(range(8))],
                ins=[d_ccin.ap().opt()], outs=[d_ccout.ap().opt()],
            ).then_inc(sems["scc"], 1)
            cnt["scc"] += 1
        on("gpsimd", s0_cc)

        def s0cc_out(sync):
            sync.wait_ge(sems["scc"], cnt["scc"])
            sync.dma_start(out=arres[:], in_=d_ccout[:, :]).then_inc(sems["scd"], 16)
            cnt["scd"] += 16
        on("sync", s0cc_out)

    for nch in range(NCH):
        if nch == 3:
            emit_s0cc()
            V_ARRES = val("scd")
            tap("arres", arres[:], "scd", V_ARRES)
        buf, vw = load_strip(d_wvT[:, nch * 128:(nch + 1) * 128], D)
        parts = [(0, 512), (512, 1024), (1024, 1280)]
        evs = []
        for pi, (a, b_) in enumerate(parts):
            bk = "PB6" if (nch * 3 + pi) % 2 == 0 else "PB7"

            def fmm(tensor, buf=buf, vw=vw, a=a, b_=b_, bk=bk):
                tensor.wait_ge(sems[vw[0]], vw[1])
                t_wait_bank(tensor, bk)
                for kc in range(NCH):
                    ins = tensor.matmul(
                        pb[bk][:, 0:b_ - a], buf[:, kc * 128:(kc + 1) * 128],
                        xb[:, kc * TH + a:kc * TH + b_],
                        start=(kc == 0), stop=(kc == NCH - 1))
                inc(ins, "st")
            on("tensor", fmm)
            bank_w[bk] = val("st")
            VW = val("st")

            def fev(vector, a=a, b_=b_, bk=bk, VW=VW):
                vector.wait_ge(sems["st"], VW)
                ins = vector.tensor_copy(qstage[:, a:b_], pb[bk][:, 0:b_ - a])
                inc(ins, "sv")
            on("vector", fev)
            bank_r[bk] = ("sv", val("sv"))
            evs.append(val("sv"))
        strip_consumed("st")

        def ftrv(tensor, nch=nch, ev=evs[2]):
            tensor.wait_ge(sems["sv"], ev)
            t_wait_bank(tensor, "pb2")
            t_wait_bank(tensor, "pb3")
            for tt in range(KT):
                bk_ = pb2 if tt < 8 else pb3
                ins = tensor.transpose(bk_[:, (tt % 8) * 128:(tt % 8 + 1) * 128],
                                       qstage[:, tt * 128:(tt + 1) * 128], ibf[:])
            inc(ins, "st")
        on("tensor", ftrv)
        bank_w["pb2"] = val("st")
        bank_w["pb3"] = val("st")
        VTR = val("st")

        def fevv(vector, nch=nch, VTR=VTR):
            vector.wait_ge(sems["st"], VTR)
            for tt in range(KT):
                bk_ = pb2 if tt < 8 else pb3
                ins = vector.tensor_copy(
                    vT[:, tt * 2048 + nch * 128:tt * 2048 + (nch + 1) * 128],
                    bk_[:, (tt % 8) * 128:(tt % 8 + 1) * 128])
            inc(ins, "sv")
        on("vector", fevv)
        bank_r["pb2"] = ("sv", val("sv"))
        bank_r["pb3"] = ("sv", val("sv"))

    def s1_xm(vector):
        vector.wait_ge(sems["scd"], V_ARRES)
        for b_ in range(2):
            ins = vector.tensor_scalar(
                xmb[:].rearrange("p (c two) -> p c two", two=2)[:, :, b_],
                arres[:, b_ * 32:b_ * 32 + 16], 1.0 / S, None, op0=ALU.mult)
        inc(ins, "sv")
    on("vector", s1_xm)
    V_XM = val("sv")

    for hc in range(4):
        buf, vw = load_strip(d_wc1T[:, hc * 128:(hc + 1) * 128], D)

        def fmm(tensor, buf=buf, vw=vw, hc=hc):
            tensor.wait_ge(sems[vw[0]], vw[1])
            tensor.wait_ge(sems["sv"], V_XM)
            t_wait_bank(tensor, "PB6")
            for kc in range(NCH):
                ins = tensor.matmul(
                    PB6[:, 0:2], buf[:, kc * 128:(kc + 1) * 128],
                    xmb[:, kc * 2:kc * 2 + 2],
                    start=(kc == 0), stop=(kc == NCH - 1))
            inc(ins, "st")
        on("tensor", fmm)
        bank_w["PB6"] = val("st")
        VW = val("st")

        def fev(vector, hc=hc, VW=VW):
            vector.wait_ge(sems["st"], VW)
            ins = vector.tensor_copy(hid[:].rearrange("p (b h) -> p b h", b=2)[:, :, hc],
                                     PB6[:, 0:2])
            inc(ins, "sv")
        on("vector", fev)
        bank_r["PB6"] = ("sv", val("sv"))
        strip_consumed("st")
    V_HID = val("sv")
    tap("hid", hid[:], "ss", V_HID)
    tap("xmb", xmb[:], "sv", V_XM)

    def s1_sig2(scalar):
        scalar.wait_ge(sems["sv"], V_HID)
        ins = scalar.activation(l2t[:], hid[:], AF.Sigmoid)
        inc(ins, "ss")
    on("scalar", s1_sig2)
    V_SIG = val("ss")

    def s1_hv(vector):
        vector.wait_ge(sems["ss"], V_SIG)
        vector.tensor_mul(is1t[:], hid[:], l2t[:])
        vector.drain()
        vector.tensor_mul(is1t[:], is1t[:], wc2r[:])
        vector.drain()
        vector.reduce_sum(pre6[:, 0:1], is1t[:, 0:4], axis=AX.X)
        vector.reduce_sum(pre6[:, 1:2], is1t[:, 4:8], axis=AX.X)
        for g_, dc in enumerate([2, 4, 3, 5]):
            ins = vector.reduce_sum(pre6[:, dc:dc + 1],
                                    arres[:, g_ * 16:(g_ + 1) * 16], axis=AX.X)
        vector.drain()
        inc(ins, "sv")
    on("vector", s1_hv)
    V_HS = val("sv")
    tap("pre6", pre6[:], "sv", V_HS)
    tap("is1t", is1t[:], "sv", V_HS)

    def s1_pred(tensor):
        tensor.wait_ge(sems["sv"], V_HS)
        t_wait_bank(tensor, "PB6")
        ins = tensor.matmul(PB6[:, 0:6], onesf[:], pre6[:, 0:6], start=True, stop=True)
        inc(ins, "st")
    on("tensor", s1_pred)
    bank_w["PB6"] = val("st")
    V_PRED = val("st")

    def s1_chain_v(vector):
        vector.wait_ge(sems["st"], V_PRED)
        vector.tensor_copy(brow[:, 0:6], PB6[:, 0:6])
        vector.drain()
        inc(vector.engine_nop(), "sv")
        vector.tensor_mul(brow[:, 6:8], brow[:, 2:4], brow[:, 2:4])
        vector.drain()
        vector.tensor_scalar(brow[:, 6:8], brow[:, 6:8], -1.0 / NTOT, None, op0=ALU.mult)
        vector.drain()
        vector.tensor_add(brow[:, 6:8], brow[:, 6:8], brow[:, 4:6])
        vector.drain()
        vector.tensor_scalar(brow[:, 6:8], brow[:, 6:8], 1.0 / (NTOT - 1), None,
                             op0=ALU.mult)
        vector.drain()
        ins = vector.tensor_scalar(brow[:, 6:8], brow[:, 6:8], 10.0, -5.0,
                                   op0=ALU.mult, op1=ALU.add)
        vector.drain()
        inc(ins, "sv")
    on("vector", s1_chain_v)
    bank_r["PB6"] = ("sv", val("sv"))
    V_VAR = val("sv")

    def s1_fin_s(scalar):
        scalar.wait_ge(sems["sv"], V_VAR)
        scalar.activation(brow[:, 6:8], brow[:, 6:8], AF.Sigmoid)
        ins = scalar.activation(brow[:, 0:2], brow[:, 0:2], AF.Sigmoid)
        inc(ins, "ss")
    on("scalar", s1_fin_s)
    V_SIG3 = val("ss")

    def s1_w(vector):
        vector.wait_ge(sems["ss"], V_SIG3)
        vector.tensor_add(brow[:, 0:2], brow[:, 0:2], brow[:, 6:8])
        vector.drain()
        vector.reduce_sum(wcol[:], brow[:, 0:2], axis=AX.X)
        vector.drain()
        ins = vector.tensor_scalar(wcol[:], wcol[:], 48.0, 64.0,
                                   op0=ALU.mult, op1=ALU.add)
        vector.drain()
        inc(ins, "sv")
    on("vector", s1_w)
    V_WROW = val("sv")

    def s1_bias(vector):
        vector.tensor_scalar(biasw[:], maskR[:], wcol[:], -1.0,
                             op0=ALU.subtract, op1=ALU.is_le)
        vector.drain()
        vector.tensor_max(biasw[:], biasw[:], maskC[:])
        vector.drain()
        vector.tensor_scalar(biasw[:], biasw[:], 1.0, MASKBIG,
                             op0=ALU.subtract, op1=ALU.mult)
        vector.drain()
        vector.tensor_copy(biasc[:], biasw[:])
        vector.tensor_copy(bias0[:], biasw[:])
        vector.tensor_copy(bias1[:], biasw[:])
        vector.drain()
        vector.tensor_scalar(bias0[:, 0:256], biasw[:, 0:256], sneg[:], None, op0=ALU.min)
        ins = vector.tensor_scalar(bias1[:, 0:128], biasw[:, 0:128], sneg[:], None,
                                   op0=ALU.min)
        vector.drain()
        inc(ins, "sv")
    on("vector", s1_bias)
    V_BIAS = val("sv")
    tap("wcol", wcol[:], "sv", V_BIAS)
    tap("biasc", biasc[:], "sv", V_BIAS)
    tap("kr", kr[:], "sv", V_BIAS)
    tap("vT", vT[:], "sv", V_BIAS)
    tap("xb", xb[:], "sv", V_BIAS)

    qh_rope = {}
    SE0 = val("se")
    ss_base_h = {}
    exp_specs_h = {}
    tail_tr = {}
    pend = {}

    pre_state = {}

    def pre_mms(hn, pi, kcs, first, last):
        buf, vw = pre_state[hn]["strip"]
        bk = "PB6" if pi == 0 else "PB7"
        a, b_ = pi * 512, (pi + 1) * 512

        def fmm(tensor, buf=buf, vw=vw, a=a, b_=b_, bk=bk, kcs=tuple(kcs),
                first=first, last=last):
            if first:
                tensor.wait_ge(sems[vw[0]], vw[1])
                t_wait_bank(tensor, bk)
            for i, kc in enumerate(kcs):
                ins = tensor.matmul(
                    pb[bk][:, 0:512], buf[:, kc * 128:(kc + 1) * 128],
                    xb[:, kc * TH + H + a:kc * TH + H + b_],
                    start=(first and i == 0), stop=(last and i == len(kcs) - 1))
            inc(ins, "st")
        on("tensor", fmm)
        if last:
            bank_w[bk] = val("st")
            pre_state[hn]["mmv%d" % pi] = val("st")
            if pi == 1:
                strip_consumed("st")

    def pre_evac(hn, pi):
        bk = "PB6" if pi == 0 else "PB7"
        a, b_ = pi * 512, (pi + 1) * 512
        VW = pre_state[hn]["mmv%d" % pi]

        def fev(vector, a=a, b_=b_, bk=bk, VW=VW):
            vector.wait_ge(sems["st"], VW)
            ins = vector.tensor_copy(qstage[:, a:b_], pb[bk][:, 0:512])
            inc(ins, "sv")
        on("vector", fev)
        bank_r[bk] = ("sv", val("sv"))
        pre_state[hn]["ev%d" % pi] = val("sv")

    def pre_pswap(hn, pi):
        bk = "PB6" if pi == 0 else "PB7"
        a, b_ = pi * 512, (pi + 1) * 512
        ev = pre_state[hn]["ev%d" % pi]

        def fsw(tensor, a=a, b_=b_, bk=bk, ev=ev):
            tensor.wait_ge(sems["sv"], ev)
            t_wait_bank(tensor, bk)
            ins = tensor.matmul(pb[bk][:, 0:512], pswap[:],
                                qstage[:, a:b_], start=True, stop=True)
            inc(ins, "st")
        on("tensor", fsw)
        bank_w[bk] = val("st")
        pre_state[hn]["sw%d" % pi] = val("st")

    def pre_swpmul(hn, pi):
        bk = "PB6" if pi == 0 else "PB7"
        a, b_ = pi * 512, (pi + 1) * 512
        VW2 = pre_state[hn]["sw%d" % pi]

        def fr1(vector, a=a, b_=b_, bk=bk, VW2=VW2):
            vector.wait_ge(sems["st"], VW2)
            ins = vector.tensor_mul(swp[:, a:b_], pb[bk][:, 0:512],
                                    sinu[:, H + a:H + b_])
            inc(ins, "sv")
        on("vector", fr1)
        bank_r[bk] = ("sv", val("sv"))

    def emit_pre_piece(hn, step):
        if step == 0:
            pre_state[hn] = {
                "strip": load_strip(d_wqT[:, hn * 128:(hn + 1) * 128], D)}
            pre_mms(hn, 0, range(0, 7), True, False)
        elif step == 1:
            pre_mms(hn, 0, range(7, 14), False, False)
        elif step == 2:
            pre_mms(hn, 0, range(14, 16), False, True)
            pre_mms(hn, 1, range(0, 5), True, False)
            pre_evac(hn, 0)
        elif step == 3:
            pre_mms(hn, 1, range(5, 12), False, False)
            pre_pswap(hn, 0)
        elif step == 4:
            pre_mms(hn, 1, range(12, 16), False, True)
            pre_evac(hn, 1)
        elif step == 5:
            pre_pswap(hn, 1)
            pre_swpmul(hn, 0)
        elif step == 6:
            def fr2a(vector):
                vector.drain()
                ins = vector.tensor_mul(tmp[:, 0:T], qstage[:, 0:T],
                                        cosu[:, H:H + T])
                inc(ins, "sv")
            on("vector", fr2a)
            pre_swpmul(hn, 1)
        elif step == 7:
            def fr2b(vector, hn=hn):
                vector.drain()
                ins = vector.tensor_add(qh[:, (hn % 2) * T:(hn % 2 + 1) * T],
                                        tmp[:, 0:T], swp[:, 0:T])
                inc(ins, "sv")
            on("vector", fr2b)
            qh_rope[hn] = val("sv")

    def emit_pre(h):
        for step in range(8):
            emit_pre_piece(h, step)

    def emit_scores_f(fl, h, qt):
        bk = "PB0" if fl % 2 == 0 else "PB1"
        bias_t = bias0 if qt == 0 else (bias1 if qt == 1 else biasc)

        def fsc(tensor, h=h, qt=qt, bk=bk, bias_t=bias_t):
            if qt == 0:
                tensor.wait_ge(sems["sv"], qh_rope[h])
            t_wait_bank(tensor, bk)
            tensor.matmul(pb[bk][:, 0:384],
                          qh[:, (h % 2) * T + qt * 128:(h % 2) * T + (qt + 1) * 128],
                          kr[:, h * TH + qt * 128:h * TH + qt * 128 + 384],
                          start=True, stop=False)
            ins = tensor.matmul(pb[bk][:, 0:384], ibf[:], bias_t[:],
                                start=False, stop=True)
            inc(ins, "st")
        on("tensor", fsc)
        bank_w[bk] = val("st")
        return (h, qt, bk, val("st"))

    def emit_sm1_f(fl):
        h, qt, bk, VSC = pend[fl]
        nslot = fl % 2

        def fsm1(vector, bk=bk, VSC=VSC, nslot=nslot):
            vector.wait_ge(sems["st"], VSC)
            ins = vector.reduce_max(negmax[:, nslot:nslot + 1], pb[bk][:, 0:384],
                                    axis=AX.X, negate=True)
            inc(ins, "sv")
        on("vector", fsm1)
        V_NM = val("sv")
        V_EXP = ss_base_h[h] + qt + 1
        exp_specs_h[h].append((bk, V_NM, fl))
        bank_r[bk] = ("se", V_EXP)
        pend[fl] = (h, qt, V_EXP)

    def emit_sm2_f(fl):
        h, qt, V_EXP = pend[fl]
        pslot = (fl % 3) * 384
        pes = psE if fl % 2 == 0 else psE1
        trw = tail_tr.get(fl - 3, 0)

        def fsm2(vector, V_EXP=V_EXP, pslot=pslot, pes=pes, fl=fl, trw=trw):
            vector.wait_ge(sems["se"], V_EXP)
            if trw:
                vector.wait_ge(sems["st"], trw)
            ins = vector.tensor_scalar(pbt[:, pslot:pslot + 384], pes[:],
                                       rcp[:, fl % 2:fl % 2 + 1], None,
                                       op0=ALU.mult)
            inc(ins, "sv")
        on("vector", fsm2)
        pend[fl] = (h, qt, V_EXP, val("sv"), pslot)

    def emit_ftr_f(fl):
        h, qt, V_EXP, V_PB, pslot = pend[fl]
        tb = "pb2" if fl % 2 == 0 else "pb3"
        tbank = pb2 if fl % 2 == 0 else pb3
        tslot = (fl % 2) * 384

        def ftr(tensor, V_PB=V_PB, pslot=pslot, tbank=tbank, tb=tb):
            tensor.wait_ge(sems["sv"], V_PB)
            t_wait_bank(tensor, tb)
            for j in range(3):
                ins = tensor.transpose(tbank[:, j * 128:(j + 1) * 128],
                                       pbt[:, pslot + j * 128:pslot + (j + 1) * 128],
                                       ibf[:])
            inc(ins, "st")
        on("tensor", ftr)
        bank_w[tb] = val("st")
        tail_tr[fl] = val("st")
        pend[fl] = (h, qt, tslot, val("st"))

    def emit_av_f(fl):
        h, qt, tslot, V_TR = pend[fl]
        tb = "pb2" if fl % 2 == 0 else "pb3"
        tbank = pb2 if fl % 2 == 0 else pb3

        def fevp(vector, V_TR=V_TR, tslot=tslot, tbank=tbank):
            vector.wait_ge(sems["st"], V_TR)
            ins = vector.tensor_copy(pbts[:, tslot:tslot + 384],
                                     tbank[:, 0:384])
            inc(ins, "sv")
        on("vector", fevp)
        bank_r[tb] = ("sv", val("sv"))
        V_EVP = val("sv")

        bkav = "PB4" if fl % 2 == 0 else "PB5"

        def fav(tensor, h=h, qt=qt, bkav=bkav, V_EVP=V_EVP, tslot=tslot):
            tensor.wait_ge(sems["sv"], V_EVP)
            t_wait_bank(tensor, bkav)
            for j in range(3):
                ins = tensor.matmul(
                    pb[bkav][:, 0:128],
                    vT[:, (qt + j) * 2048 + h * 128:(qt + j) * 2048 + (h + 1) * 128],
                    pbts[:, tslot + j * 128:tslot + (j + 1) * 128],
                    start=(j == 0), stop=(j == 2))
            inc(ins, "st")
        on("tensor", fav)
        bank_w[bkav] = val("st")
        pend[fl] = (h, qt, bkav, val("st"))

    def emit_feva_f(fl):
        h, qt, bkav, V_AV = pend.pop(fl)

        def feva(vector, h=h, qt=qt, bkav=bkav, V_AV=V_AV):
            vector.wait_ge(sems["st"], V_AV)
            ins = vector.tensor_copy(ao[:, h * T + qt * 128:h * T + (qt + 1) * 128],
                                     pb[bkav][:, 0:128])
            inc(ins, "sv")
        on("vector", feva)
        bank_r[bkav] = ("sv", val("sv"))
        if qt == QT - 1:
            ao_ready[h] = val("sv")

    ao_ready = {}
    emit_pre(0)
    NFL = HEADS * QT
    for fl in range(NFL + 4):
        if fl < NFL:
            h, qt = fl // QT, fl % QT
            if qt == 0:
                ss_base_h[h] = SE0 + QT * h
                exp_specs_h[h] = []
            pend[fl] = emit_scores_f(fl, h, qt)
        if 0 <= fl - 2 < NFL:
            emit_ftr_f(fl - 2)
        if 0 <= fl - 3 < NFL:
            emit_av_f(fl - 3)
        if 0 <= fl - 4 < NFL:
            emit_feva_f(fl - 4)
        if fl < NFL and fl // QT + 1 < HEADS:
            emit_pre_piece(fl // QT + 1, fl % QT)
        if fl < NFL:
            emit_sm1_f(fl)
        if 0 <= fl - 1 < NFL:
            emit_sm2_f(fl - 1)
        if fl < NFL and fl % QT == QT - 1:
            h = fl // QT

            def fexp_all(scalar, specs=exp_specs_h[h], h=h):
                for bk_, V_NM_, fl_ in specs:
                    scalar.wait_ge(sems["sv"], V_NM_)
                    pes = psE if fl_ % 2 == 0 else psE1
                    sl_ = fl_ % 2
                    scalar.activation(pes[:], pb[bk_][:, 0:384], AF.Exp,
                                      bias=negmax[:, sl_:sl_ + 1],
                                      accum_out=ssum[:, sl_:sl_ + 1])
                    scalar.drain()
                    scalar.activation(lnv[:, sl_:sl_ + 1], ssum[:, sl_:sl_ + 1],
                                      AF.Ln)
                    scalar.drain()
                    ins = scalar.activation(rcp[:, sl_:sl_ + 1], lnv[:, sl_:sl_ + 1],
                                            AF.Exp, scale=-1.0)
                    inc(ins, "se")
            on("scalar", fexp_all)
            assert val("se") == SE0 + QT * (h + 1)

    V_AOALL = val("sv")
    tap("ao", ao[:], "sv", V_AOALL)

    for oc in range(OC):
        buf, vw = load_strip(d_woT[:, oc * 128:(oc + 1) * 128], D)
        for pi in range(2):
            a, b_ = pi * 512, (pi + 1) * 512
            bk = "PB6" if (oc * 2 + pi) % 2 == 0 else "PB7"

            def fmm(tensor, buf=buf, vw=vw, a=a, b_=b_, bk=bk, oc=oc, pi=pi):
                tensor.wait_ge(sems[vw[0]], vw[1])
                t_wait_bank(tensor, bk)
                for kc in range(NCH):
                    if oc == 0 and pi == 0:
                        tensor.wait_ge(sems["sv"], ao_ready[kc])
                    ins = tensor.matmul(
                        pb[bk][:, 0:512], buf[:, kc * 128:(kc + 1) * 128],
                        ao[:, kc * T + a:kc * T + b_],
                        start=(kc == 0), stop=(kc == NCH - 1))
                inc(ins, "st")
            on("tensor", fmm)
            bank_w[bk] = val("st")
            VW = val("st")

            def fev(vector, oc=oc, a=a, b_=b_, bk=bk, VW=VW):
                vector.wait_ge(sems["st"], VW)
                ins = vector.tensor_add(rF[:, oc * T + a:oc * T + b_],
                                        pb[bk][:, 0:512],
                                        xb[:, oc * TH + H + a:oc * TH + H + b_])
                inc(ins, "sv")
            on("vector", fev)
            bank_r[bk] = ("sv", val("sv"))
        strip_consumed("st")
    tap("r1", rF[:], "sv", val("sv"))

    def ln_stats():
        evs = []
        for c2 in range(NCH):
            sl = c2 % 2

            def fcast(vector, c2=c2, sl=sl):
                if c2 >= 2:
                    vector.wait_ge(sems["st"], evs[c2 - 2])
                ins = vector.tensor_copy(rb1[:, sl * 1024:(sl + 1) * 1024],
                                         rF[:, c2 * T:(c2 + 1) * T])
                inc(ins, "sv")
            on("vector", fcast)
            VC = val("sv")

            def fsq(scalar, c2=c2, sl=sl, VC=VC):
                scalar.wait_ge(sems["sv"], VC)
                if c2 >= 2:
                    scalar.wait_ge(sems["st"], evs[c2 - 2])
                ins = scalar.activation(rq1[:, sl * 1024:(sl + 1) * 1024],
                                        rb1[:, sl * 1024:(sl + 1) * 1024], AF.Square)
                inc(ins, "ss")
            on("scalar", fsq)
            VQ = val("ss")

            def fst(tensor, c2=c2, sl=sl, VC=VC, VQ=VQ):
                tensor.wait_ge(sems["sv"], VC)
                tensor.wait_ge(sems["ss"], VQ)
                if c2 == 0:
                    for bkx in ["PB0", "PB1", "PB4", "PB5"]:
                        t_wait_bank(tensor, bkx)
                for half in range(2):
                    ha = half * 512
                    bs = pb["PB0" if half == 0 else "PB1"]
                    bq = pb["PB4" if half == 0 else "PB5"]
                    tensor.matmul(bs[:, 0:512], onesb[:],
                                  rb1[:, sl * 1024 + ha:sl * 1024 + ha + 512],
                                  start=(c2 == 0), stop=(c2 == NCH - 1))
                    ins = tensor.matmul(bq[:, 0:512], onesb[:],
                                        rq1[:, sl * 1024 + ha:sl * 1024 + ha + 512],
                                        start=(c2 == 0), stop=(c2 == NCH - 1))
                inc(ins, "st")
            on("tensor", fst)
            evs.append(val("st"))
        for bkx in ["PB0", "PB1", "PB4", "PB5"]:
            bank_w[bkx] = val("st")
        V_ST = val("st")

        def ffin_v(vector, V_ST=V_ST):
            vector.wait_ge(sems["st"], V_ST)
            for half in range(2):
                ha = half * 512
                bmu = pb["PB0" if half == 0 else "PB1"]
                bsq = pb["PB4" if half == 0 else "PB5"]
                vector.tensor_scalar(mubc[:, ha:ha + 512], bmu[:, 0:512], 1.0 / D,
                                     None, op0=ALU.mult)
                vector.tensor_scalar(rsbc[:, ha:ha + 512], bsq[:, 0:512], 1.0 / D,
                                     None, op0=ALU.mult)
                vector.drain()
                vector.tensor_mul(bmu[:, 0:512], mubc[:, ha:ha + 512],
                                  mubc[:, ha:ha + 512])
                vector.drain()
                vector.tensor_sub(rsbc[:, ha:ha + 512], rsbc[:, ha:ha + 512],
                                  bmu[:, 0:512])
                vector.drain()
                ins = vector.tensor_scalar(rsbc[:, ha:ha + 512], rsbc[:, ha:ha + 512],
                                           EPS, None, op0=ALU.add)
            inc(ins, "sv")
        on("vector", ffin_v)
        V_MU = val("sv")
        for bkx in ["PB0", "PB1"]:
            bank_r[bkx] = ("sv", V_MU)

        def ffin_s(scalar, V_MU=V_MU):
            scalar.wait_ge(sems["sv"], V_MU)
            for half in range(2):
                ha = half * 512
                bsq = pb["PB4" if half == 0 else "PB5"]
                scalar.activation(bsq[:, 0:512], rsbc[:, ha:ha + 512], AF.Ln)
                scalar.drain()
                ins = scalar.activation(rsbc[:, ha:ha + 512], bsq[:, 0:512], AF.Exp,
                                        scale=-0.5)
            inc(ins, "ss")
        on("scalar", ffin_s)
        V_RSTD = val("ss")
        for bkx in ["PB4", "PB5"]:
            bank_r[bkx] = ("ss", V_RSTD)
        return V_RSTD

    V_LN1 = ln_stats()

    for c2 in range(NCH):
        def fnorm(vector, c2=c2):
            if c2 == 0:
                vector.wait_ge(sems["ss"], V_LN1)
            vector.tensor_sub(rF[:, c2 * T:(c2 + 1) * T], rF[:, c2 * T:(c2 + 1) * T],
                              mubc[:])
            vector.drain()
            ins = vector.tensor_mul(rF[:, c2 * T:(c2 + 1) * T],
                                    rF[:, c2 * T:(c2 + 1) * T], rsbc[:])
            inc(ins, "sv")
        on("vector", fnorm)
        VN = val("sv")

        def fg(tensor, c2=c2, VN=VN):
            tensor.wait_ge(sems["sv"], VN)
            if c2 == 0:
                t_wait_bank(tensor, "PB0")
                t_wait_bank(tensor, "PB1")
            for half in range(2):
                ha = half * 512
                bk_ = pb["PB0" if half == 0 else "PB1"]
                ins = tensor.matmul(bk_[0:8, 0:512], wg1T[:, c2 * 8:(c2 + 1) * 8],
                                    rF[:, c2 * T + ha:c2 * T + ha + 512],
                                    start=(c2 == 0), stop=(c2 == NCH - 1))
            inc(ins, "st")
        on("tensor", fg)

        def fyb(vector, c2=c2):
            vector.drain()
            ins = vector.tensor_copy(yb[:, c2 * T:(c2 + 1) * T],
                                     rF[:, c2 * T:(c2 + 1) * T])
            inc(ins, "sv")
        on("vector", fyb)
    bank_w["PB0"] = val("st")
    bank_w["PB1"] = val("st")
    V_G1 = val("st")
    V_YB = val("sv")
    tap("yb", yb[:], "sv", V_YB)

    def ftanh(scalar):
        scalar.wait_ge(sems["st"], V_G1)
        scalar.activation(T1[:, 0:512], PB0[0:8, 0:512], AF.Tanh)
        ins = scalar.activation(T1[:, 512:1024], PB1[0:8, 0:512], AF.Tanh)
        inc(ins, "ss")
    on("scalar", ftanh)
    bank_r["PB0"] = ("ss", val("ss"))
    bank_r["PB1"] = ("ss", val("ss"))
    V_T1 = val("ss")

    def fg2(tensor):
        tensor.wait_ge(sems["ss"], V_T1)
        t_wait_bank(tensor, "PB4")
        t_wait_bank(tensor, "PB5")
        tensor.matmul(PB4[0:8, 0:512], wg2T[:], T1[:, 0:512], start=True, stop=True)
        ins = tensor.matmul(PB5[0:8, 0:512], wg2T[:], T1[:, 512:1024],
                            start=True, stop=True)
        inc(ins, "st")
    on("tensor", fg2)
    bank_w["PB4"] = val("st")
    bank_w["PB5"] = val("st")
    V_G2 = val("st")

    def fevL(vector):
        vector.wait_ge(sems["st"], V_G2)
        vector.tensor_copy(LfT[:, 0:512], PB4[0:8, 0:512])
        ins = vector.tensor_copy(LfT[:, 512:1024], PB5[0:8, 0:512])
        inc(ins, "sv")
    on("vector", fevL)
    bank_r["PB4"] = ("sv", val("sv"))
    bank_r["PB5"] = ("sv", val("sv"))
    V_LF = val("sv")

    V_SFB_box = [0]

    def emit_s5():
        def ftrL(tensor):
            tensor.wait_ge(sems["sv"], V_LF)
            t_wait_bank(tensor, "PB6")
            for tc in range(8):
                ins = tensor.transpose(PB6[:, tc * 8:(tc + 1) * 8],
                                       LfT[:, tc * 128:(tc + 1) * 128], idf8[:])
            inc(ins, "st")
        on("tensor", ftrL)
        bank_w["PB6"] = val("st")
        V_LT = val("st")

        def fevLt(vector):
            vector.wait_ge(sems["st"], V_LT)
            ins = vector.tensor_copy(Lt[:], PB6[:, 0:64])
            vector.drain()
            inc(ins, "sv")
        on("vector", fevLt)
        bank_r["PB6"] = ("sv", val("sv"))

        for tc in range(8):
            def ftop_v1(vector, tc=tc):
                Ltc = Lt[:, tc * 8:(tc + 1) * 8]
                vector.reduce_max(m1[:], Ltc, axis=AX.X)
                vector.drain()
                vector.tensor_scalar(is1t[:], Ltc, m1[:], 0.0, op0=ALU.subtract,
                                     op1=ALU.is_ge)
                vector.drain()
                vector.scalar_tensor_tensor(l2t[:], is1t[:], -1e30, Ltc,
                                            op0=ALU.mult, op1=ALU.add)
                vector.drain()
                vector.reduce_max(m2[:], l2t[:], axis=AX.X)
                vector.drain()
                vector.tensor_scalar(is2t[:], l2t[:], m2[:], 0.0, op0=ALU.subtract,
                                     op1=ALU.is_ge)
                ins = vector.tensor_scalar(dd[:], m1[:], m2[:, 0:1], None,
                                           op0=ALU.subtract)
                vector.drain()
                inc(ins, "sv")
            on("vector", ftop_v1)
            V_TOPA = val("sv")

            def ftop_s(scalar, V_TOPA=V_TOPA):
                scalar.wait_ge(sems["sv"], V_TOPA)
                ins = scalar.activation(p1c[:], dd[:], AF.Sigmoid)
                inc(ins, "ss")
            on("scalar", ftop_s)
            V_TOPS = val("ss")

            def ftop_v2(vector, tc=tc, V_TOPS=V_TOPS):
                vector.wait_ge(sems["ss"], V_TOPS)
                vector.tensor_scalar(p2c[:], p1c[:], -1.0, 1.0, op0=ALU.mult, op1=ALU.add)
                vector.tensor_scalar(stok[:, tc * 8:(tc + 1) * 8], is1t[:], p1c[:], None,
                                     op0=ALU.mult)
                vector.drain()
                ins = vector.scalar_tensor_tensor(stok[:, tc * 8:(tc + 1) * 8], is2t[:],
                                                  p2c[:], stok[:, tc * 8:(tc + 1) * 8],
                                                  op0=ALU.mult, op1=ALU.add)
                vector.drain()
                inc(ins, "sv")
            on("vector", ftop_v2)
        V_STOK = val("sv")

        def ftrS(tensor):
            tensor.wait_ge(sems["sv"], V_STOK)
            t_wait_bank(tensor, "PB6")
            t_wait_bank(tensor, "PB7")
            for tc in range(8):
                bk_ = PB6 if tc < 4 else PB7
                ins = tensor.transpose(bk_[0:8, (tc % 4) * 128:(tc % 4 + 1) * 128],
                                       stok[:, tc * 8:(tc + 1) * 8], idf[:])
            inc(ins, "st")
        on("tensor", ftrS)
        bank_w["PB6"] = val("st")
        bank_w["PB7"] = val("st")
        V_SF = val("st")

        def fevS(vector):
            vector.wait_ge(sems["st"], V_SF)
            vector.tensor_copy(sfF[:, 0:512], PB6[0:8, 0:512])
            vector.tensor_copy(sfF[:, 512:1024], PB7[0:8, 0:512])
            vector.drain()
            ins = vector.tensor_copy(sfb[:], sfF[:])
            inc(ins, "sv")
        on("vector", fevS)
        bank_r["PB6"] = ("sv", val("sv"))
        bank_r["PB7"] = ("sv", val("sv"))
        V_SFB_box[0] = val("sv")

    tap("sfF", sfF[:], "sv", V_SFB_box[0])

    def g_bdn(gpsimd):
        gpsimd.dma_start(out=bdn[:], in_=d_bdown).then_inc(sems["sw"], 16)
        cnt["sw"] += 16
    on("gpsimd", g_bdn)

    up_state = {}

    def emit_up_mm(g, el):
        e = 2 * g + el
        ssE = val("ss")
        silu_specs = []
        for hc in range(HC):
            buf, vw = load_strip(d_wupT[e][:, hc * 128:(hc + 1) * 128], D,
                                 alt=True)
            unit = el * HC + hc
            for pi in range(2):
                a, b_ = pi * 512, (pi + 1) * 512
                bk = ["PB4", "PB5", "PB6", "PB7"][(unit * 2 + pi) % 4]

                def fmm(tensor, buf=buf, vw=vw, a=a, b_=b_, bk=bk):
                    tensor.wait_ge(sems[vw[0]], vw[1])
                    t_wait_bank(tensor, bk)
                    for kc in range(NCH):
                        ins = tensor.matmul(
                            pb[bk][:, 0:512], buf[:, kc * 128:(kc + 1) * 128],
                            yb[:, kc * T + a:kc * T + b_],
                            start=(kc == 0), stop=(kc == NCH - 1))
                    inc(ins, "st")
                on("tensor", fmm)
                bank_w[bk] = val("st")
                silu_specs.append((val("st"), bk, e, hc, unit, a, b_))
                bank_r[bk] = ("ss", ssE + len(silu_specs))
            strip_consumed("st")

        def fsilu_all(scalar, silu_specs=silu_specs):
            for VW_, bk_, e_, hc_, unit_, a_, b__ in silu_specs:
                scalar.wait_ge(sems["st"], VW_)
                ins = scalar.activation(
                    hp[:, unit_ * 1024 + a_:unit_ * 1024 + b__],
                    pb[bk_][:, 0:512], AF.Silu,
                    bias=bupr[:, e_ * 8 + hc_:e_ * 8 + hc_ + 1])
                inc(ins, "ss")
        on("scalar", fsilu_all)
        assert val("ss") == ssE + 16
        V_SI = val("ss")

        up_state[e] = (ssE, el)

    hp_ready = {}

    def emit_up_fin(g, el):
        e = 2 * g + el
        ssE, _el = up_state[e]
        V_SFB = V_SFB_box[0]
        def fbce(tensor, e=e):
            tensor.wait_ge(sems["sv"], V_SFB)
            t_wait_bank(tensor, "PB0")
            t_wait_bank(tensor, "PB1")
            tensor.matmul(PB0[:, 0:512], selm[:, e * 128:(e + 1) * 128],
                          sfb[:, 0:512], start=True, stop=True)
            ins = tensor.matmul(PB1[:, 0:512], selm[:, e * 128:(e + 1) * 128],
                                sfb[:, 512:1024], start=True, stop=True)
            inc(ins, "st")
        on("tensor", fbce)
        bank_w["PB0"] = val("st")
        bank_w["PB1"] = val("st")
        V_BCE = val("st")
        bslot = el

        def fevbc(vector, bslot=bslot, V_BCE=V_BCE):
            vector.wait_ge(sems["st"], V_BCE)
            vector.tensor_copy(bc[:, bslot * 1024:bslot * 1024 + 512], PB0[:, 0:512])
            ins = vector.tensor_copy(bc[:, bslot * 1024 + 512:(bslot + 1) * 1024],
                                     PB1[:, 0:512])
            vector.drain()
            inc(ins, "sv")
        on("vector", fevbc)
        bank_r["PB0"] = ("sv", val("sv"))
        bank_r["PB1"] = ("sv", val("sv"))

        def fmul_all(vector, el=el, bslot=bslot, ssE=ssE, g=g):
            for hc in range(HC):
                unit = el * HC + hc
                vector.wait_ge(sems["ss"], ssE + 2 * (hc + 1))
                ins = vector.tensor_mul(
                    hp[:, unit * 1024:(unit + 1) * 1024],
                    hp[:, unit * 1024:(unit + 1) * 1024],
                    bc[:, bslot * 1024:(bslot + 1) * 1024])
                hp_ready[(g, unit)] = inc(ins, "sv")
        on("vector", fmul_all)

    for g in range(NPASS):
        if g == 0:
            emit_up_mm(0, 0)
            emit_up_mm(0, 1)
            emit_s5()
            emit_up_fin(0, 0)
            emit_up_fin(0, 1)
        else:
            for el in range(2):
                emit_up_mm(g, el)
                emit_up_fin(g, el)
        V_HP = val("sv")

        for oc in range(OC):
            bkpair = ["PB4", "PB5"] if oc % 2 == 0 else ["PB6", "PB7"]
            buf, vw = load_strip(d_wdn[g, oc], 2048, alt=True)

            def fmm(tensor, buf=buf, vw=vw, oc=oc, g=g, bkpair=bkpair, V_HP=V_HP):
                tensor.wait_ge(sems[vw[0]], vw[1])
                t_wait_bank(tensor, bkpair[0])
                t_wait_bank(tensor, bkpair[1])
                for c_ in range(16):
                    el = c_ // 8
                    hc = c_ % 8
                    if oc == 0:
                        tensor.wait_ge(sems["sv"], hp_ready[(g, c_)])
                    for half in range(2):
                        ha = half * 512
                        stop = (g < NPASS - 1 and c_ == 15)
                        ins = tensor.matmul(
                            pb[bkpair[half]][:, 0:512],
                            buf[:, c_ * 128:(c_ + 1) * 128],
                            hp[:, (el * 8 + hc) * 1024 + ha:
                               (el * 8 + hc) * 1024 + ha + 512],
                            start=(c_ == 0), stop=stop)
                if g == NPASS - 1:
                    tensor.wait_ge(sems["sw"], 16)
                    for half in range(2):
                        ha = half * 512
                        ins = tensor.matmul(
                            pb[bkpair[half]][:, 0:512],
                            bdn[:, oc * 128:(oc + 1) * 128],
                            sfb[:, ha:ha + 512], start=False, stop=True)
                inc(ins, "st")
            on("tensor", fmm)
            strip_consumed("st")
            mm_done = val("st")
            for half in range(2):
                bank_w[bkpair[half]] = mm_done

            def fevd(vector, oc=oc, g=g, bkpair=bkpair, mm_done=mm_done):
                vector.wait_ge(sems["st"], mm_done)
                other = yb if g == 0 else rF
                for half in range(2):
                    ha = half * 512
                    ins = vector.tensor_add(rF[:, oc * T + ha:oc * T + ha + 512],
                                            pb[bkpair[half]][:, 0:512],
                                            other[:, oc * T + ha:oc * T + ha + 512])
                vector.drain()
                inc(ins, "sv")
            on("vector", fevd)
            for half in range(2):
                bank_r[bkpair[half]] = ("sv", val("sv"))

    tap("r2", rF[:], "sv", val("sv"))
    V_LN2 = ln_stats()
    for c2 in range(NCH):
        def fnorm2(vector, c2=c2):
            if c2 == 0:
                vector.wait_ge(sems["ss"], V_LN2)
            vector.tensor_sub(rF[:, c2 * T:(c2 + 1) * T], rF[:, c2 * T:(c2 + 1) * T],
                              mubc[:])
            vector.drain()
            ins = vector.tensor_mul(rF[:, c2 * T:(c2 + 1) * T],
                                    rF[:, c2 * T:(c2 + 1) * T], rsbc[:])
            inc(ins, "sv")
        on("vector", fnorm2)
        VN2 = val("sv")

        def fstorec(sync, c2=c2, VN2=VN2):
            sync.wait_ge(sems["sv"], VN2)
            sync.dma_start(out=d_out[c2 * 128:(c2 + 1) * 128, :],
                           in_=rF[:, c2 * T:(c2 + 1) * T]).then_inc(sems["sst"], 16)
            cnt["sst"] += 16
        on("sync", fstorec)

    def fstore(sync):
        sync.wait_ge(sems["sst"], cnt["sst"])
    on("sync", fstore)

    blocks.__exit__(None, None, None)
    nc._keepalive = _stack
    return nc



def _host_prepare(inputs):
    x = _f32(inputs["L0_attn_x"])
    wqkv = _f32(inputs["L0_attn_w_qkv"])
    wout = _f32(inputs["L0_attn_w_out"])
    wc1 = _f32(inputs["L0_attn_w_c1"])
    wc2 = _f32(inputs["L0_attn_w_c2"])
    wg1 = _f32(inputs["L0_ffn_Wg1"])
    wg2 = _f32(inputs["L0_ffn_Wg2"])
    wup = _f32(inputs["L0_ffn_Wup"])
    bup = _f32(inputs["L0_ffn_bup"])
    wdown = _f32(inputs["L0_ffn_Wdown"])
    bdown = _f32(inputs["L0_ffn_bdown"])

    shared = {}
    shared["wqT"] = np.ascontiguousarray(wqkv[0:2048].T)
    shared["wkT"] = np.ascontiguousarray(wqkv[2048:4096].T)
    shared["wvT"] = np.ascontiguousarray(wqkv[4096:6144].T)
    shared["woT"] = np.ascontiguousarray(wout.T)
    shared["wc1T"] = np.ascontiguousarray(wc1.T)
    wc2r = np.zeros((128, 8), np.float32)
    for hc in range(4):
        for b_ in range(2):
            wc2r[:, b_ * 4 + hc] = wc2[0, hc * 128:(hc + 1) * 128]
    shared["wupT"] = np.ascontiguousarray(
        np.transpose(wup, (0, 2, 1))).astype(ml_dtypes.bfloat16)
    wdnT = np.transpose(wdown, (0, 2, 1))
    wdn = np.zeros((NPASS, OC, 2048, 128), np.float32)
    for g in range(NPASS):
        for oc in range(OC):
            wdn[g, oc] = np.concatenate(
                [wdnT[e][:, oc * 128:(oc + 1) * 128] for e in (2 * g, 2 * g + 1)],
                axis=0)
    shared["wdn"] = wdn.astype(ml_dtypes.bfloat16)
    bupr_ = np.zeros((128, 64), np.float32)
    for e in range(E):
        for hc in range(HC):
            bupr_[:, e * 8 + hc] = bup[e, hc * 128:(hc + 1) * 128]
    shared["bdown"] = bdown

    inv = 1.0 / (10000.0 ** (np.arange(0, HD, 2, dtype=np.float64) / HD))
    ftil = np.concatenate([inv, inv])
    sgn = np.where(np.arange(HD) % 2 == 0, -1.0, 1.0)[:, None]
    u = 128.0 ** -0.25

    qi = np.arange(128)[:, None]
    kj = np.arange(384)[None, :]
    rel = (256 + qi - kj).astype(np.float64)
    maskR = np.where(rel < 0, 1e9, rel).astype(np.float32)
    maskC = ((rel >= 0) & (rel <= 63)).astype(np.float32)

    cbf = np.zeros((128, 260), np.float32)
    cbf[:, 0:128] = np.eye(128)[np.arange(128) ^ 1]
    cbf[:, 128:256] = np.eye(128)
    cbf[:, 256] = 1.0

    selm = np.zeros((8, 1024), np.float32)
    for e in range(8):
        selm[e, e * 128:(e + 1) * 128] = 1.0

    cpak = np.zeros((128, 928), np.float32)

    def put(col_byte, arr):
        a = np.asarray(arr, np.float32)
        c0 = (col_byte - 640) // 4
        cpak[: a.shape[0], c0:c0 + a.shape[1]] = a
    put(640, np.eye(128, dtype=np.float32))
    put(1152, np.eye(8, dtype=np.float32))
    put(1216, np.ones((128, 128), np.float32))
    wg1r = np.zeros((128, 128), np.float32)
    wg1Tfull = np.ascontiguousarray(wg1.T)
    for cch in range(16):
        wg1r[:, cch * 8:(cch + 1) * 8] = wg1Tfull[cch * 128:(cch + 1) * 128, :]
    put(1728, wg1r)
    put(2240, np.asarray(wg2.T, np.float32))
    put(2304, wc2r)
    put(2432, np.broadcast_to(np.array([0.0, 0.0], np.float32), (128, 2)))
    put(4096, bupr_)
    mpak = np.concatenate([maskR, maskC], axis=1)

    shared.update({
        "cbf": _bf(cbf),
        "selm": _bf(selm),
        "mpak": np.ascontiguousarray(mpak),
        "onesb": _bf(np.ones((128, 128), np.float32)),
    })
    shared.pop("maskR", None)
    shared.pop("maskC", None)

    in_maps = []
    for c in range(8):
        b_ = c // 4
        p0 = (c % 4) * T
        xs = np.zeros((TH, D), np.float32)
        lo = max(p0 - H, 0)
        xs[H - (p0 - lo):] = x[b_, lo:p0 + T]
        xT = np.ascontiguousarray(xs.T)

        pos = np.arange(p0 - H, p0 + T, dtype=np.float64)
        ang = pos[None, :] * ftil[:, None]
        tabs = np.concatenate([np.cos(ang) * u, np.sin(ang) * sgn * u], axis=1)

        cpak_c = cpak.copy()
        cpak_c[:, (2368 - 640) // 4] = -MASKBIG if p0 == 0 else 0.0
        cpak_c[:, (2432 - 640) // 4:(2432 - 640) // 4 + 2] = (
            np.array([1.0, 0.0] if b_ == 0 else [0.0, 1.0], np.float32))
        m = {
            "xT": xT,
            "tabs": _bf(tabs),
            "cpak": cpak_c,
        }
        m.update(shared)
        in_maps.append(m)
    return in_maps


_NC_CACHE = {}


def kernel(**inputs):
    in_maps = _host_prepare(inputs)
    if "nc" not in _NC_CACHE:
        _NC_CACHE["nc"] = build_program()
    nc = _NC_CACHE["nc"]
    res = run_bass_kernel_spmd(nc, in_maps, core_ids=list(range(8)))
    out = np.zeros((B, S, D), np.float32)
    for c in range(8):
        b_ = c // 4
        p0 = (c % 4) * T
        out[b_, p0:p0 + T] = np.asarray(res.results[c]["outT"], np.float32).T
    return out
